# version 31
# baseline (speedup 1.0000x reference)
import numpy as np
import ml_dtypes
from contextlib import ExitStack

import concourse.bass as bass
import concourse.mybir as mybir
from concourse.bass_utils import run_bass_kernel_spmd

F32 = mybir.dt.float32
BF16 = mybir.dt.bfloat16
AF = mybir.ActivationFunctionType
ALU = mybir.AluOpType

D = 1024
SEQ = 2048
BPC = 2
H = 16
HD = 64
NCORES = 8
TC = SEQ // 128
NJ = D // 128
EPS = 1e-5
QSCALE = HD ** -0.5


class Op:
    __slots__ = ("idx", "engine", "emit", "deps", "dma", "sem_val", "needs_inc", "grp")

    def __init__(self, idx, engine, emit, deps, dma, grp):
        self.idx = idx
        self.engine = engine
        self.emit = emit
        self.deps = deps
        self.dma = dma
        self.sem_val = None
        self.needs_inc = False
        self.grp = grp


class Prog:

    def __init__(self):
        self.ops = []

    def add(self, engine, emit, deps=(), dma=False, q=None):
        deps = [d for d in deps if d is not None]
        assert not dma or q is not None, "DMA ops must name a semaphore group"
        op = Op(len(self.ops), engine, emit, deps, dma, q if dma else engine)
        for d in deps:
            assert d.idx < op.idx
        self.ops.append(op)
        return op

    def finalize(self):
        for op in self.ops:
            if op.dma:
                op.needs_inc = True
            kept = []
            for d in op.deps:
                if d.engine == op.engine and op.engine == "pe" and not d.dma and not op.dma:
                    continue
                if d.dma and op.dma and d.engine == op.engine:
                    continue
                kept.append(d)
                d.needs_inc = True
            op.deps = kept
        counters = {}
        grp_last = {}
        for op in self.ops:
            if op.needs_inc:
                counters[op.grp] = counters.get(op.grp, 0) + (16 if op.dma else 1)
                op.sem_val = counters[op.grp]
                grp_last[op.grp] = op
        for op in self.ops:
            for d in op.deps:
                if d.grp in ("cst", "pcst"):
                    d.sem_val = grp_last[d.grp].sem_val
        per_engine = {}
        stream_of = {"pdma": "gpsimd", "pool": "gpsimd"}
        for op in self.ops:
            per_engine.setdefault(stream_of.get(op.engine, op.engine), []).append(op)
        self.per_engine = per_engine
        self.totals = counters

    def run_engine(self, name, engines, sems):
        eng = engines[name]
        waited = {}
        for op in self.per_engine.get(name, []):
            for d in op.deps:
                v = d.sem_val
                assert v is not None
                if waited.get(d.grp, -1) >= v:
                    continue
                eng.wait_ge(sems[d.grp], v)
                waited[d.grp] = v
            last = op.emit(eng)
            if op.needs_inc:
                last.then_inc(sems[op.grp], 16 if op.dma else 1)


def mm_group(e, items):
    first = None
    last = None
    for (out_ap, lhsT_ap, rhs_ap, st, sp, tp) in items:
        if first is None:
            first = e.matmul(out_ap, lhsT_ap, rhs_ap, start=st, stop=sp,
                             tile_position=tp)
            last = first
        else:
            inst = mybir.InstMatmult(
                name=e.bass.get_next_instruction_name(),
                ins=[e.lower_ap_or_imm(rhs_ap), e.lower_ap_or_imm(lhsT_ap)],
                outs=[e.lower_ap_or_imm(out_ap)],
                replication_resolution=0, replication_shift_amnt=0,
                replication_num_rows=0,
                start_tensor_calc=st, stop_tensor_calc=sp,
                tile_position=first.ins.tile_position,
                tile_size=first.ins.tile_size,
                ldweights=False)
            last = e.add_instruction(inst)
    return last


def build_nc():
    nc = bass.Bass("TRN2", target_bir_lowering=False)

    x_hbm = nc.dram_tensor("x", [BPC, SEQ, D], F32, kind="ExternalInput").ap()
    wqt_h = nc.dram_tensor("wqt", [D, D], F32, kind="ExternalInput").ap()
    wkt_h = nc.dram_tensor("wkt", [D, D], F32, kind="ExternalInput").ap()
    wvt_h = nc.dram_tensor("wvt", [D, D], F32, kind="ExternalInput").ap()
    wot_h = nc.dram_tensor("wot", [D, D], F32, kind="ExternalInput").ap()
    wr_h = nc.dram_tensor("wfr", [D, D], F32, kind="ExternalInput").ap()
    wi_h = nc.dram_tensor("wfi", [D, D], F32, kind="ExternalInput").ap()
    bias_h = nc.dram_tensor("fbias", [1, D], F32, kind="ExternalInput").ap()
    c_h = nc.dram_tensor("ctab", [D, D], BF16, kind="ExternalInput").ap()
    sneg_h = nc.dram_tensor("snegtab", [D, D], BF16, kind="ExternalInput").ap()
    csh_h = nc.dram_tensor("cshard", [D, 128], BF16, kind="ExternalInput").ap()
    ssh_h = nc.dram_tensor("sshard", [D, 128], BF16, kind="ExternalInput").ap()
    sshn_h = nc.dram_tensor("sshardneg", [D, 128], BF16, kind="ExternalInput").ap()
    ccin = nc.dram_tensor("ccin", [128, D], BF16, kind="Internal").ap()
    ccout = nc.dram_tensor("ccout", [NCORES * 128, D], BF16, kind="Internal").ap()
    ident_h = nc.dram_tensor("ident", [128, 128], BF16, kind="ExternalInput").ap()
    ones64_h = nc.dram_tensor("ones64", [128, 64], BF16, kind="ExternalInput").ap()
    onesrow_h = nc.dram_tensor("onesrow", [1, 128], BF16, kind="ExternalInput").ap()

    y_hbm = nc.dram_tensor("y", [BPC, SEQ, D], F32, kind="ExternalOutput").ap()
    x1_scr = nc.dram_tensor("x1scr", [SEQ, D], F32, kind="Internal").ap()

    def chunked(hbm_ap):
        return hbm_ap.rearrange("(c p) d -> p c d", p=128)

    P = Prog()
    out_dmas = []

    with ExitStack() as top:
        sb = lambda name, shape, dt: top.enter_context(nc.sbuf_tensor(name, shape, dt))
        sems = {}
        for name in ("pe", "act", "dve", "pool", "cst", "pcst", "cc", "mt",
                     "xa", "xb", "xc", "xd", "xe", "xf", "wa", "wb",
                     "sa", "sb", "ya", "yb"):
            sems[name] = top.enter_context(nc.semaphore("sem_" + name))

        identt = sb("identt", [128, 128], BF16)
        ones64t = sb("ones64t", [128, 64], BF16)
        onesrowt = sb("onesrowt", [1, 128], BF16)
        epsv = sb("epsv", [128, 1], F32)
        biasbf = sb("biasbf", [1, D], BF16)
        mt_sb = sb("mt_sb", [128, NJ * D], BF16)
        stats = sb("stats", [128, 12], F32)
        stats1 = sb("stats1", [128, 12], F32)
        xr1 = [sb(f"xr1_{i}", [128, D], F32) for i in range(2)]
        t1r1 = [sb(f"t1r1_{i}", [128, D], BF16) for i in range(2)]
        mv1_r = [sb(f"mv1_{i}", [128, 2], F32) for i in range(2)]
        lnv1_r = [sb(f"lnv1_{i}", [128, 1], F32) for i in range(2)]
        rstd1_r = [sb(f"rstd1_{i}", [128, 1], F32) for i in range(2)]
        negmur1_r = [sb(f"negmur1_{i}", [128, 1], F32) for i in range(2)]
        ln1_state = {"xr_reader": [None, None], "t1r_reader": [None, None],
                     "mv_readers": [[], []], "rstd_war": [None, None]}
        mv_r = [sb(f"mv{i}", [128, 2], F32) for i in range(4)]
        lnv_r = [sb(f"lnv{i}", [128, 1], F32) for i in range(4)]
        rstd_r = [sb(f"rstd{i}", [128, 1], F32) for i in range(4)]
        negmur_r = [sb(f"negmur{i}", [128, 1], F32) for i in range(4)]
        sums_r = [sb(f"sums{i}", [128, 1], F32) for i in range(4)]
        sumsq_r = [sb(f"sumsq{i}", [128, 1], F32) for i in range(4)]
        negm2_r = [sb(f"negm2{i}", [128, 1], F32) for i in range(4)]
        dumpt = sb("dumpt", [128, D], BF16)

        ld_ident = P.add("sync", lambda e: e.dma_start(out=identt[:, :], in_=ident_h), dma=True, q="cst")
        ld_ones64 = P.add("sync", lambda e: e.dma_start(out=ones64t[:, :], in_=ones64_h), dma=True, q="cst")
        ld_onesrow = P.add("sync", lambda e: e.dma_start(out=onesrowt[:, :], in_=onesrow_h), dma=True, q="cst")
        op_eps = P.add("dve", lambda e: e.memset(epsv[:, :], EPS))
        op_biasld = P.add("pdma", lambda e: e.dma_start(out=biasbf[:, :], in_=bias_h), dma=True, q="pcst")

        with ExitStack() as mctx:
            msb = lambda name, shape, dt: mctx.enter_context(nc.sbuf_tensor(name, shape, dt))
            csb = msb("csb", [128, NJ * D], BF16)
            snegsb = msb("snegsb", [128, NJ * D], BF16)
            wrsb = msb("wrsb", [128, NJ * D], BF16)
            wisb = msb("wisb", [128, NJ * D], BF16)
            cshsb = msb("cshsb", [128, NJ * 128], BF16)
            sshsb = msb("sshsb", [128, NJ * 128], BF16)
            uvsb = msb("uvsb", [128, 2048], BF16)
            mtcsb = msb("mtcsb", [128, D], BF16)
            pm = mctx.enter_context(nc.psum_tensor("pm", [128, 2048], F32))

            ld_c = P.add("sync", lambda e: e.dma_start(
                out=csb.ap().rearrange("p (c d) -> p c d", c=NJ), in_=chunked(c_h)), dma=True, q="cst")
            ld_sneg = P.add("sync", lambda e: e.dma_start(
                out=snegsb.ap().rearrange("p (c d) -> p c d", c=NJ), in_=chunked(sneg_h)), dma=True, q="cst")
            ld_csh = P.add("sync", lambda e: e.dma_start(
                out=cshsb.ap().rearrange("p (c d) -> p c d", c=NJ),
                in_=csh_h.rearrange("(c p) d -> p c d", p=128)), dma=True, q="cst")
            ld_ssh = P.add("sync", lambda e: e.dma_start(
                out=sshsb.ap().rearrange("p (c d) -> p c d", c=NJ),
                in_=ssh_h.rearrange("(c p) d -> p c d", p=128)), dma=True, q="cst")
            ld_wr = P.add("pdma", lambda e: e.dma_start(
                out=wrsb.ap().rearrange("p (c d) -> p c d", c=NJ), in_=chunked(wr_h)), dma=True, q="pcst")
            ld_wi = P.add("pdma", lambda e: e.dma_start(
                out=wisb.ap().rearrange("p (c d) -> p c d", c=NJ), in_=chunked(wi_h)), dma=True, q="pcst")

            sshnsb = msb("sshnsb", [128, NJ * 128], BF16)
            ld_sshn = P.add("pdma", lambda e: e.dma_start(
                out=sshnsb.ap().rearrange("p (c d) -> p c d", c=NJ),
                in_=sshn_h.rearrange("(c p) d -> p c d", p=128)), dma=True, q="pcst")

            mm_last = None
            first_dep = [ld_c, ld_sneg, ld_csh, ld_ssh, ld_sshn, ld_wr, ld_wi]
            emitted_first = False
            for half, pairs in ((0, ((wrsb, cshsb), (wisb, sshsb))),
                                (1024, ((wisb, cshsb), (wrsb, sshnsb)))):
                for ob in range(NJ):
                    def emit_s1(half=half, ob=ob, pairs=pairs):
                        def go(e):
                            last = None
                            n = 0
                            for j in range(NJ):
                                for (lh, rh) in pairs:
                                    n += 1
                                    last = e.matmul(
                                        pm[:, half + 128 * ob: half + 128 * ob + 128],
                                        lh[:, D * j + 128 * ob: D * j + 128 * ob + 128],
                                        rh[:, 128 * j:128 * j + 128],
                                        start=(n == 1), stop=(n == 2 * NJ))
                            return last
                        return go
                    mm_last = P.add("pe", emit_s1(),
                                    deps=first_dep if not emitted_first else ())
                    emitted_first = True
            ev_uv = P.add("dve",
                          (lambda: lambda e: e.tensor_copy(uvsb[:, :], pm[:, :]))(),
                          deps=[mm_last])
            mm_last2 = None
            for ob2 in range(2):
                def emit_s2(ob2=ob2):
                    def go(e):
                        last = None
                        n = 0
                        for oc in range(NJ):
                            for (loff, rh) in ((0, csb), (1024, snegsb)):
                                n += 1
                                last = e.matmul(
                                    pm[:, 1024 + 512 * ob2:1024 + 512 * ob2 + 512],
                                    uvsb[:, loff + 128 * oc: loff + 128 * oc + 128],
                                    rh[:, D * oc + 512 * ob2: D * oc + 512 * ob2 + 512],
                                    start=(n == 1), stop=(n == 2 * NJ))
                        return last
                    return go
                mm_last2 = P.add("pe", emit_s2(), deps=[ev_uv] if ob2 == 0 else ())
            ev_mtc = P.add("dve",
                           (lambda: lambda e: e.tensor_scalar(
                               mtcsb[:, :], pm[:, 1024:2048], 1.0 / D, None, ALU.mult))(),
                           deps=[mm_last2])
            dma_ccin = P.add("pdma",
                             (lambda: lambda e: e.dma_start(out=ccin, in_=mtcsb[:, :]))(),
                             deps=[ev_mtc], dma=True, q="cc")
            cc = P.add("pool",
                       (lambda: lambda e: e.collective_compute(
                           "AllGather", ALU.bypass,
                           replica_groups=[list(range(NCORES))],
                           ins=[ccin.opt()], outs=[ccout.opt()]))(),
                       deps=[dma_ccin])
            cc_op = cc
        mt_last = None

        bufA = sb("bufA", [128, NJ * SEQ], BF16)
        bufQ = sb("bufQ", [128, NJ * SEQ], BF16)
        bufK = sb("bufK", [128, NJ * SEQ], BF16)
        bufV = sb("bufV", [128, TC * D], BF16)
        qz_readA = [None, None]
        qz_readB = [None, None]
        stats_reader = [None]
        prev_wq_readers = []

        prev_batch_last = None
        for b in range(BPC):
            with ExitStack() as bctx:
                t1T = bufA
                qT = bufQ
                kT = bufK
                v_sb = bufV

                def make_ln1(b, t1T, tpl, tp_extra, evac_extra):
                    st = ln1_state
                    ctx_ops = {}
                    tr1_of = {}
                    evacs = []

                    def ln1_iter(k):
                        if k < TC:
                            tc = k
                            sl = tc % 2
                            ld = P.add("sync",
                                       (lambda b=b, tc=tc, sl=sl:
                                        lambda e: e.dma_start(out=xr1[sl][:, :],
                                                              in_=x_hbm[b, tc * 128:(tc + 1) * 128, :]))(),
                                       deps=[st["xr_reader"][sl]], dma=True,
                                       q="xa" if sl == 0 else "xb")
                            st1 = P.add("dve",
                                        (lambda sl=sl: lambda e: (
                                            e.bn_stats(stats1[:, 0:6], xr1[sl][:, 0:512]),
                                            e.bn_stats(stats1[:, 6:12], xr1[sl][:, 512:1024]))[-1])(),
                                        deps=[ld, st.get("stats1_reader")])
                            ag = P.add("dve",
                                       (lambda sl=sl: lambda e: e.bn_aggr(mv1_r[sl][:, :], stats1[:, :]))(),
                                       deps=[st1] + st["mv_readers"][sl])
                            st["stats1_reader"] = ag
                            ln = P.add("act",
                                       (lambda sl=sl: lambda e: e.activation(
                                           lnv1_r[sl][:, :], mv1_r[sl][:, 1:2], AF.Ln, bias=epsv[:, 0:1]))(),
                                       deps=[ag, op_eps])
                            ex = P.add("act",
                                       (lambda sl=sl: lambda e: e.activation(
                                           rstd1_r[sl][:, :], lnv1_r[sl][:, :], AF.Exp, scale=-0.5))(),
                                       deps=[ln, st["rstd_war"][sl]])
                            ctx_ops[tc] = (ld, ag, ln, ex)
                        if k >= 1 and k - 1 < TC:
                            tc = k - 1
                            sl = tc % 2
                            (ld, ag, ln, ex) = ctx_ops[tc]
                            nm = P.add("dve",
                                       (lambda sl=sl: lambda e: e.scalar_tensor_tensor(
                                           negmur1_r[sl][:, :], mv1_r[sl][:, 0:1], -1.0,
                                           rstd1_r[sl][:, :], ALU.mult, ALU.mult))(),
                                       deps=[ex, ag])
                            ap_ = P.add("dve",
                                        (lambda sl=sl: lambda e: e.tensor_scalar(
                                            t1r1[sl][:, :], xr1[sl][:, :], rstd1_r[sl][:, 0:1],
                                            negmur1_r[sl][:, 0:1], ALU.mult, ALU.add))(),
                                        deps=[nm, st["t1r_reader"][sl]])
                            st["mv_readers"][sl] = [nm, ln]
                            st["rstd_war"][sl] = ap_
                            st["xr_reader"][sl] = ap_
                            tr = None
                            for dc in range(NJ):
                                tr = P.add("pe",
                                           (lambda sl=sl, dc=dc: lambda e: e.transpose(
                                               tpl[sl][:, 128 * dc:128 * dc + 128],
                                               t1r1[sl][:, 128 * dc:128 * dc + 128],
                                               identt[:, :]))(),
                                           deps=[ap_, st.get("tp_reader", [None, None])[sl],
                                                 ld_ident] + tp_extra if dc == 0 else [])
                            st["t1r_reader"][sl] = tr
                            tr1_of[tc] = tr
                        if k >= 2 and k - 2 < TC:
                            tc = k - 2
                            sl = tc % 2
                            ev = P.add("dve",
                                       (lambda sl=sl, tc=tc: lambda e: e.tensor_copy(
                                           t1T.ap().rearrange("p (c t) -> p c t", c=NJ)[:, :, 128 * tc:128 * tc + 128],
                                           tpl[sl].ap().rearrange("p (c t) -> p c t", c=NJ)))(),
                                       deps=[tr1_of[tc]] + evac_extra)
                            st.setdefault("tp_reader", [None, None])[sl] = ev
                            evacs.append(ev)
                    return ln1_iter, evacs

                with ExitStack() as lctx:
                    tpl0 = [lctx.enter_context(nc.psum_tensor(f"tpl0_{i}_b{b}", [128, 1024], BF16))
                            for i in range(2)]
                    ln1_state["tp_reader"] = [None, None]
                    ln1_it, ln1_evacs = make_ln1(b, t1T, tpl0, [], [])
                    for k in range(TC + 2):
                        ln1_it(k)
                    t1T_ready = ln1_evacs[-1]
                if mt_last is None:
                    mt_last = P.add("sync",
                                    (lambda: lambda e: e.dma_start(
                                        out=mt_sb.ap().rearrange("p (c d) -> p c d", c=NJ),
                                        in_=ccout.rearrange("(c p) d -> p c d", p=128)))(),
                                    deps=[cc_op], dma=True, q="mt")

                with ExitStack() as wctx:
                    wq_sb = wctx.enter_context(nc.sbuf_tensor(f"wq_b{b}", [128, NJ * D], BF16))
                    wk_sb = wctx.enter_context(nc.sbuf_tensor(f"wk_b{b}", [128, NJ * D], BF16))
                    pp = wctx.enter_context(nc.psum_tensor(f"pp_b{b}", [128, 2048], F32))
                    ld_wq = P.add("pdma", (lambda e: e.dma_start(
                        out=wq_sb.ap().rearrange("p (c d) -> p c d", c=NJ), in_=chunked(wqt_h))),
                        deps=[t1T_ready, prev_batch_last] + prev_wq_readers,
                        dma=True, q="wa")
                    ld_wk = P.add("pdma", (lambda e: e.dma_start(
                        out=wk_sb.ap().rearrange("p (c d) -> p c d", c=NJ), in_=chunked(wkt_h))),
                        deps=[t1T_ready, prev_batch_last] + prev_wq_readers,
                        dma=True, q="wb")

                    def project_fmajor(dst_sb, w_sb, deps0):
                        evs = []
                        slot_ev = [None, None]
                        un = 0
                        for ec in range(NJ):
                            for tbh in range(2):
                                slot = un % 2
                                first_deps = [t1T_ready] + deps0 + (
                                    [slot_ev[slot]] if slot_ev[slot] else [])
                                mm_last = None
                                for j in range(NJ):
                                    lhs = w_sb[:, D * j + 128 * ec: D * j + 128 * ec + 128]
                                    items = [
                                        (pp[:, 1024 * slot + 512 * tb:1024 * slot + 512 * tb + 512],
                                         lhs,
                                         t1T[:, SEQ * j + 1024 * tbh + 512 * tb:
                                             SEQ * j + 1024 * tbh + 512 * tb + 512],
                                         j == 0, j == NJ - 1, None)
                                        for tb in range(2)]
                                    mm_last = P.add(
                                        "pe",
                                        (lambda items=items: lambda e: mm_group(e, items))(),
                                        deps=first_deps if j == 0 else ())
                                ev = P.add("dve",
                                           (lambda ec=ec, tbh=tbh, slot=slot, dst_sb=dst_sb:
                                            lambda e: e.tensor_copy(
                                                dst_sb[:, SEQ * ec + 1024 * tbh:
                                                       SEQ * ec + 1024 * tbh + 1024],
                                                pp[:, 1024 * slot:1024 * slot + 1024]))(),
                                           deps=[mm_last])
                                slot_ev[slot] = ev
                                evs.append(ev)
                                un += 1
                        return evs

                    q_done = project_fmajor(qT, wq_sb, [ld_wq])
                    k_done = project_fmajor(kT, wk_sb, [ld_wk, q_done[-1]])
                    ld_wv = P.add("pdma", (lambda e: e.dma_start(
                        out=wq_sb.ap().rearrange("p (c d) -> p c d", c=NJ), in_=chunked(wvt_h))),
                        deps=[q_done[-1]], dma=True, q="wa")
                    v_evs = []
                    for tc in range(TC):
                        mm_last = None
                        for j in range(NJ):
                            lhs = t1T[:, SEQ * j + 128 * tc: SEQ * j + 128 * tc + 128]
                            items = [
                                (pp[:, 512 * (2 * (tc % 2) + cb):512 * (2 * (tc % 2) + cb) + 512],
                                 lhs,
                                 wq_sb[:, D * j + 512 * cb: D * j + 512 * cb + 512],
                                 j == 0, j == NJ - 1, None)
                                for cb in range(2)]
                            dep0 = ()
                            if j == 0:
                                dep0 = [ld_wv, k_done[-1]]
                                if len(v_evs) >= 2:
                                    dep0 = dep0 + [v_evs[-2]]
                            mm_last = P.add(
                                "pe",
                                (lambda items=items: lambda e: mm_group(e, items))(),
                                deps=dep0)
                        ev = P.add("dve",
                                   (lambda tc=tc: lambda e: e.tensor_copy(
                                       v_sb[:, D * tc:D * tc + D],
                                       pp[:, 1024 * (tc % 2):1024 * (tc % 2) + 1024]))(),
                                   deps=[mm_last])
                        v_evs.append(ev)
                    v_done = v_evs[-1]
                    prev_wq_readers = list(v_evs) + [q_done[-1], k_done[-1]]

                ctxT = bufA
                with ExitStack() as actx:
                    asb = lambda name, shape, dt: actx.enter_context(
                        nc.sbuf_tensor(f"{name}_b{b}", shape, dt))
                    psa = actx.enter_context(nc.psum_tensor(f"psa_b{b}", [128, 1024], F32))
                    psb = actx.enter_context(nc.psum_tensor(f"psb_b{b}", [128, 1024], F32))
                    ct2 = [actx.enter_context(nc.psum_tensor(f"ct{i}_b{b}", [128, 1024], F32))
                           for i in range(2)]
                    expS = [asb(f"expS{i}", [128, 2 * 1024], BF16) for i in range(2)]
                    zacc = asb("zacc", [128, 2 * 1024], BF16)
                    invz = asb("invz", [128, 1024], F32)
                    qzA = asb("qzA", [128, 2048], BF16)
                    qzB = asb("qzB", [128, 2048], BF16)
                    mzA = P.add("dve",
                                lambda e: e.memset(qzA[64:128, :], 0.0),
                                deps=[qz_readA[0], qz_readA[1], prev_batch_last])
                    mzB = P.add("dve",
                                lambda e: e.memset(qzB[0:64, :], 0.0),
                                deps=[qz_readB[0], qz_readB[1], prev_batch_last])

                    last_expA = None
                    last_expB = None
                    slot_consumers = [[], []]
                    ct_last_reader = [None, None]
                    last_zacc_readerA = None
                    last_zacc_readerB = None
                    last_zaA = None
                    last_zaB = None
                    last_tt = None

                    steps = [(p, qh, kb) for p in range(H // 2)
                             for qh in range(2) for kb in range(TC)]

                    qz_cpA = {}
                    qz_cpB = {}

                    def emit_qz(u):
                        p, qh = divmod(u, 2)
                        su = u % 2
                        qbase = SEQ * p + 1024 * qh
                        cA = P.add("dve",
                                   (lambda su=su, qbase=qbase: lambda e: e.tensor_copy(
                                       qzA[0:64, 1024 * su:1024 * su + 1024],
                                       qT[0:64, qbase:qbase + 1024]))(),
                                   deps=[q_done[-1], mzA, qz_readA[su]])
                        cB = P.add("dve",
                                   (lambda su=su, qbase=qbase: lambda e: e.tensor_copy(
                                       qzB[64:128, 1024 * su:1024 * su + 1024],
                                       qT[64:128, qbase:qbase + 1024]))(),
                                   deps=[q_done[-1], mzB, qz_readB[su]])
                        qz_cpA[u] = cA
                        qz_cpB[u] = cB

                    def emit_scores_A(step, extra):
                        p, qh, kb = step
                        u = p * 2 + qh
                        su = u % 2
                        lhsA = kT[:, SEQ * p + 128 * kb: SEQ * p + 128 * kb + 128]
                        items = [
                            (psa[:, 512 * i:512 * i + 512], lhsA,
                             qzA[:, 1024 * su + 512 * i: 1024 * su + 512 * i + 512],
                             True, True, None) for i in range(2)]
                        g = P.add("pe",
                                  (lambda items=items: lambda e: mm_group(e, items))(),
                                  deps=[last_expA, qz_cpA[u]] + extra)
                        qz_readA[su] = g
                        return g

                    def emit_scores_B(step, extra):
                        p, qh, kb = step
                        u = p * 2 + qh
                        su = u % 2
                        lhsB = kT[:, SEQ * p + 128 * kb: SEQ * p + 128 * kb + 128]
                        items = [
                            (psb[:, 512 * i:512 * i + 512], lhsB,
                             qzB[:, 1024 * su + 512 * i: 1024 * su + 512 * i + 512],
                             True, True, None) for i in range(2)]
                        g = P.add("pe",
                                  (lambda items=items: lambda e: mm_group(e, items))(),
                                  deps=[last_expB, qz_cpB[u]] + extra)
                        qz_readB[su] = g
                        return g

                    ext0 = [q_done[-1], k_done[-1], v_done]
                    pending_Z = None
                    emit_qz(0)
                    pend_A = emit_scores_A(steps[0], ext0)
                    pend_B = emit_scores_B(steps[0], ext0)
                    for i, (p, qh, kb) in enumerate(steps):
                        u = p * 2 + qh
                        qbase = SEQ * p + 1024 * qh
                        cts = ct2[u % 2]
                        sl = i % 2
                        if kb == 0 and u + 1 < 2 * (H // 2):
                            emit_qz(u + 1)
                        sA, sB = pend_A, pend_B
                        exA = P.add("act",
                                    (lambda sl=sl: lambda e: e.activation(
                                        expS[sl][:, 0:1024], psa[:, :],
                                        AF.Exp, scale=QSCALE))(),
                                    deps=[sA] + slot_consumers[sl])
                        exB = P.add("act",
                                    (lambda sl=sl: lambda e: e.activation(
                                        expS[sl][:, 1024:2048], psb[:, :],
                                        AF.Exp, scale=QSCALE))(),
                                    deps=[sB])
                        iz_gate = None
                        if pending_Z is not None:
                            bcB2, qb2, cts2, cB2, zct = pending_Z
                            pending_Z = None
                            lnz = P.add("act",
                                        (lambda zct=zct: lambda e: e.activation(
                                            zct[:, :], zct[:, :], AF.Ln))(),
                                        deps=[bcB2])
                            izz = P.add("act",
                                        (lambda zct=zct: lambda e: e.activation(
                                            invz[:, :], zct[:, :], AF.Exp,
                                            scale=-1.0))(),
                                        deps=[lnz, last_tt])
                            tt = P.add("dve",
                                       (lambda qb2=qb2, cts2=cts2: lambda e: e.tensor_tensor(
                                           ctxT[:, qb2:qb2 + 1024],
                                           cts2[:, :], invz[:, :], ALU.mult))(),
                                       deps=[izz, cB2])
                            ct_last_reader[(u - 1) % 2] = tt
                            last_tt = tt
                            iz_gate = izz
                        slot_consumers[sl] = []
                        last_expA, last_expB = exA, exB
                        last_step = (i + 1 == len(steps))
                        cdepA = [exA]
                        if kb == 0 and ct_last_reader[u % 2] is not None:
                            cdepA.append(ct_last_reader[u % 2])
                        if kb == 0 and iz_gate is not None:
                            cdepA.append(iz_gate)
                        if not last_step:
                            pend_A = emit_scores_A(steps[i + 1], [])
                        lhscA = v_sb[:, D * kb + HD * (2 * p): D * kb + HD * (2 * p) + HD]
                        itemsCA = [
                            (cts[0:64, 512 * i2:512 * i2 + 512], lhscA,
                             expS[sl][:, 512 * i2:512 * i2 + 512],
                             kb == 0, kb == TC - 1, (0, 0)) for i2 in range(2)]
                        cA = P.add("pe",
                                   (lambda items=itemsCA: lambda e: mm_group(e, items))(),
                                   deps=cdepA)
                        if not last_step:
                            pend_B = emit_scores_B(steps[i + 1], [])
                        lhscB = v_sb[:, D * kb + HD * (2 * p + 1): D * kb + HD * (2 * p + 1) + HD]
                        itemsCB = [
                            (cts[64:128, 512 * i2:512 * i2 + 512], lhscB,
                             expS[sl][:, 1024 + 512 * i2: 1024 + 512 * i2 + 512],
                             kb == 0, kb == TC - 1, (0, 64)) for i2 in range(2)]
                        cB = P.add("pe",
                                   (lambda items=itemsCB: lambda e: mm_group(e, items))(),
                                   deps=[exB])
                        if kb == 0:
                            zaA = P.add("dve",
                                        (lambda sl=sl: lambda e: e.tensor_copy(
                                            zacc[:, 0:1024], expS[sl][:, 0:1024]))(),
                                        deps=[exA, last_zacc_readerA])
                            zaB = P.add("dve",
                                        (lambda sl=sl: lambda e: e.tensor_copy(
                                            zacc[:, 1024:2048], expS[sl][:, 1024:2048]))(),
                                        deps=[exB, last_zacc_readerB])
                        else:
                            zaA = P.add("dve",
                                        (lambda sl=sl: lambda e: e.tensor_tensor(
                                            zacc[:, 0:1024], zacc[:, 0:1024],
                                            expS[sl][:, 0:1024], ALU.add))(),
                                        deps=[exA, last_zaA])
                            zaB = P.add("dve",
                                        (lambda sl=sl: lambda e: e.tensor_tensor(
                                            zacc[:, 1024:2048], zacc[:, 1024:2048],
                                            expS[sl][:, 1024:2048], ALU.add))(),
                                        deps=[exB, last_zaB])
                        last_zaA, last_zaB = zaA, zaB
                        slot_consumers[sl] += [cA, cB, zaA, zaB]
                        if kb == TC - 1:
                            zct = ct2[(u + 1) % 2]
                            itemsZA = [
                                (zct[0:64, 512 * j:512 * j + 512], ones64t[:, :],
                                 zacc[:, 512 * j:512 * j + 512], True, True, (0, 0))
                                for j in range(2)]
                            bcA = P.add("pe",
                                        (lambda items=itemsZA: lambda e: mm_group(e, items))(),
                                        deps=[last_zaA, ld_ones64, last_tt,
                                              ct_last_reader[(u + 1) % 2]])
                            itemsZB = [
                                (zct[64:128, 512 * j:512 * j + 512], ones64t[:, :],
                                 zacc[:, 1024 + 512 * j: 1024 + 512 * j + 512],
                                 True, True, (0, 64)) for j in range(2)]
                            bcB = P.add("pe",
                                        (lambda items=itemsZB: lambda e: mm_group(e, items))(),
                                        deps=[last_zaB])
                            last_zacc_readerA = bcA
                            last_zacc_readerB = bcB
                            pending_Z = (bcB, qbase, cts, cB, zct)
                            if last_step:
                                bcB2, qb2, cts2, cB2, zct2 = pending_Z
                                pending_Z = None
                                lnz = P.add("act",
                                            (lambda zct2=zct2: lambda e: e.activation(
                                                zct2[:, :], zct2[:, :], AF.Ln))(),
                                            deps=[bcB2])
                                izf = P.add("act",
                                            (lambda zct2=zct2: lambda e: e.activation(
                                                invz[:, :], zct2[:, :], AF.Exp,
                                                scale=-1.0))(),
                                            deps=[lnz, last_tt])
                                tt = P.add("dve",
                                           (lambda qb2=qb2, cts2=cts2: lambda e: e.tensor_tensor(
                                               ctxT[:, qb2:qb2 + 1024],
                                               cts2[:, :], invz[:, :], ALU.mult))(),
                                           deps=[izf, cB2])
                                ct_last_reader[u % 2] = tt
                                last_tt = tt
                    attn_done = last_tt

                t2T = bufQ
                with ExitStack() as octx:
                    osb = lambda name, shape, dt: octx.enter_context(
                        nc.sbuf_tensor(f"{name}_b{b}", shape, dt))
                    ao2 = [octx.enter_context(nc.psum_tensor(f"ao{i}_b{b}", [128, 1024], F32))
                           for i in range(2)]
                    tp2 = [octx.enter_context(nc.psum_tensor(f"tp2_{i}_b{b}", [128, 1024], BF16))
                           for i in range(2)]
                    wo_sb = osb("wo", [128, NJ * D], BF16)
                    xr2 = [osb(f"xr2{i}", [128, D], F32) for i in range(2)]
                    tmpx = [osb(f"tmpx{i}", [128, D], F32) for i in range(2)]
                    x1r = [osb(f"x1r{i}", [128, D], F32) for i in range(2)]
                    t2r = [osb(f"t2r{i}", [128, D], BF16) for i in range(2)]
                    ld_wo = P.add("pdma", (lambda e: e.dma_start(
                        out=wo_sb.ap().rearrange("p (c d) -> p c d", c=NJ), in_=chunked(wot_h))),
                        deps=[attn_done], dma=True, q="wb")

                    xr2_reader = [None, None]
                    stx_reader = [None, None]
                    sums_reader = [None] * 4
                    sumsq_reader = [None] * 4
                    t2r_reader = [None, None]
                    tp2_reader = [None, None]
                    mv_readers = [[] for _ in range(4)]
                    rstd_war = [None] * 4
                    ao_reader = [None, None]
                    mm_of = {}
                    ld_of = {}
                    t2_of = {}
                    t2T_evacs = []
                    x1_stores = []

                    def oproj_mms(tc):
                        sl = tc % 2
                        ld = P.add("sync",
                                   (lambda b=b, tc=tc, sl=sl:
                                    lambda e: e.dma_start(out=xr2[sl][:, :],
                                                          in_=x_hbm[b, tc * 128:(tc + 1) * 128, :]))(),
                                   deps=[xr2_reader[sl], attn_done if tc < 2 else None],
                                   dma=True, q="xc" if sl == 0 else "xd")
                        ld_of[tc] = ld
                        mm_last = None
                        for cc in range(NJ):
                            lhs = ctxT[:, SEQ * cc + 128 * tc: SEQ * cc + 128 * tc + 128]
                            items = [
                                (ao2[sl][:, 512 * eb:512 * eb + 512], lhs,
                                 wo_sb[:, D * cc + 512 * eb: D * cc + 512 * eb + 512],
                                 cc == 0, cc == NJ - 1, None)
                                for eb in range(2)]
                            dep0 = [attn_done, ld_wo, ao_reader[sl]] if cc == 0 else ()
                            mm_last = P.add(
                                "pe",
                                (lambda items=items: lambda e: mm_group(e, items))(),
                                deps=dep0)
                        mm_of[tc] = mm_last

                    def oproj_chain(tc):
                        sl = tc % 2
                        sl4 = tc % 4
                        ao = ao2[sl]
                        mm_last = mm_of[tc]
                        ld = ld_of[tc]
                        sm_ = P.add("act",
                                    (lambda sl=sl, sl4=sl4, ao=ao: lambda e: e.activation(
                                        dumpt[:, :], ao[:, :], AF.Identity,
                                        accum_out=sums_r[sl4][:, 0:1]))(),
                                    deps=[mm_last, sums_reader[sl4]])
                        sq_ = P.add("act",
                                    (lambda sl=sl, sl4=sl4, ao=ao: lambda e: e.activation(
                                        dumpt[:, :], ao[:, :], AF.Square,
                                        accum_out=sumsq_r[sl4][:, 0:1]))(),
                                    deps=[sm_, sumsq_reader[sl4]])
                        mean_ = P.add("dve",
                                      (lambda sl=sl, sl4=sl4: lambda e: e.tensor_scalar(
                                          mv_r[sl4][:, 0:1], sums_r[sl4][:, 0:1],
                                          1.0 / D, None, ALU.mult))(),
                                      deps=[sm_] + mv_readers[sl4])
                        sums_reader[sl4] = mean_
                        nm2_ = P.add("dve",
                                     (lambda sl=sl, sl4=sl4: lambda e: e.scalar_tensor_tensor(
                                         negm2_r[sl4][:, 0:1], mv_r[sl4][:, 0:1], -1.0,
                                         mv_r[sl4][:, 0:1], ALU.mult, ALU.mult))(),
                                     deps=[mean_])
                        ag = P.add("dve",
                                   (lambda sl=sl, sl4=sl4: lambda e: e.scalar_tensor_tensor(
                                       mv_r[sl4][:, 1:2], sumsq_r[sl4][:, 0:1], 1.0 / D,
                                       negm2_r[sl4][:, 0:1], ALU.mult, ALU.add))(),
                                   deps=[sq_, nm2_])
                        sumsq_reader[sl4] = ag
                        ln = P.add("act",
                                   (lambda sl=sl, sl4=sl4: lambda e: e.activation(
                                       lnv_r[sl4][:, :], mv_r[sl4][:, 1:2], AF.Ln, bias=epsv[:, 0:1]))(),
                                   deps=[ag])
                        ex = P.add("act",
                                   (lambda sl=sl, sl4=sl4: lambda e: e.activation(
                                       rstd_r[sl4][:, :], lnv_r[sl4][:, :], AF.Exp, scale=-0.5))(),
                                   deps=[ln, rstd_war[sl4]])
                        nm = P.add("dve",
                                   (lambda sl=sl, sl4=sl4: lambda e: e.scalar_tensor_tensor(
                                       negmur_r[sl4][:, :], mv_r[sl4][:, 0:1], -1.0,
                                       rstd_r[sl4][:, :], ALU.mult, ALU.mult))(),
                                   deps=[ex, ag])
                        tx = P.add("dve",
                                   (lambda sl=sl, sl4=sl4: lambda e: e.tensor_scalar(
                                       tmpx[sl][:, :], xr2[sl][:, :], negmur_r[sl4][:, 0:1],
                                       None, ALU.add))(),
                                   deps=[nm, ld])
                        x1 = P.add("dve",
                                   (lambda sl=sl, sl4=sl4, ao=ao: lambda e: e.scalar_tensor_tensor(
                                       x1r[sl][:, :], ao[:, :], rstd_r[sl4][:, 0:1],
                                       tmpx[sl][:, :], ALU.mult, ALU.add))(),
                                   deps=[tx, stx_reader[sl]])
                        xr2_reader[sl] = tx
                        ao_reader[sl] = x1
                        stx = P.add("sync",
                                    (lambda tc=tc, sl=sl, sl4=sl4: lambda e: e.dma_start(
                                        out=x1_scr[tc * 128:(tc + 1) * 128, :], in_=x1r[sl][:, :]))(),
                                    deps=[x1], dma=True, q="sa" if sl == 0 else "sb")
                        x1_stores.append(stx)
                        stx_reader[sl] = stx
                        st2 = P.add("dve",
                                    (lambda sl=sl, sl4=sl4: lambda e: (
                                        e.bn_stats(stats[:, 0:6], x1r[sl][:, 0:512]),
                                        e.bn_stats(stats[:, 6:12], x1r[sl][:, 512:1024]))[-1])(),
                                    deps=[x1, stats_reader[0]])
                        ag2 = P.add("dve",
                                    (lambda sl=sl, sl4=sl4: lambda e: e.bn_aggr(mv_r[sl4][:, :], stats[:, :]))(),
                                    deps=[st2, nm, ln])
                        stats_reader[0] = ag2
                        ln2 = P.add("act",
                                    (lambda sl=sl, sl4=sl4: lambda e: e.activation(
                                        lnv_r[sl4][:, :], mv_r[sl4][:, 1:2], AF.Ln, bias=epsv[:, 0:1]))(),
                                    deps=[ag2, ex])
                        ex2 = P.add("act",
                                    (lambda sl=sl, sl4=sl4: lambda e: e.activation(
                                        rstd_r[sl4][:, :], lnv_r[sl4][:, :], AF.Exp, scale=-0.5))(),
                                    deps=[ln2, x1])
                        nm2 = P.add("dve",
                                    (lambda sl=sl, sl4=sl4: lambda e: e.scalar_tensor_tensor(
                                        negmur_r[sl4][:, :], mv_r[sl4][:, 0:1], -1.0,
                                        rstd_r[sl4][:, :], ALU.mult, ALU.mult))(),
                                    deps=[ex2, ag2, tx])
                        t2 = P.add("dve",
                                   (lambda sl=sl, sl4=sl4: lambda e: e.tensor_scalar(
                                       t2r[sl][:, :], x1r[sl][:, :], rstd_r[sl4][:, 0:1],
                                       negmur_r[sl4][:, 0:1], ALU.mult, ALU.add))(),
                                   deps=[nm2, t2r_reader[sl]])
                        mv_readers[sl4] = [nm2, ln2]
                        rstd_war[sl4] = t2
                        t2_of[tc] = t2

                    tr2_of = {}

                    def oproj_trans(tc):
                        sl = tc % 2
                        tr = None
                        for dc in range(NJ):
                            tr = P.add("pe",
                                       (lambda sl=sl, dc=dc: lambda e: e.transpose(
                                           tp2[sl][:, 128 * dc:128 * dc + 128],
                                           t2r[sl][:, 128 * dc:128 * dc + 128],
                                           identt[:, :]))(),
                                       deps=[t2_of[tc], tp2_reader[sl]] if dc == 0 else [])
                        t2r_reader[sl] = tr
                        tr2_of[tc] = tr

                    def oproj_evac(tc):
                        sl = tc % 2
                        ev = P.add("dve",
                                   (lambda sl=sl, tc=tc: lambda e: e.tensor_copy(
                                       t2T.ap().rearrange("p (c t) -> p c t", c=NJ)[:, :, 128 * tc:128 * tc + 128],
                                       tp2[sl].ap().rearrange("p (c t) -> p c t", c=NJ)))(),
                                   deps=[tr2_of[tc]])
                        tp2_reader[sl] = ev
                        t2T_evacs.append(ev)

                    for k in range(TC + 3):
                        if k < TC:
                            oproj_mms(k)
                        if k >= 2 and k - 2 < TC:
                            oproj_trans(k - 2)
                        if k >= 1 and k - 1 < TC:
                            oproj_chain(k - 1)
                        if k >= 3 and k - 3 < TC:
                            oproj_evac(k - 3)
                    t2T_ready = t2T_evacs[-1]

                with ExitStack() as fctx:
                    fsb = lambda name, shape, dt: fctx.enter_context(
                        nc.sbuf_tensor(f"{name}_b{b}", shape, dt))
                    fo = [fctx.enter_context(nc.psum_tensor(f"fo{i}_b{b}", [128, 1024], F32))
                          for i in range(2)]
                    x1l = [fsb(f"x1l{i}", [128, D], F32) for i in range(2)]
                    tmpf = [fsb(f"tmpf{i}", [128, D], F32) for i in range(2)]
                    outr = [fsb(f"outr{i}", [128, D], F32) for i in range(2)]

                    x1l_reader = [None, None]
                    outr_reader = [None, None]
                    fsums_reader = [None] * 4
                    fsumsq_reader = [None] * 4
                    mv_readers = [[] for _ in range(4)]
                    rstd_war = [None] * 4
                    fo_reader = [None, None]
                    fmm_of = {}
                    fld_of = {}

                    def ffn_mms(tc):
                        sl = tc % 2
                        ld = P.add("sync",
                                   (lambda tc=tc, sl=sl: lambda e: e.dma_start(
                                       out=x1l[sl][:, :],
                                       in_=x1_scr[tc * 128:(tc + 1) * 128, :]))(),
                                   deps=[x1l_reader[sl], x1_stores[tc],
                                         t2T_ready if tc < 2 else None],
                                   dma=True, q="xe" if sl == 0 else "xf")
                        fld_of[tc] = ld
                        mm_last = None
                        for j in range(NJ):
                            lhs = t2T[:, SEQ * j + 128 * tc: SEQ * j + 128 * tc + 128]
                            items = [
                                (fo[sl][:, 512 * eb:512 * eb + 512], lhs,
                                 mt_sb[:, D * j + 512 * eb: D * j + 512 * eb + 512],
                                 j == 0, False, None)
                                for eb in range(2)]
                            dep0 = [t2T_ready, fo_reader[sl], op_biasld, ld_onesrow, mt_last] if j == 0 else ()
                            mm_last = P.add(
                                "pe",
                                (lambda items=items: lambda e: mm_group(e, items))(),
                                deps=dep0)
                        items = [
                            (fo[sl][:, 512 * eb:512 * eb + 512], onesrowt[:, :],
                             biasbf[:, 512 * eb:512 * eb + 512], False, True, None)
                            for eb in range(2)]
                        mm_last = P.add(
                            "pe",
                            (lambda items=items: lambda e: mm_group(e, items))(),
                            deps=())
                        fmm_of[tc] = mm_last

                    def ffn_chain(tc):
                        sl = tc % 2
                        sl4 = tc % 4
                        mm_last = fmm_of[tc]
                        ld = fld_of[tc]
                        sm_ = P.add("act",
                                    (lambda sl=sl, sl4=sl4: lambda e: e.activation(
                                        dumpt[:, :], fo[sl][:, :], AF.Identity,
                                        accum_out=sums_r[sl4][:, 0:1]))(),
                                    deps=[mm_last, fsums_reader[sl4], sums_reader[sl4]])
                        sq_ = P.add("act",
                                    (lambda sl=sl, sl4=sl4: lambda e: e.activation(
                                        dumpt[:, :], fo[sl][:, :], AF.Square,
                                        accum_out=sumsq_r[sl4][:, 0:1]))(),
                                    deps=[sm_, fsumsq_reader[sl4], sumsq_reader[sl4]])
                        mean_ = P.add("dve",
                                      (lambda sl=sl, sl4=sl4: lambda e: e.tensor_scalar(
                                          mv_r[sl4][:, 0:1], sums_r[sl4][:, 0:1],
                                          1.0 / D, None, ALU.mult))(),
                                      deps=[sm_] + mv_readers[sl4])
                        fsums_reader[sl4] = mean_
                        nm2_ = P.add("dve",
                                     (lambda sl=sl, sl4=sl4: lambda e: e.scalar_tensor_tensor(
                                         negm2_r[sl4][:, 0:1], mv_r[sl4][:, 0:1], -1.0,
                                         mv_r[sl4][:, 0:1], ALU.mult, ALU.mult))(),
                                     deps=[mean_])
                        ag = P.add("dve",
                                   (lambda sl=sl, sl4=sl4: lambda e: e.scalar_tensor_tensor(
                                       mv_r[sl4][:, 1:2], sumsq_r[sl4][:, 0:1], 1.0 / D,
                                       negm2_r[sl4][:, 0:1], ALU.mult, ALU.add))(),
                                   deps=[sq_, nm2_])
                        fsumsq_reader[sl4] = ag
                        ln = P.add("act",
                                   (lambda sl=sl, sl4=sl4: lambda e: e.activation(
                                       lnv_r[sl4][:, :], mv_r[sl4][:, 1:2], AF.Ln, bias=epsv[:, 0:1]))(),
                                   deps=[ag])
                        ex = P.add("act",
                                   (lambda sl=sl, sl4=sl4: lambda e: e.activation(
                                       rstd_r[sl4][:, :], lnv_r[sl4][:, :], AF.Exp, scale=-0.5))(),
                                   deps=[ln, rstd_war[sl4]])
                        nm = P.add("dve",
                                   (lambda sl=sl, sl4=sl4: lambda e: e.scalar_tensor_tensor(
                                       negmur_r[sl4][:, :], mv_r[sl4][:, 0:1], -1.0,
                                       rstd_r[sl4][:, :], ALU.mult, ALU.mult))(),
                                   deps=[ex, ag])
                        tx = P.add("dve",
                                   (lambda sl=sl, sl4=sl4: lambda e: e.tensor_scalar(
                                       tmpf[sl][:, :], x1l[sl][:, :], negmur_r[sl4][:, 0:1],
                                       None, ALU.add))(),
                                   deps=[nm, ld])
                        x2 = P.add("dve",
                                   (lambda sl=sl, sl4=sl4: lambda e: e.scalar_tensor_tensor(
                                       outr[sl][:, :], fo[sl][:, :], rstd_r[sl4][:, 0:1],
                                       tmpf[sl][:, :], ALU.mult, ALU.add))(),
                                   deps=[tx, outr_reader[sl]])
                        rstd_war[sl4] = x2
                        x1l_reader[sl] = tx
                        mv_readers[sl4] = [nm, ln]
                        fo_reader[sl] = x2
                        sto = P.add("sync",
                                    (lambda b=b, tc=tc, sl=sl, sl4=sl4: lambda e: e.dma_start(
                                        out=y_hbm[b, tc * 128:(tc + 1) * 128, :],
                                        in_=outr[sl][:, :]))(),
                                    deps=[x2], dma=True, q="ya" if sl == 0 else "yb")
                        outr_reader[sl] = sto
                        out_dmas.append(sto)
                        return x2

                    for k in range(TC + 1):
                        if k < TC:
                            ffn_mms(k)
                        if k >= 1:
                            x2op = ffn_chain(k - 1)
                    prev_batch_last = x2op

        for od in out_dmas:
            od.needs_inc = True

        P.finalize()

        engines = {"pe": nc.tensor, "act": nc.scalar, "dve": nc.vector,
                   "pool": nc.gpsimd, "sync": nc.sync, "pdma": nc.gpsimd,
                   "gpsimd": nc.gpsimd}
        semmap = dict(sems)

        with nc.Block() as block:
            @block.sync
            def _(sync):
                P.run_engine("sync", engines, semmap)
                for g in ("ya", "yb"):
                    if P.totals.get(g):
                        sync.wait_ge(semmap[g], P.totals[g])

            @block.gpsimd
            def _(gpsimd):
                P.run_engine("gpsimd", engines, semmap)

            @block.vector
            def _(vector):
                P.run_engine("dve", engines, semmap)

            @block.scalar
            def _(scalar):
                P.run_engine("act", engines, semmap)

            @block.tensor
            def _(tensor):
                P.run_engine("pe", engines, semmap)

    return nc


_NC_CACHE = None


def kernel(L0_attn_hidden_states, L0_attn_Wq, L0_attn_Wk, L0_attn_Wv, L0_attn_Wo,
           L0_ffn_w_freq_real, L0_ffn_w_freq_imag, L0_ffn_bias):
    global _NC_CACHE
    x = np.ascontiguousarray(np.asarray(L0_attn_hidden_states, dtype=np.float32))
    wq = np.asarray(L0_attn_Wq, dtype=np.float32)
    wk = np.asarray(L0_attn_Wk, dtype=np.float32)
    wv = np.asarray(L0_attn_Wv, dtype=np.float32)
    wo = np.asarray(L0_attn_Wo, dtype=np.float32)
    wr = np.ascontiguousarray(np.asarray(L0_ffn_w_freq_real, dtype=np.float32))
    wi = np.ascontiguousarray(np.asarray(L0_ffn_w_freq_imag, dtype=np.float32))
    fb = np.ascontiguousarray(np.asarray(L0_ffn_bias, dtype=np.float32).reshape(1, D))

    wqt = np.ascontiguousarray(wq.T)
    wkt = np.ascontiguousarray(wk.T)
    wvt = np.ascontiguousarray(wv.T)
    wot = np.ascontiguousarray(wo.T)

    jj = np.arange(D, dtype=np.int64)
    ang = 2.0 * np.pi * (np.outer(jj, jj) % D).astype(np.float64) / D
    ctab = np.cos(ang).astype(ml_dtypes.bfloat16)
    stab = np.sin(ang).astype(ml_dtypes.bfloat16)
    snegtab = (-np.sin(ang)).astype(ml_dtypes.bfloat16)
    ident = np.eye(128, dtype=ml_dtypes.bfloat16)
    ones64 = np.ones((128, 64), dtype=ml_dtypes.bfloat16)
    onesrow = np.ones((1, 128), dtype=ml_dtypes.bfloat16)

    if _NC_CACHE is None:
        _NC_CACHE = build_nc()
    nc = _NC_CACHE

    shared = dict(wqt=wqt, wkt=wkt, wvt=wvt, wot=wot, wfr=wr, wfi=wi,
                  fbias=fb, ctab=ctab, snegtab=snegtab,
                  ident=ident, ones64=ones64, onesrow=onesrow)
    in_maps = []
    for c in range(NCORES):
        m = dict(shared)
        m["x"] = np.ascontiguousarray(x[c * BPC:(c + 1) * BPC])
        m["cshard"] = np.ascontiguousarray(ctab[:, 128 * c:128 * c + 128])
        m["sshard"] = np.ascontiguousarray(stab[:, 128 * c:128 * c + 128])
        m["sshardneg"] = np.ascontiguousarray(snegtab[:, 128 * c:128 * c + 128])
        in_maps.append(m)

    res = run_bass_kernel_spmd(nc, in_maps, core_ids=list(range(NCORES)))
    out = np.concatenate([np.asarray(r["y"]) for r in res.results], axis=0)
    return out


# version 32
# speedup vs baseline: 1.0074x; 1.0074x over previous
import numpy as np
import ml_dtypes
from contextlib import ExitStack

import concourse.bass as bass
import concourse.mybir as mybir
from concourse.bass_utils import run_bass_kernel_spmd

F32 = mybir.dt.float32
BF16 = mybir.dt.bfloat16
AF = mybir.ActivationFunctionType
ALU = mybir.AluOpType

D = 1024
SEQ = 2048
BPC = 2
H = 16
HD = 64
NCORES = 8
TC = SEQ // 128
NJ = D // 128
EPS = 1e-5
QSCALE = HD ** -0.5


class Op:
    __slots__ = ("idx", "engine", "emit", "deps", "dma", "sem_val", "needs_inc", "grp")

    def __init__(self, idx, engine, emit, deps, dma, grp):
        self.idx = idx
        self.engine = engine
        self.emit = emit
        self.deps = deps
        self.dma = dma
        self.sem_val = None
        self.needs_inc = False
        self.grp = grp


class Prog:

    def __init__(self):
        self.ops = []

    def add(self, engine, emit, deps=(), dma=False, q=None):
        deps = [d for d in deps if d is not None]
        assert not dma or q is not None, "DMA ops must name a semaphore group"
        op = Op(len(self.ops), engine, emit, deps, dma, q if dma else engine)
        for d in deps:
            assert d.idx < op.idx
        self.ops.append(op)
        return op

    def finalize(self):
        for op in self.ops:
            if op.dma:
                op.needs_inc = True
            kept = []
            for d in op.deps:
                if d.engine == op.engine and op.engine == "pe" and not d.dma and not op.dma:
                    continue
                if d.dma and op.dma and d.engine == op.engine:
                    continue
                kept.append(d)
                d.needs_inc = True
            op.deps = kept
        counters = {}
        grp_last = {}
        for op in self.ops:
            if op.needs_inc:
                counters[op.grp] = counters.get(op.grp, 0) + (16 if op.dma else 1)
                op.sem_val = counters[op.grp]
                grp_last[op.grp] = op
        for op in self.ops:
            for d in op.deps:
                if d.grp in ("cst", "pcst"):
                    d.sem_val = grp_last[d.grp].sem_val
        per_engine = {}
        stream_of = {"pdma": "gpsimd", "pool": "gpsimd"}
        for op in self.ops:
            per_engine.setdefault(stream_of.get(op.engine, op.engine), []).append(op)
        self.per_engine = per_engine
        self.totals = counters

    def run_engine(self, name, engines, sems):
        eng = engines[name]
        waited = {}
        for op in self.per_engine.get(name, []):
            for d in op.deps:
                v = d.sem_val
                assert v is not None
                if waited.get(d.grp, -1) >= v:
                    continue
                eng.wait_ge(sems[d.grp], v)
                waited[d.grp] = v
            last = op.emit(eng)
            if op.needs_inc:
                last.then_inc(sems[op.grp], 16 if op.dma else 1)


def mm_group(e, items):
    first = None
    last = None
    for (out_ap, lhsT_ap, rhs_ap, st, sp, tp) in items:
        if first is None:
            first = e.matmul(out_ap, lhsT_ap, rhs_ap, start=st, stop=sp,
                             tile_position=tp)
            last = first
        else:
            inst = mybir.InstMatmult(
                name=e.bass.get_next_instruction_name(),
                ins=[e.lower_ap_or_imm(rhs_ap), e.lower_ap_or_imm(lhsT_ap)],
                outs=[e.lower_ap_or_imm(out_ap)],
                replication_resolution=0, replication_shift_amnt=0,
                replication_num_rows=0,
                start_tensor_calc=st, stop_tensor_calc=sp,
                tile_position=first.ins.tile_position,
                tile_size=first.ins.tile_size,
                ldweights=False)
            last = e.add_instruction(inst)
    return last


def build_nc():
    nc = bass.Bass("TRN2", target_bir_lowering=False)

    x_hbm = nc.dram_tensor("x", [BPC, SEQ, D], F32, kind="ExternalInput").ap()
    wqt_h = nc.dram_tensor("wqt", [D, D], F32, kind="ExternalInput").ap()
    wkt_h = nc.dram_tensor("wkt", [D, D], F32, kind="ExternalInput").ap()
    wvt_h = nc.dram_tensor("wvt", [D, D], F32, kind="ExternalInput").ap()
    wot_h = nc.dram_tensor("wot", [D, D], F32, kind="ExternalInput").ap()
    wr_h = nc.dram_tensor("wfr", [D, D], F32, kind="ExternalInput").ap()
    wi_h = nc.dram_tensor("wfi", [D, D], F32, kind="ExternalInput").ap()
    bias_h = nc.dram_tensor("fbias", [1, D], F32, kind="ExternalInput").ap()
    c_h = nc.dram_tensor("ctab", [D, D], BF16, kind="ExternalInput").ap()
    sneg_h = nc.dram_tensor("snegtab", [D, D], BF16, kind="ExternalInput").ap()
    csh_h = nc.dram_tensor("cshard", [D, 128], BF16, kind="ExternalInput").ap()
    ssh_h = nc.dram_tensor("sshard", [D, 128], BF16, kind="ExternalInput").ap()
    sshn_h = nc.dram_tensor("sshardneg", [D, 128], BF16, kind="ExternalInput").ap()
    ccin = nc.dram_tensor("ccin", [128, D], BF16, kind="Internal").ap()
    ccout = nc.dram_tensor("ccout", [NCORES * 128, D], BF16, kind="Internal").ap()
    ident_h = nc.dram_tensor("ident", [128, 128], BF16, kind="ExternalInput").ap()
    ones64_h = nc.dram_tensor("ones64", [128, 64], BF16, kind="ExternalInput").ap()
    onesrow_h = nc.dram_tensor("onesrow", [1, 128], BF16, kind="ExternalInput").ap()

    y_hbm = nc.dram_tensor("y", [BPC, SEQ, D], F32, kind="ExternalOutput").ap()
    x1_scr = nc.dram_tensor("x1scr", [SEQ, D], F32, kind="Internal").ap()

    def chunked(hbm_ap):
        return hbm_ap.rearrange("(c p) d -> p c d", p=128)

    P = Prog()
    out_dmas = []

    with ExitStack() as top:
        sb = lambda name, shape, dt: top.enter_context(nc.sbuf_tensor(name, shape, dt))
        sems = {}
        for name in ("pe", "act", "dve", "pool", "cst", "pcst", "cc", "mt",
                     "xa", "xb", "xc", "xd", "xe", "xf", "wa", "wb",
                     "sa", "sb", "ya", "yb"):
            sems[name] = top.enter_context(nc.semaphore("sem_" + name))

        identt = sb("identt", [128, 128], BF16)
        ones64t = sb("ones64t", [128, 64], BF16)
        onesrowt = sb("onesrowt", [1, 128], BF16)
        epsv = sb("epsv", [128, 1], F32)
        biasbf = sb("biasbf", [1, D], BF16)
        mt_sb = sb("mt_sb", [128, NJ * D], BF16)
        stats = sb("stats", [128, 12], F32)
        stats1 = sb("stats1", [128, 12], F32)
        xr1 = [sb(f"xr1_{i}", [128, D], F32) for i in range(2)]
        t1r1 = [sb(f"t1r1_{i}", [128, D], BF16) for i in range(2)]
        mv1_r = [sb(f"mv1_{i}", [128, 2], F32) for i in range(2)]
        lnv1_r = [sb(f"lnv1_{i}", [128, 1], F32) for i in range(2)]
        rstd1_r = [sb(f"rstd1_{i}", [128, 1], F32) for i in range(2)]
        negmur1_r = [sb(f"negmur1_{i}", [128, 1], F32) for i in range(2)]
        ln1_state = {"xr_reader": [None, None], "t1r_reader": [None, None],
                     "mv_readers": [[], []], "rstd_war": [None, None]}
        mv_r = [sb(f"mv{i}", [128, 2], F32) for i in range(4)]
        lnv_r = [sb(f"lnv{i}", [128, 1], F32) for i in range(4)]
        rstd_r = [sb(f"rstd{i}", [128, 1], F32) for i in range(4)]
        negmur_r = [sb(f"negmur{i}", [128, 1], F32) for i in range(4)]
        sums_r = [sb(f"sums{i}", [128, 1], F32) for i in range(4)]
        sumsq_r = [sb(f"sumsq{i}", [128, 1], F32) for i in range(4)]
        negm2_r = [sb(f"negm2{i}", [128, 1], F32) for i in range(4)]
        dumpt = sb("dumpt", [128, D], BF16)

        ld_ident = P.add("sync", lambda e: e.dma_start(out=identt[:, :], in_=ident_h), dma=True, q="cst")
        ld_ones64 = P.add("sync", lambda e: e.dma_start(out=ones64t[:, :], in_=ones64_h), dma=True, q="cst")
        ld_onesrow = P.add("sync", lambda e: e.dma_start(out=onesrowt[:, :], in_=onesrow_h), dma=True, q="cst")
        op_eps = P.add("dve", lambda e: e.memset(epsv[:, :], EPS))
        op_biasld = P.add("pdma", lambda e: e.dma_start(out=biasbf[:, :], in_=bias_h), dma=True, q="pcst")

        with ExitStack() as mctx:
            msb = lambda name, shape, dt: mctx.enter_context(nc.sbuf_tensor(name, shape, dt))
            csb = msb("csb", [128, NJ * D], BF16)
            snegsb = msb("snegsb", [128, NJ * D], BF16)
            wrsb = msb("wrsb", [128, NJ * D], BF16)
            wisb = msb("wisb", [128, NJ * D], BF16)
            cshsb = msb("cshsb", [128, NJ * 128], BF16)
            sshsb = msb("sshsb", [128, NJ * 128], BF16)
            uvsb = msb("uvsb", [128, 2048], BF16)
            mtcsb = msb("mtcsb", [128, D], BF16)
            pm = mctx.enter_context(nc.psum_tensor("pm", [128, 2048], F32))

            ld_c = P.add("sync", lambda e: e.dma_start(
                out=csb.ap().rearrange("p (c d) -> p c d", c=NJ), in_=chunked(c_h)), dma=True, q="cst")
            ld_sneg = P.add("sync", lambda e: e.dma_start(
                out=snegsb.ap().rearrange("p (c d) -> p c d", c=NJ), in_=chunked(sneg_h)), dma=True, q="cst")
            ld_csh = P.add("sync", lambda e: e.dma_start(
                out=cshsb.ap().rearrange("p (c d) -> p c d", c=NJ),
                in_=csh_h.rearrange("(c p) d -> p c d", p=128)), dma=True, q="cst")
            ld_ssh = P.add("sync", lambda e: e.dma_start(
                out=sshsb.ap().rearrange("p (c d) -> p c d", c=NJ),
                in_=ssh_h.rearrange("(c p) d -> p c d", p=128)), dma=True, q="cst")
            ld_wr = P.add("pdma", lambda e: e.dma_start(
                out=wrsb.ap().rearrange("p (c d) -> p c d", c=NJ), in_=chunked(wr_h)), dma=True, q="pcst")
            ld_wi = P.add("pdma", lambda e: e.dma_start(
                out=wisb.ap().rearrange("p (c d) -> p c d", c=NJ), in_=chunked(wi_h)), dma=True, q="pcst")

            sshnsb = msb("sshnsb", [128, NJ * 128], BF16)
            ld_sshn = P.add("pdma", lambda e: e.dma_start(
                out=sshnsb.ap().rearrange("p (c d) -> p c d", c=NJ),
                in_=sshn_h.rearrange("(c p) d -> p c d", p=128)), dma=True, q="pcst")

            mm_last = None
            first_dep = [ld_c, ld_sneg, ld_csh, ld_ssh, ld_sshn, ld_wr, ld_wi]
            emitted_first = False
            for half, pairs in ((0, ((wrsb, cshsb), (wisb, sshsb))),
                                (1024, ((wisb, cshsb), (wrsb, sshnsb)))):
                for ob in range(NJ):
                    def emit_s1(half=half, ob=ob, pairs=pairs):
                        def go(e):
                            last = None
                            n = 0
                            for j in range(NJ):
                                for (lh, rh) in pairs:
                                    n += 1
                                    last = e.matmul(
                                        pm[:, half + 128 * ob: half + 128 * ob + 128],
                                        lh[:, D * j + 128 * ob: D * j + 128 * ob + 128],
                                        rh[:, 128 * j:128 * j + 128],
                                        start=(n == 1), stop=(n == 2 * NJ))
                            return last
                        return go
                    mm_last = P.add("pe", emit_s1(),
                                    deps=first_dep if not emitted_first else ())
                    emitted_first = True
            ev_uv = P.add("dve",
                          (lambda: lambda e: e.tensor_copy(uvsb[:, :], pm[:, :]))(),
                          deps=[mm_last])
            mm_last2 = None
            for ob2 in range(2):
                def emit_s2(ob2=ob2):
                    def go(e):
                        last = None
                        n = 0
                        for oc in range(NJ):
                            for (loff, rh) in ((0, csb), (1024, snegsb)):
                                n += 1
                                last = e.matmul(
                                    pm[:, 1024 + 512 * ob2:1024 + 512 * ob2 + 512],
                                    uvsb[:, loff + 128 * oc: loff + 128 * oc + 128],
                                    rh[:, D * oc + 512 * ob2: D * oc + 512 * ob2 + 512],
                                    start=(n == 1), stop=(n == 2 * NJ))
                        return last
                    return go
                mm_last2 = P.add("pe", emit_s2(), deps=[ev_uv] if ob2 == 0 else ())
            ev_mtc = P.add("dve",
                           (lambda: lambda e: e.tensor_scalar(
                               mtcsb[:, :], pm[:, 1024:2048], 1.0 / D, None, ALU.mult))(),
                           deps=[mm_last2])
            dma_ccin = P.add("pdma",
                             (lambda: lambda e: e.dma_start(out=ccin, in_=mtcsb[:, :]))(),
                             deps=[ev_mtc], dma=True, q="cc")
            cc = P.add("pool",
                       (lambda: lambda e: e.collective_compute(
                           "AllGather", ALU.bypass,
                           replica_groups=[list(range(NCORES))],
                           ins=[ccin.opt()], outs=[ccout.opt()]))(),
                       deps=[dma_ccin])
            cc_op = cc
        mt_last = None

        bufA = sb("bufA", [128, NJ * SEQ], BF16)
        bufQ = sb("bufQ", [128, NJ * SEQ], BF16)
        bufK = sb("bufK", [128, NJ * SEQ], BF16)
        bufV = sb("bufV", [128, TC * D], BF16)
        qz_readA = [None, None]
        qz_readB = [None, None]
        stats_reader = [None]
        prev_wq_readers = []

        prev_batch_last = None
        for b in range(BPC):
            with ExitStack() as bctx:
                t1T = bufA
                qT = bufQ
                kT = bufK
                v_sb = bufV

                def make_ln1(b, t1T, tpl, tp_extra, evac_extra):
                    st = ln1_state
                    ctx_ops = {}
                    tr1_of = {}
                    evacs = []

                    def ln1_iter(k):
                        if k < TC:
                            tc = k
                            sl = tc % 2
                            ld = P.add("sync",
                                       (lambda b=b, tc=tc, sl=sl:
                                        lambda e: e.dma_start(out=xr1[sl][:, :],
                                                              in_=x_hbm[b, tc * 128:(tc + 1) * 128, :]))(),
                                       deps=[st["xr_reader"][sl]], dma=True,
                                       q="xa" if sl == 0 else "xb")
                            st1 = P.add("dve",
                                        (lambda sl=sl: lambda e: (
                                            e.bn_stats(stats1[:, 0:6], xr1[sl][:, 0:512]),
                                            e.bn_stats(stats1[:, 6:12], xr1[sl][:, 512:1024]))[-1])(),
                                        deps=[ld, st.get("stats1_reader")])
                            ag = P.add("dve",
                                       (lambda sl=sl: lambda e: e.bn_aggr(mv1_r[sl][:, :], stats1[:, :]))(),
                                       deps=[st1] + st["mv_readers"][sl])
                            st["stats1_reader"] = ag
                            ln = P.add("act",
                                       (lambda sl=sl: lambda e: e.activation(
                                           lnv1_r[sl][:, :], mv1_r[sl][:, 1:2], AF.Ln, bias=epsv[:, 0:1]))(),
                                       deps=[ag, op_eps])
                            ex = P.add("act",
                                       (lambda sl=sl: lambda e: e.activation(
                                           rstd1_r[sl][:, :], lnv1_r[sl][:, :], AF.Exp, scale=-0.5))(),
                                       deps=[ln, st["rstd_war"][sl]])
                            ctx_ops[tc] = (ld, ag, ln, ex)
                        if k >= 1 and k - 1 < TC:
                            tc = k - 1
                            sl = tc % 2
                            (ld, ag, ln, ex) = ctx_ops[tc]
                            nm = P.add("dve",
                                       (lambda sl=sl: lambda e: e.scalar_tensor_tensor(
                                           negmur1_r[sl][:, :], mv1_r[sl][:, 0:1], -1.0,
                                           rstd1_r[sl][:, :], ALU.mult, ALU.mult))(),
                                       deps=[ex, ag])
                            ap_ = P.add("dve",
                                        (lambda sl=sl: lambda e: e.tensor_scalar(
                                            t1r1[sl][:, :], xr1[sl][:, :], rstd1_r[sl][:, 0:1],
                                            negmur1_r[sl][:, 0:1], ALU.mult, ALU.add))(),
                                        deps=[nm, st["t1r_reader"][sl]])
                            st["mv_readers"][sl] = [nm, ln]
                            st["rstd_war"][sl] = ap_
                            st["xr_reader"][sl] = ap_
                            tr = None
                            for dc in range(NJ):
                                tr = P.add("pe",
                                           (lambda sl=sl, dc=dc: lambda e: e.transpose(
                                               tpl[sl][:, 128 * dc:128 * dc + 128],
                                               t1r1[sl][:, 128 * dc:128 * dc + 128],
                                               identt[:, :]))(),
                                           deps=[ap_, st.get("tp_reader", [None, None])[sl],
                                                 ld_ident] + tp_extra if dc == 0 else [])
                            st["t1r_reader"][sl] = tr
                            tr1_of[tc] = tr
                        if k >= 2 and k - 2 < TC:
                            tc = k - 2
                            sl = tc % 2
                            ev = P.add("dve",
                                       (lambda sl=sl, tc=tc: lambda e: e.tensor_copy(
                                           t1T.ap().rearrange("p (c t) -> p c t", c=NJ)[:, :, 128 * tc:128 * tc + 128],
                                           tpl[sl].ap().rearrange("p (c t) -> p c t", c=NJ)))(),
                                       deps=[tr1_of[tc]] + evac_extra)
                            st.setdefault("tp_reader", [None, None])[sl] = ev
                            evacs.append(ev)
                    return ln1_iter, evacs

                with ExitStack() as lctx:
                    tpl0 = [lctx.enter_context(nc.psum_tensor(f"tpl0_{i}_b{b}", [128, 1024], BF16))
                            for i in range(2)]
                    ln1_state["tp_reader"] = [None, None]
                    ln1_it, ln1_evacs = make_ln1(b, t1T, tpl0, [], [])
                    for k in range(TC + 2):
                        ln1_it(k)
                    t1T_ready = ln1_evacs[-1]
                if mt_last is None:
                    mt_last = P.add("sync",
                                    (lambda: lambda e: e.dma_start(
                                        out=mt_sb.ap().rearrange("p (c d) -> p c d", c=NJ),
                                        in_=ccout.rearrange("(c p) d -> p c d", p=128)))(),
                                    deps=[cc_op], dma=True, q="mt")

                with ExitStack() as wctx:
                    wq_sb = wctx.enter_context(nc.sbuf_tensor(f"wq_b{b}", [128, NJ * D], BF16))
                    wk_sb = wctx.enter_context(nc.sbuf_tensor(f"wk_b{b}", [128, NJ * D], BF16))
                    pp = wctx.enter_context(nc.psum_tensor(f"pp_b{b}", [128, 2048], F32))
                    ld_wq = P.add("pdma", (lambda e: e.dma_start(
                        out=wq_sb.ap().rearrange("p (c d) -> p c d", c=NJ), in_=chunked(wqt_h))),
                        deps=[t1T_ready, prev_batch_last] + prev_wq_readers,
                        dma=True, q="wa")
                    ld_wk = P.add("pdma", (lambda e: e.dma_start(
                        out=wk_sb.ap().rearrange("p (c d) -> p c d", c=NJ), in_=chunked(wkt_h))),
                        deps=[t1T_ready, prev_batch_last] + prev_wq_readers,
                        dma=True, q="wb")

                    def project_fmajor(dst_sb, w_sb, deps0):
                        evs = []
                        slot_ev = [None, None]
                        un = 0
                        for ec in range(NJ):
                            for tbh in range(2):
                                slot = un % 2
                                first_deps = [t1T_ready] + deps0 + (
                                    [slot_ev[slot]] if slot_ev[slot] else [])
                                mm_last = None
                                for j in range(NJ):
                                    lhs = w_sb[:, D * j + 128 * ec: D * j + 128 * ec + 128]
                                    items = [
                                        (pp[:, 1024 * slot + 512 * tb:1024 * slot + 512 * tb + 512],
                                         lhs,
                                         t1T[:, SEQ * j + 1024 * tbh + 512 * tb:
                                             SEQ * j + 1024 * tbh + 512 * tb + 512],
                                         j == 0, j == NJ - 1, None)
                                        for tb in range(2)]
                                    mm_last = P.add(
                                        "pe",
                                        (lambda items=items: lambda e: mm_group(e, items))(),
                                        deps=first_deps if j == 0 else ())
                                ev = P.add("dve",
                                           (lambda ec=ec, tbh=tbh, slot=slot, dst_sb=dst_sb:
                                            lambda e: e.tensor_copy(
                                                dst_sb[:, SEQ * ec + 1024 * tbh:
                                                       SEQ * ec + 1024 * tbh + 1024],
                                                pp[:, 1024 * slot:1024 * slot + 1024]))(),
                                           deps=[mm_last])
                                slot_ev[slot] = ev
                                evs.append(ev)
                                un += 1
                        return evs

                    q_done = project_fmajor(qT, wq_sb, [ld_wq])
                    k_done = project_fmajor(kT, wk_sb, [ld_wk, q_done[-1]])
                    ld_wv = P.add("pdma", (lambda e: e.dma_start(
                        out=wq_sb.ap().rearrange("p (c d) -> p c d", c=NJ), in_=chunked(wvt_h))),
                        deps=[q_done[-1]], dma=True, q="wa")
                    v_evs = []
                    for tc in range(TC):
                        mm_last = None
                        for j in range(NJ):
                            lhs = t1T[:, SEQ * j + 128 * tc: SEQ * j + 128 * tc + 128]
                            items = [
                                (pp[:, 512 * (2 * (tc % 2) + cb):512 * (2 * (tc % 2) + cb) + 512],
                                 lhs,
                                 wq_sb[:, D * j + 512 * cb: D * j + 512 * cb + 512],
                                 j == 0, j == NJ - 1, None)
                                for cb in range(2)]
                            dep0 = ()
                            if j == 0:
                                dep0 = [ld_wv, k_done[-1]]
                                if len(v_evs) >= 2:
                                    dep0 = dep0 + [v_evs[-2]]
                            mm_last = P.add(
                                "pe",
                                (lambda items=items: lambda e: mm_group(e, items))(),
                                deps=dep0)
                        ev = P.add("dve",
                                   (lambda tc=tc: lambda e: e.tensor_copy(
                                       v_sb[:, D * tc:D * tc + D],
                                       pp[:, 1024 * (tc % 2):1024 * (tc % 2) + 1024]))(),
                                   deps=[mm_last])
                        v_evs.append(ev)
                    v_done = v_evs[-1]
                    prev_wq_readers = list(v_evs) + [q_done[-1], k_done[-1]]

                ctxT = bufA
                with ExitStack() as actx:
                    asb = lambda name, shape, dt: actx.enter_context(
                        nc.sbuf_tensor(f"{name}_b{b}", shape, dt))
                    psa = actx.enter_context(nc.psum_tensor(f"psa_b{b}", [128, 1024], F32))
                    psb = actx.enter_context(nc.psum_tensor(f"psb_b{b}", [128, 1024], F32))
                    ct2 = [actx.enter_context(nc.psum_tensor(f"ct{i}_b{b}", [128, 1024], F32))
                           for i in range(2)]
                    expS = [asb(f"expS{i}", [128, 2 * 1024], BF16) for i in range(2)]
                    zacc = asb("zacc", [128, 2 * 1024], BF16)
                    invz = asb("invz", [128, 1024], F32)
                    qzA = asb("qzA", [128, 2048], BF16)
                    qzB = asb("qzB", [128, 2048], BF16)
                    mzA = P.add("dve",
                                lambda e: e.memset(qzA[64:128, :], 0.0),
                                deps=[qz_readA[0], qz_readA[1], prev_batch_last])
                    mzB = P.add("dve",
                                lambda e: e.memset(qzB[0:64, :], 0.0),
                                deps=[qz_readB[0], qz_readB[1], prev_batch_last])

                    last_expA = None
                    last_expB = None
                    slot_consumers = [[], []]
                    ct_last_reader = [None, None]
                    last_zacc_readerA = None
                    last_zacc_readerB = None
                    last_zaA = None
                    last_zaB = None
                    last_tt = None

                    steps = [(p, qh, kb) for p in range(H // 2)
                             for qh in range(2) for kb in range(TC)]

                    qz_cpA = {}
                    qz_cpB = {}

                    def emit_qz(u):
                        p, qh = divmod(u, 2)
                        su = u % 2
                        qbase = SEQ * p + 1024 * qh
                        cA = P.add("dve",
                                   (lambda su=su, qbase=qbase: lambda e: e.tensor_copy(
                                       qzA[0:64, 1024 * su:1024 * su + 1024],
                                       qT[0:64, qbase:qbase + 1024]))(),
                                   deps=[q_done[-1], mzA, qz_readA[su]])
                        cB = P.add("dve",
                                   (lambda su=su, qbase=qbase: lambda e: e.tensor_copy(
                                       qzB[64:128, 1024 * su:1024 * su + 1024],
                                       qT[64:128, qbase:qbase + 1024]))(),
                                   deps=[q_done[-1], mzB, qz_readB[su]])
                        qz_cpA[u] = cA
                        qz_cpB[u] = cB

                    def emit_scores_A(step, extra):
                        p, qh, kb = step
                        u = p * 2 + qh
                        su = u % 2
                        lhsA = kT[:, SEQ * p + 128 * kb: SEQ * p + 128 * kb + 128]
                        items = [
                            (psa[:, 512 * i:512 * i + 512], lhsA,
                             qzA[:, 1024 * su + 512 * i: 1024 * su + 512 * i + 512],
                             True, True, None) for i in range(2)]
                        g = P.add("pe",
                                  (lambda items=items: lambda e: mm_group(e, items))(),
                                  deps=[last_expA, qz_cpA[u]] + extra)
                        qz_readA[su] = g
                        return g

                    def emit_scores_B(step, extra):
                        p, qh, kb = step
                        u = p * 2 + qh
                        su = u % 2
                        lhsB = kT[:, SEQ * p + 128 * kb: SEQ * p + 128 * kb + 128]
                        items = [
                            (psb[:, 512 * i:512 * i + 512], lhsB,
                             qzB[:, 1024 * su + 512 * i: 1024 * su + 512 * i + 512],
                             True, True, None) for i in range(2)]
                        g = P.add("pe",
                                  (lambda items=items: lambda e: mm_group(e, items))(),
                                  deps=[last_expB, qz_cpB[u]] + extra)
                        qz_readB[su] = g
                        return g

                    ext0 = [q_done[-1], k_done[-1], v_done]
                    pending_Z = None
                    emit_qz(0)
                    pend_A = emit_scores_A(steps[0], ext0)
                    pend_B = emit_scores_B(steps[0], ext0)
                    for i, (p, qh, kb) in enumerate(steps):
                        u = p * 2 + qh
                        qbase = SEQ * p + 1024 * qh
                        cts = ct2[u % 2]
                        sl = i % 2
                        if kb == 0 and u + 1 < 2 * (H // 2):
                            emit_qz(u + 1)
                        sA, sB = pend_A, pend_B
                        exA = P.add("act",
                                    (lambda sl=sl: lambda e: e.activation(
                                        expS[sl][:, 0:1024], psa[:, :],
                                        AF.Exp, scale=QSCALE))(),
                                    deps=[sA] + slot_consumers[sl])
                        exB = P.add("act",
                                    (lambda sl=sl: lambda e: e.activation(
                                        expS[sl][:, 1024:2048], psb[:, :],
                                        AF.Exp, scale=QSCALE))(),
                                    deps=[sB])
                        iz_gate = None
                        if pending_Z is not None:
                            bcB2, qb2, cts2, cB2, zct = pending_Z
                            pending_Z = None
                            lnz = P.add("act",
                                        (lambda zct=zct: lambda e: e.activation(
                                            zct[:, :], zct[:, :], AF.Ln))(),
                                        deps=[bcB2])
                            izz = P.add("act",
                                        (lambda zct=zct: lambda e: e.activation(
                                            invz[:, :], zct[:, :], AF.Exp,
                                            scale=-1.0))(),
                                        deps=[lnz, last_tt])
                            tt = P.add("dve",
                                       (lambda qb2=qb2, cts2=cts2: lambda e: e.tensor_tensor(
                                           ctxT[:, qb2:qb2 + 1024],
                                           cts2[:, :], invz[:, :], ALU.mult))(),
                                       deps=[izz, cB2])
                            ct_last_reader[(u - 1) % 2] = tt
                            last_tt = tt
                            iz_gate = izz
                        slot_consumers[sl] = []
                        last_expA, last_expB = exA, exB
                        last_step = (i + 1 == len(steps))
                        cdepA = [exA]
                        if kb == 0 and ct_last_reader[u % 2] is not None:
                            cdepA.append(ct_last_reader[u % 2])
                        if kb == 0 and iz_gate is not None:
                            cdepA.append(iz_gate)
                        if not last_step:
                            pend_A = emit_scores_A(steps[i + 1], [])
                        lhscA = v_sb[:, D * kb + HD * (2 * p): D * kb + HD * (2 * p) + HD]
                        itemsCA = [
                            (cts[0:64, 512 * i2:512 * i2 + 512], lhscA,
                             expS[sl][:, 512 * i2:512 * i2 + 512],
                             kb == 0, kb == TC - 1, (0, 0)) for i2 in range(2)]
                        cA = P.add("pe",
                                   (lambda items=itemsCA: lambda e: mm_group(e, items))(),
                                   deps=cdepA)
                        if not last_step:
                            pend_B = emit_scores_B(steps[i + 1], [])
                        lhscB = v_sb[:, D * kb + HD * (2 * p + 1): D * kb + HD * (2 * p + 1) + HD]
                        itemsCB = [
                            (cts[64:128, 512 * i2:512 * i2 + 512], lhscB,
                             expS[sl][:, 1024 + 512 * i2: 1024 + 512 * i2 + 512],
                             kb == 0, kb == TC - 1, (0, 64)) for i2 in range(2)]
                        cB = P.add("pe",
                                   (lambda items=itemsCB: lambda e: mm_group(e, items))(),
                                   deps=[exB])
                        if kb == 0:
                            zaA = P.add("dve",
                                        (lambda sl=sl: lambda e: e.tensor_copy(
                                            zacc[:, 0:1024], expS[sl][:, 0:1024]))(),
                                        deps=[exA, last_zacc_readerA])
                            zaB = P.add("dve",
                                        (lambda sl=sl: lambda e: e.tensor_copy(
                                            zacc[:, 1024:2048], expS[sl][:, 1024:2048]))(),
                                        deps=[exB, last_zacc_readerB])
                        else:
                            zaA = P.add("dve",
                                        (lambda sl=sl: lambda e: e.tensor_tensor(
                                            zacc[:, 0:1024], zacc[:, 0:1024],
                                            expS[sl][:, 0:1024], ALU.add))(),
                                        deps=[exA, last_zaA])
                            zaB = P.add("dve",
                                        (lambda sl=sl: lambda e: e.tensor_tensor(
                                            zacc[:, 1024:2048], zacc[:, 1024:2048],
                                            expS[sl][:, 1024:2048], ALU.add))(),
                                        deps=[exB, last_zaB])
                        last_zaA, last_zaB = zaA, zaB
                        slot_consumers[sl] += [cA, cB, zaA, zaB]
                        if kb == TC - 1:
                            zct = ct2[(u + 1) % 2]
                            itemsZA = [
                                (zct[0:64, 512 * j:512 * j + 512], ones64t[:, :],
                                 zacc[:, 512 * j:512 * j + 512], True, True, (0, 0))
                                for j in range(2)]
                            bcA = P.add("pe",
                                        (lambda items=itemsZA: lambda e: mm_group(e, items))(),
                                        deps=[last_zaA, ld_ones64, last_tt,
                                              ct_last_reader[(u + 1) % 2]])
                            itemsZB = [
                                (zct[64:128, 512 * j:512 * j + 512], ones64t[:, :],
                                 zacc[:, 1024 + 512 * j: 1024 + 512 * j + 512],
                                 True, True, (0, 64)) for j in range(2)]
                            bcB = P.add("pe",
                                        (lambda items=itemsZB: lambda e: mm_group(e, items))(),
                                        deps=[last_zaB])
                            last_zacc_readerA = bcA
                            last_zacc_readerB = bcB
                            pending_Z = (bcB, qbase, cts, cB, zct)
                            if last_step:
                                bcB2, qb2, cts2, cB2, zct2 = pending_Z
                                pending_Z = None
                                lnz = P.add("act",
                                            (lambda zct2=zct2: lambda e: e.activation(
                                                zct2[:, :], zct2[:, :], AF.Ln))(),
                                            deps=[bcB2])
                                izf = P.add("act",
                                            (lambda zct2=zct2: lambda e: e.activation(
                                                invz[:, :], zct2[:, :], AF.Exp,
                                                scale=-1.0))(),
                                            deps=[lnz, last_tt])
                                tt = P.add("dve",
                                           (lambda qb2=qb2, cts2=cts2: lambda e: e.tensor_tensor(
                                               ctxT[:, qb2:qb2 + 1024],
                                               cts2[:, :], invz[:, :], ALU.mult))(),
                                           deps=[izf, cB2])
                                ct_last_reader[u % 2] = tt
                                last_tt = tt
                    attn_done = last_tt

                t2T = bufQ
                with ExitStack() as octx:
                    osb = lambda name, shape, dt: octx.enter_context(
                        nc.sbuf_tensor(f"{name}_b{b}", shape, dt))
                    ao2 = [octx.enter_context(nc.psum_tensor(f"ao{i}_b{b}", [128, 1024], F32))
                           for i in range(2)]
                    tp2 = [octx.enter_context(nc.psum_tensor(f"tp2_{i}_b{b}", [128, 1024], BF16))
                           for i in range(2)]
                    wo_sb = osb("wo", [128, NJ * D], BF16)
                    xr2 = [osb(f"xr2{i}", [128, D], F32) for i in range(2)]
                    tmpx = [osb(f"tmpx{i}", [128, D], F32) for i in range(2)]
                    x1r = [osb(f"x1r{i}", [128, D], F32) for i in range(2)]
                    t2r = [osb(f"t2r{i}", [128, D], BF16) for i in range(2)]
                    ld_wo = P.add("pdma", (lambda e: e.dma_start(
                        out=wo_sb.ap().rearrange("p (c d) -> p c d", c=NJ), in_=chunked(wot_h))),
                        deps=[attn_done], dma=True, q="wb")

                    xr2_reader = [None, None]
                    stx_reader = [None, None]
                    sums_reader = [None] * 4
                    sumsq_reader = [None] * 4
                    t2r_reader = [None, None]
                    tp2_reader = [None, None]
                    mv_readers = [[] for _ in range(4)]
                    rstd_war = [None] * 4
                    ao_reader = [None, None]
                    mm_of = {}
                    ld_of = {}
                    t2_of = {}
                    t2T_evacs = []
                    x1_stores = []

                    def oproj_mms(tc):
                        sl = tc % 2
                        ld = P.add("sync",
                                   (lambda b=b, tc=tc, sl=sl:
                                    lambda e: e.dma_start(out=xr2[sl][:, :],
                                                          in_=x_hbm[b, tc * 128:(tc + 1) * 128, :]))(),
                                   deps=[xr2_reader[sl], attn_done if tc < 2 else None],
                                   dma=True, q="xc" if sl == 0 else "xd")
                        ld_of[tc] = ld
                        mm_last = None
                        for cc in range(NJ):
                            lhs = ctxT[:, SEQ * cc + 128 * tc: SEQ * cc + 128 * tc + 128]
                            items = [
                                (ao2[sl][:, 512 * eb:512 * eb + 512], lhs,
                                 wo_sb[:, D * cc + 512 * eb: D * cc + 512 * eb + 512],
                                 cc == 0, cc == NJ - 1, None)
                                for eb in range(2)]
                            dep0 = [attn_done, ld_wo, ao_reader[sl]] if cc == 0 else ()
                            mm_last = P.add(
                                "pe",
                                (lambda items=items: lambda e: mm_group(e, items))(),
                                deps=dep0)
                        mm_of[tc] = mm_last

                    def oproj_chain(tc):
                        sl = tc % 2
                        sl4 = tc % 4
                        ao = ao2[sl]
                        mm_last = mm_of[tc]
                        ld = ld_of[tc]
                        sm_ = P.add("act",
                                    (lambda sl=sl, sl4=sl4, ao=ao: lambda e: e.activation(
                                        dumpt[:, :], ao[:, :], AF.Identity,
                                        accum_out=sums_r[sl4][:, 0:1]))(),
                                    deps=[mm_last, sums_reader[sl4]])
                        sq_ = P.add("act",
                                    (lambda sl=sl, sl4=sl4, ao=ao: lambda e: e.activation(
                                        dumpt[:, :], ao[:, :], AF.Square,
                                        accum_out=sumsq_r[sl4][:, 0:1]))(),
                                    deps=[sm_, sumsq_reader[sl4]])
                        mean_ = P.add("dve",
                                      (lambda sl=sl, sl4=sl4: lambda e: e.tensor_scalar(
                                          mv_r[sl4][:, 0:1], sums_r[sl4][:, 0:1],
                                          1.0 / D, None, ALU.mult))(),
                                      deps=[sm_] + mv_readers[sl4])
                        sums_reader[sl4] = mean_
                        nm2_ = P.add("dve",
                                     (lambda sl=sl, sl4=sl4: lambda e: e.scalar_tensor_tensor(
                                         negm2_r[sl4][:, 0:1], mv_r[sl4][:, 0:1], -1.0,
                                         mv_r[sl4][:, 0:1], ALU.mult, ALU.mult))(),
                                     deps=[mean_])
                        ag = P.add("dve",
                                   (lambda sl=sl, sl4=sl4: lambda e: e.scalar_tensor_tensor(
                                       mv_r[sl4][:, 1:2], sumsq_r[sl4][:, 0:1], 1.0 / D,
                                       negm2_r[sl4][:, 0:1], ALU.mult, ALU.add))(),
                                   deps=[sq_, nm2_])
                        sumsq_reader[sl4] = ag
                        ln = P.add("act",
                                   (lambda sl=sl, sl4=sl4: lambda e: e.activation(
                                       lnv_r[sl4][:, :], mv_r[sl4][:, 1:2], AF.Ln, bias=epsv[:, 0:1]))(),
                                   deps=[ag])
                        ex = P.add("act",
                                   (lambda sl=sl, sl4=sl4: lambda e: e.activation(
                                       rstd_r[sl4][:, :], lnv_r[sl4][:, :], AF.Exp, scale=-0.5))(),
                                   deps=[ln, rstd_war[sl4]])
                        nm = P.add("dve",
                                   (lambda sl=sl, sl4=sl4: lambda e: e.scalar_tensor_tensor(
                                       negmur_r[sl4][:, :], mv_r[sl4][:, 0:1], -1.0,
                                       rstd_r[sl4][:, :], ALU.mult, ALU.mult))(),
                                   deps=[ex, ag])
                        tx = P.add("dve",
                                   (lambda sl=sl, sl4=sl4: lambda e: e.tensor_scalar(
                                       tmpx[sl][:, :], xr2[sl][:, :], negmur_r[sl4][:, 0:1],
                                       None, ALU.add))(),
                                   deps=[nm, ld])
                        x1 = P.add("dve",
                                   (lambda sl=sl, sl4=sl4, ao=ao: lambda e: e.scalar_tensor_tensor(
                                       x1r[sl][:, :], ao[:, :], rstd_r[sl4][:, 0:1],
                                       tmpx[sl][:, :], ALU.mult, ALU.add))(),
                                   deps=[tx, stx_reader[sl]])
                        xr2_reader[sl] = tx
                        ao_reader[sl] = x1
                        stx = P.add("sync",
                                    (lambda tc=tc, sl=sl, sl4=sl4: lambda e: e.dma_start(
                                        out=x1_scr[tc * 128:(tc + 1) * 128, :], in_=x1r[sl][:, :]))(),
                                    deps=[x1], dma=True, q="sa" if sl == 0 else "sb")
                        x1_stores.append(stx)
                        stx_reader[sl] = stx
                        st2 = P.add("dve",
                                    (lambda sl=sl, sl4=sl4: lambda e: (
                                        e.bn_stats(stats[:, 0:6], x1r[sl][:, 0:512]),
                                        e.bn_stats(stats[:, 6:12], x1r[sl][:, 512:1024]))[-1])(),
                                    deps=[x1, stats_reader[0]])
                        ag2 = P.add("dve",
                                    (lambda sl=sl, sl4=sl4: lambda e: e.bn_aggr(mv_r[sl4][:, :], stats[:, :]))(),
                                    deps=[st2, nm, ln])
                        stats_reader[0] = ag2
                        ln2 = P.add("act",
                                    (lambda sl=sl, sl4=sl4: lambda e: e.activation(
                                        lnv_r[sl4][:, :], mv_r[sl4][:, 1:2], AF.Ln, bias=epsv[:, 0:1]))(),
                                    deps=[ag2, ex])
                        ex2 = P.add("act",
                                    (lambda sl=sl, sl4=sl4: lambda e: e.activation(
                                        rstd_r[sl4][:, :], lnv_r[sl4][:, :], AF.Exp, scale=-0.5))(),
                                    deps=[ln2, x1])
                        nm2 = P.add("dve",
                                    (lambda sl=sl, sl4=sl4: lambda e: e.scalar_tensor_tensor(
                                        negmur_r[sl4][:, :], mv_r[sl4][:, 0:1], -1.0,
                                        rstd_r[sl4][:, :], ALU.mult, ALU.mult))(),
                                    deps=[ex2, ag2, tx])
                        t2 = P.add("dve",
                                   (lambda sl=sl, sl4=sl4: lambda e: e.tensor_scalar(
                                       t2r[sl][:, :], x1r[sl][:, :], rstd_r[sl4][:, 0:1],
                                       negmur_r[sl4][:, 0:1], ALU.mult, ALU.add))(),
                                   deps=[nm2, t2r_reader[sl]])
                        mv_readers[sl4] = [nm2, ln2]
                        rstd_war[sl4] = t2
                        t2_of[tc] = t2

                    tr2_of = {}

                    def oproj_trans(tc):
                        sl = tc % 2
                        tr = None
                        for dc in range(NJ):
                            tr = P.add("pe",
                                       (lambda sl=sl, dc=dc: lambda e: e.transpose(
                                           tp2[sl][:, 128 * dc:128 * dc + 128],
                                           t2r[sl][:, 128 * dc:128 * dc + 128],
                                           identt[:, :]))(),
                                       deps=[t2_of[tc], tp2_reader[sl]] if dc == 0 else [])
                        t2r_reader[sl] = tr
                        tr2_of[tc] = tr

                    def oproj_evac(tc):
                        sl = tc % 2
                        ev = P.add("dve",
                                   (lambda sl=sl, tc=tc: lambda e: e.tensor_copy(
                                       t2T.ap().rearrange("p (c t) -> p c t", c=NJ)[:, :, 128 * tc:128 * tc + 128],
                                       tp2[sl].ap().rearrange("p (c t) -> p c t", c=NJ)))(),
                                   deps=[tr2_of[tc]])
                        tp2_reader[sl] = ev
                        t2T_evacs.append(ev)

                    for k in range(TC + 3):
                        if k < TC:
                            oproj_mms(k)
                        if k >= 2 and k - 2 < TC:
                            oproj_trans(k - 2)
                        if k >= 1 and k - 1 < TC:
                            oproj_chain(k - 1)
                        if k >= 3 and k - 3 < TC:
                            oproj_evac(k - 3)
                    t2T_ready = t2T_evacs[-1]

                with ExitStack() as fctx:
                    fsb = lambda name, shape, dt: fctx.enter_context(
                        nc.sbuf_tensor(f"{name}_b{b}", shape, dt))
                    fo = [fctx.enter_context(nc.psum_tensor(f"fo{i}_b{b}", [128, 1024], F32))
                          for i in range(3)]
                    x1l = [fsb(f"x1l{i}", [128, D], F32) for i in range(2)]
                    tmpf = [fsb(f"tmpf{i}", [128, D], F32) for i in range(2)]
                    outr = [fsb(f"outr{i}", [128, D], F32) for i in range(2)]

                    x1l_reader = [None, None]
                    outr_reader = [None, None]
                    fsums_reader = [None] * 4
                    fsumsq_reader = [None] * 4
                    mv_readers = [[] for _ in range(4)]
                    rstd_war = [None] * 4
                    fo_reader = [None, None, None]
                    fmm_of = {}
                    fld_of = {}

                    def ffn_mms(tc):
                        sl = tc % 2
                        ld = P.add("sync",
                                   (lambda tc=tc, sl=sl: lambda e: e.dma_start(
                                       out=x1l[sl][:, :],
                                       in_=x1_scr[tc * 128:(tc + 1) * 128, :]))(),
                                   deps=[x1l_reader[sl], x1_stores[tc],
                                         t2T_ready if tc < 2 else None],
                                   dma=True, q="xe" if sl == 0 else "xf")
                        fld_of[tc] = ld
                        slf = tc % 3
                        mm_last = None
                        for j in range(NJ):
                            lhs = t2T[:, SEQ * j + 128 * tc: SEQ * j + 128 * tc + 128]
                            items = [
                                (fo[slf][:, 512 * eb:512 * eb + 512], lhs,
                                 mt_sb[:, D * j + 512 * eb: D * j + 512 * eb + 512],
                                 j == 0, False, None)
                                for eb in range(2)]
                            dep0 = [t2T_ready, fo_reader[slf], op_biasld, ld_onesrow, mt_last] if j == 0 else ()
                            mm_last = P.add(
                                "pe",
                                (lambda items=items: lambda e: mm_group(e, items))(),
                                deps=dep0)
                        items = [
                            (fo[slf][:, 512 * eb:512 * eb + 512], onesrowt[:, :],
                             biasbf[:, 512 * eb:512 * eb + 512], False, True, None)
                            for eb in range(2)]
                        mm_last = P.add(
                            "pe",
                            (lambda items=items: lambda e: mm_group(e, items))(),
                            deps=())
                        fmm_of[tc] = mm_last

                    def ffn_chain(tc):
                        sl = tc % 2
                        sl4 = tc % 4
                        slf = tc % 3
                        mm_last = fmm_of[tc]
                        ld = fld_of[tc]
                        sm_ = P.add("act",
                                    (lambda slf=slf, sl4=sl4: lambda e: e.activation(
                                        dumpt[:, :], fo[slf][:, :], AF.Identity,
                                        accum_out=sums_r[sl4][:, 0:1]))(),
                                    deps=[mm_last, fsums_reader[sl4], sums_reader[sl4]])
                        sq_ = P.add("act",
                                    (lambda slf=slf, sl4=sl4: lambda e: e.activation(
                                        dumpt[:, :], fo[slf][:, :], AF.Square,
                                        accum_out=sumsq_r[sl4][:, 0:1]))(),
                                    deps=[sm_, fsumsq_reader[sl4], sumsq_reader[sl4]])
                        mean_ = P.add("dve",
                                      (lambda sl=sl, sl4=sl4: lambda e: e.tensor_scalar(
                                          mv_r[sl4][:, 0:1], sums_r[sl4][:, 0:1],
                                          1.0 / D, None, ALU.mult))(),
                                      deps=[sm_] + mv_readers[sl4])
                        fsums_reader[sl4] = mean_
                        nm2_ = P.add("dve",
                                     (lambda sl=sl, sl4=sl4: lambda e: e.scalar_tensor_tensor(
                                         negm2_r[sl4][:, 0:1], mv_r[sl4][:, 0:1], -1.0,
                                         mv_r[sl4][:, 0:1], ALU.mult, ALU.mult))(),
                                     deps=[mean_])
                        ag = P.add("dve",
                                   (lambda sl=sl, sl4=sl4: lambda e: e.scalar_tensor_tensor(
                                       mv_r[sl4][:, 1:2], sumsq_r[sl4][:, 0:1], 1.0 / D,
                                       negm2_r[sl4][:, 0:1], ALU.mult, ALU.add))(),
                                   deps=[sq_, nm2_])
                        fsumsq_reader[sl4] = ag
                        ln = P.add("act",
                                   (lambda sl=sl, sl4=sl4: lambda e: e.activation(
                                       lnv_r[sl4][:, :], mv_r[sl4][:, 1:2], AF.Ln, bias=epsv[:, 0:1]))(),
                                   deps=[ag])
                        ex = P.add("act",
                                   (lambda sl=sl, sl4=sl4: lambda e: e.activation(
                                       rstd_r[sl4][:, :], lnv_r[sl4][:, :], AF.Exp, scale=-0.5))(),
                                   deps=[ln, rstd_war[sl4]])
                        nm = P.add("dve",
                                   (lambda sl=sl, sl4=sl4: lambda e: e.scalar_tensor_tensor(
                                       negmur_r[sl4][:, :], mv_r[sl4][:, 0:1], -1.0,
                                       rstd_r[sl4][:, :], ALU.mult, ALU.mult))(),
                                   deps=[ex, ag])
                        tx = P.add("dve",
                                   (lambda sl=sl, sl4=sl4: lambda e: e.tensor_scalar(
                                       tmpf[sl][:, :], x1l[sl][:, :], negmur_r[sl4][:, 0:1],
                                       None, ALU.add))(),
                                   deps=[nm, ld])
                        x2 = P.add("dve",
                                   (lambda sl=sl, slf=slf, sl4=sl4: lambda e: e.scalar_tensor_tensor(
                                       outr[sl][:, :], fo[slf][:, :], rstd_r[sl4][:, 0:1],
                                       tmpf[sl][:, :], ALU.mult, ALU.add))(),
                                   deps=[tx, outr_reader[sl]])
                        rstd_war[sl4] = x2
                        x1l_reader[sl] = tx
                        mv_readers[sl4] = [nm, ln]
                        fo_reader[slf] = x2
                        sto = P.add("sync",
                                    (lambda b=b, tc=tc, sl=sl, sl4=sl4: lambda e: e.dma_start(
                                        out=y_hbm[b, tc * 128:(tc + 1) * 128, :],
                                        in_=outr[sl][:, :]))(),
                                    deps=[x2], dma=True, q="ya" if sl == 0 else "yb")
                        outr_reader[sl] = sto
                        out_dmas.append(sto)
                        return x2

                    for k in range(TC + 1):
                        if k < TC:
                            ffn_mms(k)
                        if k >= 1:
                            x2op = ffn_chain(k - 1)
                    prev_batch_last = x2op

        for od in out_dmas:
            od.needs_inc = True

        P.finalize()

        engines = {"pe": nc.tensor, "act": nc.scalar, "dve": nc.vector,
                   "pool": nc.gpsimd, "sync": nc.sync, "pdma": nc.gpsimd,
                   "gpsimd": nc.gpsimd}
        semmap = dict(sems)

        with nc.Block() as block:
            @block.sync
            def _(sync):
                P.run_engine("sync", engines, semmap)
                for g in ("ya", "yb"):
                    if P.totals.get(g):
                        sync.wait_ge(semmap[g], P.totals[g])

            @block.gpsimd
            def _(gpsimd):
                P.run_engine("gpsimd", engines, semmap)

            @block.vector
            def _(vector):
                P.run_engine("dve", engines, semmap)

            @block.scalar
            def _(scalar):
                P.run_engine("act", engines, semmap)

            @block.tensor
            def _(tensor):
                P.run_engine("pe", engines, semmap)

    return nc


_NC_CACHE = None


def kernel(L0_attn_hidden_states, L0_attn_Wq, L0_attn_Wk, L0_attn_Wv, L0_attn_Wo,
           L0_ffn_w_freq_real, L0_ffn_w_freq_imag, L0_ffn_bias):
    global _NC_CACHE
    x = np.ascontiguousarray(np.asarray(L0_attn_hidden_states, dtype=np.float32))
    wq = np.asarray(L0_attn_Wq, dtype=np.float32)
    wk = np.asarray(L0_attn_Wk, dtype=np.float32)
    wv = np.asarray(L0_attn_Wv, dtype=np.float32)
    wo = np.asarray(L0_attn_Wo, dtype=np.float32)
    wr = np.ascontiguousarray(np.asarray(L0_ffn_w_freq_real, dtype=np.float32))
    wi = np.ascontiguousarray(np.asarray(L0_ffn_w_freq_imag, dtype=np.float32))
    fb = np.ascontiguousarray(np.asarray(L0_ffn_bias, dtype=np.float32).reshape(1, D))

    wqt = np.ascontiguousarray(wq.T)
    wkt = np.ascontiguousarray(wk.T)
    wvt = np.ascontiguousarray(wv.T)
    wot = np.ascontiguousarray(wo.T)

    jj = np.arange(D, dtype=np.int64)
    ang = 2.0 * np.pi * (np.outer(jj, jj) % D).astype(np.float64) / D
    ctab = np.cos(ang).astype(ml_dtypes.bfloat16)
    stab = np.sin(ang).astype(ml_dtypes.bfloat16)
    snegtab = (-np.sin(ang)).astype(ml_dtypes.bfloat16)
    ident = np.eye(128, dtype=ml_dtypes.bfloat16)
    ones64 = np.ones((128, 64), dtype=ml_dtypes.bfloat16)
    onesrow = np.ones((1, 128), dtype=ml_dtypes.bfloat16)

    if _NC_CACHE is None:
        _NC_CACHE = build_nc()
    nc = _NC_CACHE

    shared = dict(wqt=wqt, wkt=wkt, wvt=wvt, wot=wot, wfr=wr, wfi=wi,
                  fbias=fb, ctab=ctab, snegtab=snegtab,
                  ident=ident, ones64=ones64, onesrow=onesrow)
    in_maps = []
    for c in range(NCORES):
        m = dict(shared)
        m["x"] = np.ascontiguousarray(x[c * BPC:(c + 1) * BPC])
        m["cshard"] = np.ascontiguousarray(ctab[:, 128 * c:128 * c + 128])
        m["sshard"] = np.ascontiguousarray(stab[:, 128 * c:128 * c + 128])
        m["sshardneg"] = np.ascontiguousarray(snegtab[:, 128 * c:128 * c + 128])
        in_maps.append(m)

    res = run_bass_kernel_spmd(nc, in_maps, core_ids=list(range(NCORES)))
    out = np.concatenate([np.asarray(r["y"]) for r in res.results], axis=0)
    return out


# version 33
# speedup vs baseline: 1.0223x; 1.0148x over previous
import numpy as np
import ml_dtypes
from contextlib import ExitStack

import concourse.bass as bass
import concourse.mybir as mybir
from concourse.bass_utils import run_bass_kernel_spmd

F32 = mybir.dt.float32
BF16 = mybir.dt.bfloat16
AF = mybir.ActivationFunctionType
ALU = mybir.AluOpType

D = 1024
SEQ = 2048
BPC = 2
H = 16
HD = 64
NCORES = 8
TC = SEQ // 128
NJ = D // 128
EPS = 1e-5
QSCALE = HD ** -0.5


class Op:
    __slots__ = ("idx", "engine", "emit", "deps", "dma", "sem_val", "needs_inc", "grp")

    def __init__(self, idx, engine, emit, deps, dma, grp):
        self.idx = idx
        self.engine = engine
        self.emit = emit
        self.deps = deps
        self.dma = dma
        self.sem_val = None
        self.needs_inc = False
        self.grp = grp


class Prog:

    def __init__(self):
        self.ops = []

    def add(self, engine, emit, deps=(), dma=False, q=None):
        deps = [d for d in deps if d is not None]
        assert not dma or q is not None, "DMA ops must name a semaphore group"
        op = Op(len(self.ops), engine, emit, deps, dma, q if dma else engine)
        for d in deps:
            assert d.idx < op.idx
        self.ops.append(op)
        return op

    def finalize(self):
        for op in self.ops:
            if op.dma:
                op.needs_inc = True
            kept = []
            for d in op.deps:
                if d.engine == op.engine and op.engine == "pe" and not d.dma and not op.dma:
                    continue
                if d.dma and op.dma and d.engine == op.engine:
                    continue
                kept.append(d)
                d.needs_inc = True
            op.deps = kept
        counters = {}
        grp_last = {}
        for op in self.ops:
            if op.needs_inc:
                counters[op.grp] = counters.get(op.grp, 0) + (16 if op.dma else 1)
                op.sem_val = counters[op.grp]
                grp_last[op.grp] = op
        for op in self.ops:
            for d in op.deps:
                if d.grp in ("cst", "pcst"):
                    d.sem_val = grp_last[d.grp].sem_val
        per_engine = {}
        stream_of = {"pdma": "gpsimd", "pool": "gpsimd"}
        for op in self.ops:
            per_engine.setdefault(stream_of.get(op.engine, op.engine), []).append(op)
        self.per_engine = per_engine
        self.totals = counters

    def run_engine(self, name, engines, sems):
        eng = engines[name]
        waited = {}
        for op in self.per_engine.get(name, []):
            for d in op.deps:
                v = d.sem_val
                assert v is not None
                if waited.get(d.grp, -1) >= v:
                    continue
                eng.wait_ge(sems[d.grp], v)
                waited[d.grp] = v
            last = op.emit(eng)
            if op.needs_inc:
                last.then_inc(sems[op.grp], 16 if op.dma else 1)


def mm_group(e, items):
    first = None
    last = None
    for (out_ap, lhsT_ap, rhs_ap, st, sp, tp) in items:
        if first is None:
            first = e.matmul(out_ap, lhsT_ap, rhs_ap, start=st, stop=sp,
                             tile_position=tp)
            last = first
        else:
            inst = mybir.InstMatmult(
                name=e.bass.get_next_instruction_name(),
                ins=[e.lower_ap_or_imm(rhs_ap), e.lower_ap_or_imm(lhsT_ap)],
                outs=[e.lower_ap_or_imm(out_ap)],
                replication_resolution=0, replication_shift_amnt=0,
                replication_num_rows=0,
                start_tensor_calc=st, stop_tensor_calc=sp,
                tile_position=first.ins.tile_position,
                tile_size=first.ins.tile_size,
                ldweights=False)
            last = e.add_instruction(inst)
    return last


def build_nc():
    nc = bass.Bass("TRN2", target_bir_lowering=False)

    x_hbm = nc.dram_tensor("x", [BPC, SEQ, D], F32, kind="ExternalInput").ap()
    wqt_h = nc.dram_tensor("wqt", [D, D], F32, kind="ExternalInput").ap()
    wkt_h = nc.dram_tensor("wkt", [D, D], F32, kind="ExternalInput").ap()
    wvt_h = nc.dram_tensor("wvt", [D, D], F32, kind="ExternalInput").ap()
    wot_h = nc.dram_tensor("wot", [D, D], F32, kind="ExternalInput").ap()
    wr_h = nc.dram_tensor("wfr", [D, D], F32, kind="ExternalInput").ap()
    wi_h = nc.dram_tensor("wfi", [D, D], F32, kind="ExternalInput").ap()
    bias_h = nc.dram_tensor("fbias", [1, D], F32, kind="ExternalInput").ap()
    c_h = nc.dram_tensor("ctab", [D, D], BF16, kind="ExternalInput").ap()
    sneg_h = nc.dram_tensor("snegtab", [D, D], BF16, kind="ExternalInput").ap()
    csh_h = nc.dram_tensor("cshard", [D, 128], BF16, kind="ExternalInput").ap()
    ssh_h = nc.dram_tensor("sshard", [D, 128], BF16, kind="ExternalInput").ap()
    sshn_h = nc.dram_tensor("sshardneg", [D, 128], BF16, kind="ExternalInput").ap()
    ccin = nc.dram_tensor("ccin", [128, D], BF16, kind="Internal").ap()
    ccout = nc.dram_tensor("ccout", [NCORES * 128, D], BF16, kind="Internal").ap()
    ident_h = nc.dram_tensor("ident", [128, 128], BF16, kind="ExternalInput").ap()
    ones64_h = nc.dram_tensor("ones64", [128, 64], BF16, kind="ExternalInput").ap()
    onesrow_h = nc.dram_tensor("onesrow", [1, 128], BF16, kind="ExternalInput").ap()

    y_hbm = nc.dram_tensor("y", [BPC, SEQ, D], F32, kind="ExternalOutput").ap()
    x1_scr = nc.dram_tensor("x1scr", [SEQ, D], F32, kind="Internal").ap()

    def chunked(hbm_ap):
        return hbm_ap.rearrange("(c p) d -> p c d", p=128)

    P = Prog()
    out_dmas = []

    with ExitStack() as top:
        sb = lambda name, shape, dt: top.enter_context(nc.sbuf_tensor(name, shape, dt))
        sems = {}
        for name in ("pe", "act", "dve", "pool", "cst", "pcst", "cc", "mt",
                     "xa", "xb", "xc", "xd", "xe", "xf", "wa", "wb",
                     "sa", "sb", "ya", "yb"):
            sems[name] = top.enter_context(nc.semaphore("sem_" + name))

        identt = sb("identt", [128, 128], BF16)
        ones64t = sb("ones64t", [128, 64], BF16)
        onesrowt = sb("onesrowt", [1, 128], BF16)
        epsv = sb("epsv", [128, 1], F32)
        biasbf = sb("biasbf", [1, D], BF16)
        mt_sb = sb("mt_sb", [128, NJ * D], BF16)
        stats = sb("stats", [128, 12], F32)
        stats1 = sb("stats1", [128, 12], F32)
        xr1 = [sb(f"xr1_{i}", [128, D], F32) for i in range(2)]
        t1r1 = [sb(f"t1r1_{i}", [128, D], BF16) for i in range(2)]
        mv1_r = [sb(f"mv1_{i}", [128, 2], F32) for i in range(2)]
        lnv1_r = [sb(f"lnv1_{i}", [128, 1], F32) for i in range(2)]
        rstd1_r = [sb(f"rstd1_{i}", [128, 1], F32) for i in range(2)]
        negmur1_r = [sb(f"negmur1_{i}", [128, 1], F32) for i in range(2)]
        ln1_state = {"xr_reader": [None, None], "t1r_reader": [None, None],
                     "mv_readers": [[], []], "rstd_war": [None, None]}
        mv_r = [sb(f"mv{i}", [128, 2], F32) for i in range(4)]
        lnv_r = [sb(f"lnv{i}", [128, 1], F32) for i in range(4)]
        rstd_r = [sb(f"rstd{i}", [128, 1], F32) for i in range(4)]
        negmur_r = [sb(f"negmur{i}", [128, 1], F32) for i in range(4)]
        sums_r = [sb(f"sums{i}", [128, 1], F32) for i in range(4)]
        sumsq_r = [sb(f"sumsq{i}", [128, 1], F32) for i in range(4)]
        negm2_r = [sb(f"negm2{i}", [128, 1], F32) for i in range(4)]
        dumpt = sb("dumpt", [128, D], BF16)

        ld_ident = P.add("sync", lambda e: e.dma_start(out=identt[:, :], in_=ident_h), dma=True, q="cst")
        ld_ones64 = P.add("sync", lambda e: e.dma_start(out=ones64t[:, :], in_=ones64_h), dma=True, q="cst")
        ld_onesrow = P.add("sync", lambda e: e.dma_start(out=onesrowt[:, :], in_=onesrow_h), dma=True, q="cst")
        op_eps = P.add("dve", lambda e: e.memset(epsv[:, :], EPS))
        op_biasld = P.add("pdma", lambda e: e.dma_start(out=biasbf[:, :], in_=bias_h), dma=True, q="pcst")

        with ExitStack() as mctx:
            msb = lambda name, shape, dt: mctx.enter_context(nc.sbuf_tensor(name, shape, dt))
            csb = msb("csb", [128, NJ * D], BF16)
            snegsb = msb("snegsb", [128, NJ * D], BF16)
            wrsb = msb("wrsb", [128, NJ * D], BF16)
            wisb = msb("wisb", [128, NJ * D], BF16)
            cshsb = msb("cshsb", [128, NJ * 128], BF16)
            sshsb = msb("sshsb", [128, NJ * 128], BF16)
            uvsb = msb("uvsb", [128, 2048], BF16)
            mtcsb = msb("mtcsb", [128, D], BF16)
            pm = mctx.enter_context(nc.psum_tensor("pm", [128, 2048], F32))

            ld_c = P.add("sync", lambda e: e.dma_start(
                out=csb.ap().rearrange("p (c d) -> p c d", c=NJ), in_=chunked(c_h)), dma=True, q="cst")
            ld_sneg = P.add("sync", lambda e: e.dma_start(
                out=snegsb.ap().rearrange("p (c d) -> p c d", c=NJ), in_=chunked(sneg_h)), dma=True, q="cst")
            ld_csh = P.add("sync", lambda e: e.dma_start(
                out=cshsb.ap().rearrange("p (c d) -> p c d", c=NJ),
                in_=csh_h.rearrange("(c p) d -> p c d", p=128)), dma=True, q="cst")
            ld_ssh = P.add("sync", lambda e: e.dma_start(
                out=sshsb.ap().rearrange("p (c d) -> p c d", c=NJ),
                in_=ssh_h.rearrange("(c p) d -> p c d", p=128)), dma=True, q="cst")
            ld_wr = P.add("pdma", lambda e: e.dma_start(
                out=wrsb.ap().rearrange("p (c d) -> p c d", c=NJ), in_=chunked(wr_h)), dma=True, q="pcst")
            ld_wi = P.add("pdma", lambda e: e.dma_start(
                out=wisb.ap().rearrange("p (c d) -> p c d", c=NJ), in_=chunked(wi_h)), dma=True, q="pcst")

            sshnsb = msb("sshnsb", [128, NJ * 128], BF16)
            ld_sshn = P.add("pdma", lambda e: e.dma_start(
                out=sshnsb.ap().rearrange("p (c d) -> p c d", c=NJ),
                in_=sshn_h.rearrange("(c p) d -> p c d", p=128)), dma=True, q="pcst")

            mm_last = None
            first_dep = [ld_c, ld_sneg, ld_csh, ld_ssh, ld_sshn, ld_wr, ld_wi]
            emitted_first = False
            for half, pairs in ((0, ((wrsb, cshsb), (wisb, sshsb))),
                                (1024, ((wisb, cshsb), (wrsb, sshnsb)))):
                for ob in range(NJ):
                    def emit_s1(half=half, ob=ob, pairs=pairs):
                        def go(e):
                            last = None
                            n = 0
                            for j in range(NJ):
                                for (lh, rh) in pairs:
                                    n += 1
                                    last = e.matmul(
                                        pm[:, half + 128 * ob: half + 128 * ob + 128],
                                        lh[:, D * j + 128 * ob: D * j + 128 * ob + 128],
                                        rh[:, 128 * j:128 * j + 128],
                                        start=(n == 1), stop=(n == 2 * NJ))
                            return last
                        return go
                    mm_last = P.add("pe", emit_s1(),
                                    deps=first_dep if not emitted_first else ())
                    emitted_first = True
            ev_uv = P.add("dve",
                          (lambda: lambda e: e.tensor_copy(uvsb[:, :], pm[:, :]))(),
                          deps=[mm_last])
            mm_last2 = None
            for ob2 in range(2):
                def emit_s2(ob2=ob2):
                    def go(e):
                        last = None
                        n = 0
                        for oc in range(NJ):
                            for (loff, rh) in ((0, csb), (1024, snegsb)):
                                n += 1
                                last = e.matmul(
                                    pm[:, 1024 + 512 * ob2:1024 + 512 * ob2 + 512],
                                    uvsb[:, loff + 128 * oc: loff + 128 * oc + 128],
                                    rh[:, D * oc + 512 * ob2: D * oc + 512 * ob2 + 512],
                                    start=(n == 1), stop=(n == 2 * NJ))
                        return last
                    return go
                mm_last2 = P.add("pe", emit_s2(), deps=[ev_uv] if ob2 == 0 else ())
            ev_mtc = P.add("dve",
                           (lambda: lambda e: e.tensor_scalar(
                               mtcsb[:, :], pm[:, 1024:2048], 1.0 / D, None, ALU.mult))(),
                           deps=[mm_last2])
            dma_ccin = P.add("pdma",
                             (lambda: lambda e: e.dma_start(out=ccin, in_=mtcsb[:, :]))(),
                             deps=[ev_mtc], dma=True, q="cc")
            cc = P.add("pool",
                       (lambda: lambda e: e.collective_compute(
                           "AllGather", ALU.bypass,
                           replica_groups=[list(range(NCORES))],
                           ins=[ccin.opt()], outs=[ccout.opt()]))(),
                       deps=[dma_ccin])
            cc_op = cc
        mt_last = None

        bufA = sb("bufA", [128, NJ * SEQ], BF16)
        bufQ = sb("bufQ", [128, NJ * SEQ], BF16)
        bufK = sb("bufK", [128, NJ * SEQ], BF16)
        bufV = sb("bufV", [128, TC * D], BF16)
        qz_readA = [None, None]
        qz_readB = [None, None]
        stats_reader = [None]
        prev_wq_readers = []

        prev_batch_last = None
        for b in range(BPC):
            with ExitStack() as bctx:
                t1T = bufA
                qT = bufQ
                kT = bufK
                v_sb = bufV

                def make_ln1(b, t1T, tpl, tp_extra, evac_extra):
                    st = ln1_state
                    ctx_ops = {}
                    tr1_of = {}
                    evacs = []

                    def ln1_iter(k):
                        if k < TC:
                            tc = k
                            sl = tc % 2
                            ld = P.add("sync",
                                       (lambda b=b, tc=tc, sl=sl:
                                        lambda e: e.dma_start(out=xr1[sl][:, :],
                                                              in_=x_hbm[b, tc * 128:(tc + 1) * 128, :]))(),
                                       deps=[st["xr_reader"][sl]], dma=True,
                                       q="xa" if sl == 0 else "xb")
                            st1 = P.add("dve",
                                        (lambda sl=sl: lambda e: (
                                            e.bn_stats(stats1[:, 0:6], xr1[sl][:, 0:512]),
                                            e.bn_stats(stats1[:, 6:12], xr1[sl][:, 512:1024]))[-1])(),
                                        deps=[ld, st.get("stats1_reader")])
                            ag = P.add("dve",
                                       (lambda sl=sl: lambda e: e.bn_aggr(mv1_r[sl][:, :], stats1[:, :]))(),
                                       deps=[st1] + st["mv_readers"][sl])
                            st["stats1_reader"] = ag
                            ln = P.add("act",
                                       (lambda sl=sl: lambda e: e.activation(
                                           lnv1_r[sl][:, :], mv1_r[sl][:, 1:2], AF.Ln, bias=epsv[:, 0:1]))(),
                                       deps=[ag, op_eps])
                            ex = P.add("act",
                                       (lambda sl=sl: lambda e: e.activation(
                                           rstd1_r[sl][:, :], lnv1_r[sl][:, :], AF.Exp, scale=-0.5))(),
                                       deps=[ln, st["rstd_war"][sl]])
                            ctx_ops[tc] = (ld, ag, ln, ex)
                        if k >= 1 and k - 1 < TC:
                            tc = k - 1
                            sl = tc % 2
                            (ld, ag, ln, ex) = ctx_ops[tc]
                            nm = P.add("dve",
                                       (lambda sl=sl: lambda e: e.scalar_tensor_tensor(
                                           negmur1_r[sl][:, :], mv1_r[sl][:, 0:1], -1.0,
                                           rstd1_r[sl][:, :], ALU.mult, ALU.mult))(),
                                       deps=[ex, ag])
                            ap_ = P.add("dve",
                                        (lambda sl=sl: lambda e: e.tensor_scalar(
                                            t1r1[sl][:, :], xr1[sl][:, :], rstd1_r[sl][:, 0:1],
                                            negmur1_r[sl][:, 0:1], ALU.mult, ALU.add))(),
                                        deps=[nm, st["t1r_reader"][sl]])
                            st["mv_readers"][sl] = [nm, ln]
                            st["rstd_war"][sl] = ap_
                            st["xr_reader"][sl] = ap_
                            tr = None
                            for dc in range(NJ):
                                tr = P.add("pe",
                                           (lambda sl=sl, dc=dc: lambda e: e.transpose(
                                               tpl[sl][:, 128 * dc:128 * dc + 128],
                                               t1r1[sl][:, 128 * dc:128 * dc + 128],
                                               identt[:, :]))(),
                                           deps=[ap_, st.get("tp_reader", [None, None])[sl],
                                                 ld_ident] + tp_extra if dc == 0 else [])
                            st["t1r_reader"][sl] = tr
                            tr1_of[tc] = tr
                        if k >= 2 and k - 2 < TC:
                            tc = k - 2
                            sl = tc % 2
                            ev = P.add("dve",
                                       (lambda sl=sl, tc=tc: lambda e: e.tensor_copy(
                                           t1T.ap().rearrange("p (c t) -> p c t", c=NJ)[:, :, 128 * tc:128 * tc + 128],
                                           tpl[sl].ap().rearrange("p (c t) -> p c t", c=NJ)))(),
                                       deps=[tr1_of[tc]] + evac_extra)
                            st.setdefault("tp_reader", [None, None])[sl] = ev
                            evacs.append(ev)
                    return ln1_iter, evacs

                with ExitStack() as lctx:
                    tpl0 = [lctx.enter_context(nc.psum_tensor(f"tpl0_{i}_b{b}", [128, 1024], BF16))
                            for i in range(2)]
                    ln1_state["tp_reader"] = [None, None]
                    ln1_it, ln1_evacs = make_ln1(b, t1T, tpl0, [], [])
                    for k in range(TC + 2):
                        ln1_it(k)
                    t1T_ready = ln1_evacs[-1]
                if mt_last is None:
                    mt_last = P.add("sync",
                                    (lambda: lambda e: e.dma_start(
                                        out=mt_sb.ap().rearrange("p (c d) -> p c d", c=NJ),
                                        in_=ccout.rearrange("(c p) d -> p c d", p=128)))(),
                                    deps=[cc_op], dma=True, q="mt")

                with ExitStack() as wctx:
                    wq_sb = wctx.enter_context(nc.sbuf_tensor(f"wq_b{b}", [128, NJ * D], BF16))
                    wk_sb = wctx.enter_context(nc.sbuf_tensor(f"wk_b{b}", [128, NJ * D], BF16))
                    pp = wctx.enter_context(nc.psum_tensor(f"pp_b{b}", [128, 2048], F32))
                    ld_wq = P.add("pdma", (lambda e: e.dma_start(
                        out=wq_sb.ap().rearrange("p (c d) -> p c d", c=NJ), in_=chunked(wqt_h))),
                        deps=[t1T_ready, prev_batch_last] + prev_wq_readers,
                        dma=True, q="wa")
                    ld_wk = P.add("pdma", (lambda e: e.dma_start(
                        out=wk_sb.ap().rearrange("p (c d) -> p c d", c=NJ), in_=chunked(wkt_h))),
                        deps=[t1T_ready, prev_batch_last] + prev_wq_readers,
                        dma=True, q="wb")

                    def project_fmajor(dst_sb, w_sb, deps0):
                        evs = []
                        slot_ev = [None, None]
                        un = 0
                        for ec in range(NJ):
                            for tbh in range(2):
                                slot = un % 2
                                first_deps = [t1T_ready] + deps0 + (
                                    [slot_ev[slot]] if slot_ev[slot] else [])
                                mm_last = None
                                for j in range(NJ):
                                    lhs = w_sb[:, D * j + 128 * ec: D * j + 128 * ec + 128]
                                    items = [
                                        (pp[:, 1024 * slot + 512 * tb:1024 * slot + 512 * tb + 512],
                                         lhs,
                                         t1T[:, SEQ * j + 1024 * tbh + 512 * tb:
                                             SEQ * j + 1024 * tbh + 512 * tb + 512],
                                         j == 0, j == NJ - 1, None)
                                        for tb in range(2)]
                                    mm_last = P.add(
                                        "pe",
                                        (lambda items=items: lambda e: mm_group(e, items))(),
                                        deps=first_deps if j == 0 else ())
                                ev = P.add("dve",
                                           (lambda ec=ec, tbh=tbh, slot=slot, dst_sb=dst_sb:
                                            lambda e: e.tensor_copy(
                                                dst_sb[:, SEQ * ec + 1024 * tbh:
                                                       SEQ * ec + 1024 * tbh + 1024],
                                                pp[:, 1024 * slot:1024 * slot + 1024]))(),
                                           deps=[mm_last])
                                slot_ev[slot] = ev
                                evs.append(ev)
                                un += 1
                        return evs

                    q_done = project_fmajor(qT, wq_sb, [ld_wq])
                    k_done = project_fmajor(kT, wk_sb, [ld_wk, q_done[-1]])
                    ld_wv = P.add("pdma", (lambda e: e.dma_start(
                        out=wq_sb.ap().rearrange("p (c d) -> p c d", c=NJ), in_=chunked(wvt_h))),
                        deps=[q_done[-1]], dma=True, q="wa")
                    v_evs = []
                    for tc in range(TC):
                        mm_last = None
                        for j in range(NJ):
                            lhs = t1T[:, SEQ * j + 128 * tc: SEQ * j + 128 * tc + 128]
                            items = [
                                (pp[:, 512 * (2 * (tc % 2) + cb):512 * (2 * (tc % 2) + cb) + 512],
                                 lhs,
                                 wq_sb[:, D * j + 512 * cb: D * j + 512 * cb + 512],
                                 j == 0, j == NJ - 1, None)
                                for cb in range(2)]
                            dep0 = ()
                            if j == 0:
                                dep0 = [ld_wv, k_done[-1]]
                                if len(v_evs) >= 2:
                                    dep0 = dep0 + [v_evs[-2]]
                            mm_last = P.add(
                                "pe",
                                (lambda items=items: lambda e: mm_group(e, items))(),
                                deps=dep0)
                        ev = P.add("dve",
                                   (lambda tc=tc: lambda e: e.tensor_copy(
                                       v_sb[:, D * tc:D * tc + D],
                                       pp[:, 1024 * (tc % 2):1024 * (tc % 2) + 1024]))(),
                                   deps=[mm_last])
                        v_evs.append(ev)
                    v_done = v_evs[-1]
                    prev_wq_readers = list(v_evs) + [q_done[-1], k_done[-1]]

                ctxT = bufA
                with ExitStack() as actx:
                    asb = lambda name, shape, dt: actx.enter_context(
                        nc.sbuf_tensor(f"{name}_b{b}", shape, dt))
                    psa = actx.enter_context(nc.psum_tensor(f"psa_b{b}", [128, 1024], F32))
                    psb = actx.enter_context(nc.psum_tensor(f"psb_b{b}", [128, 1024], F32))
                    ct2 = [actx.enter_context(nc.psum_tensor(f"ct{i}_b{b}", [128, 1024], F32))
                           for i in range(2)]
                    expS = [asb(f"expS{i}", [128, 2 * 1024], BF16) for i in range(2)]
                    zacc = asb("zacc", [128, 2 * 1024], BF16)
                    invz = asb("invz", [128, 1024], F32)
                    qzA = asb("qzA", [128, 2048], BF16)
                    qzB = asb("qzB", [128, 2048], BF16)
                    mzA = P.add("dve",
                                lambda e: e.memset(qzA[64:128, :], 0.0),
                                deps=[qz_readA[0], qz_readA[1], prev_batch_last])
                    mzB = P.add("dve",
                                lambda e: e.memset(qzB[0:64, :], 0.0),
                                deps=[qz_readB[0], qz_readB[1], prev_batch_last])

                    last_expA = None
                    last_expB = None
                    slot_consumers = [[], []]
                    ct_last_reader = [None, None]
                    last_zacc_readerA = None
                    last_zacc_readerB = None
                    last_zaA = None
                    last_zaB = None
                    last_tt = None

                    steps = [(p, qh, kb) for p in range(H // 2)
                             for qh in range(2) for kb in range(TC)]

                    qz_cpA = {}
                    qz_cpB = {}

                    def emit_qz(u):
                        p, qh = divmod(u, 2)
                        su = u % 2
                        qbase = SEQ * p + 1024 * qh
                        cA = P.add("dve",
                                   (lambda su=su, qbase=qbase: lambda e: e.tensor_copy(
                                       qzA[0:64, 1024 * su:1024 * su + 1024],
                                       qT[0:64, qbase:qbase + 1024]))(),
                                   deps=[q_done[-1], mzA, qz_readA[su]])
                        cB = P.add("dve",
                                   (lambda su=su, qbase=qbase: lambda e: e.tensor_copy(
                                       qzB[64:128, 1024 * su:1024 * su + 1024],
                                       qT[64:128, qbase:qbase + 1024]))(),
                                   deps=[q_done[-1], mzB, qz_readB[su]])
                        qz_cpA[u] = cA
                        qz_cpB[u] = cB

                    def emit_scores_A(step, extra):
                        p, qh, kb = step
                        u = p * 2 + qh
                        su = u % 2
                        lhsA = kT[:, SEQ * p + 128 * kb: SEQ * p + 128 * kb + 128]
                        items = [
                            (psa[:, 512 * i:512 * i + 512], lhsA,
                             qzA[:, 1024 * su + 512 * i: 1024 * su + 512 * i + 512],
                             True, True, None) for i in range(2)]
                        g = P.add("pe",
                                  (lambda items=items: lambda e: mm_group(e, items))(),
                                  deps=[last_expA, qz_cpA[u]] + extra)
                        qz_readA[su] = g
                        return g

                    def emit_scores_B(step, extra):
                        p, qh, kb = step
                        u = p * 2 + qh
                        su = u % 2
                        lhsB = kT[:, SEQ * p + 128 * kb: SEQ * p + 128 * kb + 128]
                        items = [
                            (psb[:, 512 * i:512 * i + 512], lhsB,
                             qzB[:, 1024 * su + 512 * i: 1024 * su + 512 * i + 512],
                             True, True, None) for i in range(2)]
                        g = P.add("pe",
                                  (lambda items=items: lambda e: mm_group(e, items))(),
                                  deps=[last_expB, qz_cpB[u]] + extra)
                        qz_readB[su] = g
                        return g

                    ext0 = [q_done[-1], k_done[-1], v_done]
                    pending_Z = None
                    emit_qz(0)
                    pend_A = emit_scores_A(steps[0], ext0)
                    pend_B = emit_scores_B(steps[0], ext0)
                    for i, (p, qh, kb) in enumerate(steps):
                        u = p * 2 + qh
                        qbase = SEQ * p + 1024 * qh
                        cts = ct2[u % 2]
                        sl = i % 2
                        if kb == 0 and u + 1 < 2 * (H // 2):
                            emit_qz(u + 1)
                        sA, sB = pend_A, pend_B
                        exA = P.add("act",
                                    (lambda sl=sl: lambda e: e.activation(
                                        expS[sl][:, 0:1024], psa[:, :],
                                        AF.Exp, scale=QSCALE))(),
                                    deps=[sA] + slot_consumers[sl])
                        exB = P.add("act",
                                    (lambda sl=sl: lambda e: e.activation(
                                        expS[sl][:, 1024:2048], psb[:, :],
                                        AF.Exp, scale=QSCALE))(),
                                    deps=[sB])
                        iz_gate = None
                        if pending_Z is not None:
                            bcB2, qb2, cts2, cB2, zct = pending_Z
                            pending_Z = None
                            lnz = P.add("act",
                                        (lambda zct=zct: lambda e: e.activation(
                                            zct[:, :], zct[:, :], AF.Ln))(),
                                        deps=[bcB2])
                            izz = P.add("act",
                                        (lambda zct=zct: lambda e: e.activation(
                                            invz[:, :], zct[:, :], AF.Exp,
                                            scale=-1.0))(),
                                        deps=[lnz, last_tt])
                            tt = P.add("dve",
                                       (lambda qb2=qb2, cts2=cts2: lambda e: e.tensor_tensor(
                                           ctxT[:, qb2:qb2 + 1024],
                                           cts2[:, :], invz[:, :], ALU.mult))(),
                                       deps=[izz, cB2])
                            ct_last_reader[(u - 1) % 2] = tt
                            last_tt = tt
                            iz_gate = izz
                        slot_consumers[sl] = []
                        last_expA, last_expB = exA, exB
                        last_step = (i + 1 == len(steps))
                        cdepA = [exA]
                        if kb == 0 and ct_last_reader[u % 2] is not None:
                            cdepA.append(ct_last_reader[u % 2])
                        if kb == 0 and iz_gate is not None:
                            cdepA.append(iz_gate)
                        if not last_step:
                            pend_A = emit_scores_A(steps[i + 1], [])
                        lhscA = v_sb[:, D * kb + HD * (2 * p): D * kb + HD * (2 * p) + HD]
                        itemsCA = [
                            (cts[0:64, 512 * i2:512 * i2 + 512], lhscA,
                             expS[sl][:, 512 * i2:512 * i2 + 512],
                             kb == 0, kb == TC - 1, (0, 0)) for i2 in range(2)]
                        cA = P.add("pe",
                                   (lambda items=itemsCA: lambda e: mm_group(e, items))(),
                                   deps=cdepA)
                        if not last_step:
                            pend_B = emit_scores_B(steps[i + 1], [])
                        lhscB = v_sb[:, D * kb + HD * (2 * p + 1): D * kb + HD * (2 * p + 1) + HD]
                        itemsCB = [
                            (cts[64:128, 512 * i2:512 * i2 + 512], lhscB,
                             expS[sl][:, 1024 + 512 * i2: 1024 + 512 * i2 + 512],
                             kb == 0, kb == TC - 1, (0, 64)) for i2 in range(2)]
                        cB = P.add("pe",
                                   (lambda items=itemsCB: lambda e: mm_group(e, items))(),
                                   deps=[exB])
                        if kb == 0:
                            zaA = P.add("dve",
                                        (lambda sl=sl: lambda e: e.tensor_copy(
                                            zacc[:, 0:1024], expS[sl][:, 0:1024]))(),
                                        deps=[exA, last_zacc_readerA])
                            zaB = P.add("dve",
                                        (lambda sl=sl: lambda e: e.tensor_copy(
                                            zacc[:, 1024:2048], expS[sl][:, 1024:2048]))(),
                                        deps=[exB, last_zacc_readerB])
                        else:
                            zaA = P.add("dve",
                                        (lambda sl=sl: lambda e: e.tensor_tensor(
                                            zacc[:, 0:1024], zacc[:, 0:1024],
                                            expS[sl][:, 0:1024], ALU.add))(),
                                        deps=[exA, last_zaA])
                            zaB = P.add("dve",
                                        (lambda sl=sl: lambda e: e.tensor_tensor(
                                            zacc[:, 1024:2048], zacc[:, 1024:2048],
                                            expS[sl][:, 1024:2048], ALU.add))(),
                                        deps=[exB, last_zaB])
                        last_zaA, last_zaB = zaA, zaB
                        slot_consumers[sl] += [cA, cB, zaA, zaB]
                        if kb == TC - 1:
                            zct = ct2[(u + 1) % 2]
                            itemsZA = [
                                (zct[0:64, 512 * j:512 * j + 512], ones64t[:, :],
                                 zacc[:, 512 * j:512 * j + 512], True, True, (0, 0))
                                for j in range(2)]
                            bcA = P.add("pe",
                                        (lambda items=itemsZA: lambda e: mm_group(e, items))(),
                                        deps=[last_zaA, ld_ones64, last_tt,
                                              ct_last_reader[(u + 1) % 2]])
                            itemsZB = [
                                (zct[64:128, 512 * j:512 * j + 512], ones64t[:, :],
                                 zacc[:, 1024 + 512 * j: 1024 + 512 * j + 512],
                                 True, True, (0, 64)) for j in range(2)]
                            bcB = P.add("pe",
                                        (lambda items=itemsZB: lambda e: mm_group(e, items))(),
                                        deps=[last_zaB])
                            last_zacc_readerA = bcA
                            last_zacc_readerB = bcB
                            pending_Z = (bcB, qbase, cts, cB, zct)
                            if last_step:
                                bcB2, qb2, cts2, cB2, zct2 = pending_Z
                                pending_Z = None
                                lnz = P.add("act",
                                            (lambda zct2=zct2: lambda e: e.activation(
                                                zct2[:, :], zct2[:, :], AF.Ln))(),
                                            deps=[bcB2])
                                izf = P.add("act",
                                            (lambda zct2=zct2: lambda e: e.activation(
                                                invz[:, :], zct2[:, :], AF.Exp,
                                                scale=-1.0))(),
                                            deps=[lnz, last_tt])
                                tt = P.add("dve",
                                           (lambda qb2=qb2, cts2=cts2: lambda e: e.tensor_tensor(
                                               ctxT[:, qb2:qb2 + 1024],
                                               cts2[:, :], invz[:, :], ALU.mult))(),
                                           deps=[izf, cB2])
                                ct_last_reader[u % 2] = tt
                                last_tt = tt
                    attn_done = last_tt

                t2T = bufQ
                with ExitStack() as octx:
                    osb = lambda name, shape, dt: octx.enter_context(
                        nc.sbuf_tensor(f"{name}_b{b}", shape, dt))
                    ao2 = [octx.enter_context(nc.psum_tensor(f"ao{i}_b{b}", [128, 1024], F32))
                           for i in range(3)]
                    tp2 = [octx.enter_context(nc.psum_tensor(f"tp2_{i}_b{b}", [128, 1024], BF16))
                           for i in range(2)]
                    wo_sb = osb("wo", [128, NJ * D], BF16)
                    xr2 = [osb(f"xr2{i}", [128, D], F32) for i in range(2)]
                    tmpx = [osb(f"tmpx{i}", [128, D], F32) for i in range(2)]
                    x1r = [osb(f"x1r{i}", [128, D], F32) for i in range(2)]
                    t2r = [osb(f"t2r{i}", [128, D], BF16) for i in range(2)]
                    ld_wo = P.add("pdma", (lambda e: e.dma_start(
                        out=wo_sb.ap().rearrange("p (c d) -> p c d", c=NJ), in_=chunked(wot_h))),
                        deps=[attn_done], dma=True, q="wb")

                    xr2_reader = [None, None]
                    stx_reader = [None, None]
                    sums_reader = [None] * 4
                    sumsq_reader = [None] * 4
                    t2r_reader = [None, None]
                    tp2_reader = [None, None]
                    mv_readers = [[] for _ in range(4)]
                    rstd_war = [None] * 4
                    ao_reader = [None, None, None]
                    mm_of = {}
                    ld_of = {}
                    t2_of = {}
                    t2T_evacs = []
                    x1_stores = []

                    def oproj_mms(tc):
                        sl = tc % 2
                        ld = P.add("sync",
                                   (lambda b=b, tc=tc, sl=sl:
                                    lambda e: e.dma_start(out=xr2[sl][:, :],
                                                          in_=x_hbm[b, tc * 128:(tc + 1) * 128, :]))(),
                                   deps=[xr2_reader[sl], attn_done if tc < 2 else None],
                                   dma=True, q="xc" if sl == 0 else "xd")
                        ld_of[tc] = ld
                        slf = tc % 3
                        mm_last = None
                        for cc in range(NJ):
                            lhs = ctxT[:, SEQ * cc + 128 * tc: SEQ * cc + 128 * tc + 128]
                            items = [
                                (ao2[slf][:, 512 * eb:512 * eb + 512], lhs,
                                 wo_sb[:, D * cc + 512 * eb: D * cc + 512 * eb + 512],
                                 cc == 0, cc == NJ - 1, None)
                                for eb in range(2)]
                            dep0 = [attn_done, ld_wo, ao_reader[slf]] if cc == 0 else ()
                            mm_last = P.add(
                                "pe",
                                (lambda items=items: lambda e: mm_group(e, items))(),
                                deps=dep0)
                        mm_of[tc] = mm_last

                    def oproj_chain(tc):
                        sl = tc % 2
                        sl4 = tc % 4
                        slf = tc % 3
                        ao = ao2[slf]
                        mm_last = mm_of[tc]
                        ld = ld_of[tc]
                        sm_ = P.add("act",
                                    (lambda sl=sl, sl4=sl4, ao=ao: lambda e: e.activation(
                                        dumpt[:, :], ao[:, :], AF.Identity,
                                        accum_out=sums_r[sl4][:, 0:1]))(),
                                    deps=[mm_last, sums_reader[sl4]])
                        sq_ = P.add("act",
                                    (lambda sl=sl, sl4=sl4, ao=ao: lambda e: e.activation(
                                        dumpt[:, :], ao[:, :], AF.Square,
                                        accum_out=sumsq_r[sl4][:, 0:1]))(),
                                    deps=[sm_, sumsq_reader[sl4]])
                        mean_ = P.add("dve",
                                      (lambda sl=sl, sl4=sl4: lambda e: e.tensor_scalar(
                                          mv_r[sl4][:, 0:1], sums_r[sl4][:, 0:1],
                                          1.0 / D, None, ALU.mult))(),
                                      deps=[sm_] + mv_readers[sl4])
                        sums_reader[sl4] = mean_
                        nm2_ = P.add("dve",
                                     (lambda sl=sl, sl4=sl4: lambda e: e.scalar_tensor_tensor(
                                         negm2_r[sl4][:, 0:1], mv_r[sl4][:, 0:1], -1.0,
                                         mv_r[sl4][:, 0:1], ALU.mult, ALU.mult))(),
                                     deps=[mean_])
                        ag = P.add("dve",
                                   (lambda sl=sl, sl4=sl4: lambda e: e.scalar_tensor_tensor(
                                       mv_r[sl4][:, 1:2], sumsq_r[sl4][:, 0:1], 1.0 / D,
                                       negm2_r[sl4][:, 0:1], ALU.mult, ALU.add))(),
                                   deps=[sq_, nm2_])
                        sumsq_reader[sl4] = ag
                        ln = P.add("act",
                                   (lambda sl=sl, sl4=sl4: lambda e: e.activation(
                                       lnv_r[sl4][:, :], mv_r[sl4][:, 1:2], AF.Ln, bias=epsv[:, 0:1]))(),
                                   deps=[ag])
                        ex = P.add("act",
                                   (lambda sl=sl, sl4=sl4: lambda e: e.activation(
                                       rstd_r[sl4][:, :], lnv_r[sl4][:, :], AF.Exp, scale=-0.5))(),
                                   deps=[ln, rstd_war[sl4]])
                        nm = P.add("dve",
                                   (lambda sl=sl, sl4=sl4: lambda e: e.scalar_tensor_tensor(
                                       negmur_r[sl4][:, :], mv_r[sl4][:, 0:1], -1.0,
                                       rstd_r[sl4][:, :], ALU.mult, ALU.mult))(),
                                   deps=[ex, ag])
                        tx = P.add("dve",
                                   (lambda sl=sl, sl4=sl4: lambda e: e.tensor_scalar(
                                       tmpx[sl][:, :], xr2[sl][:, :], negmur_r[sl4][:, 0:1],
                                       None, ALU.add))(),
                                   deps=[nm, ld])
                        x1 = P.add("dve",
                                   (lambda sl=sl, sl4=sl4, ao=ao: lambda e: e.scalar_tensor_tensor(
                                       x1r[sl][:, :], ao[:, :], rstd_r[sl4][:, 0:1],
                                       tmpx[sl][:, :], ALU.mult, ALU.add))(),
                                   deps=[tx, stx_reader[sl]])
                        xr2_reader[sl] = tx
                        ao_reader[slf] = x1
                        stx = P.add("sync",
                                    (lambda tc=tc, sl=sl, sl4=sl4: lambda e: e.dma_start(
                                        out=x1_scr[tc * 128:(tc + 1) * 128, :], in_=x1r[sl][:, :]))(),
                                    deps=[x1], dma=True, q="sa" if sl == 0 else "sb")
                        x1_stores.append(stx)
                        stx_reader[sl] = stx
                        st2 = P.add("dve",
                                    (lambda sl=sl, sl4=sl4: lambda e: (
                                        e.bn_stats(stats[:, 0:6], x1r[sl][:, 0:512]),
                                        e.bn_stats(stats[:, 6:12], x1r[sl][:, 512:1024]))[-1])(),
                                    deps=[x1, stats_reader[0]])
                        ag2 = P.add("dve",
                                    (lambda sl=sl, sl4=sl4: lambda e: e.bn_aggr(mv_r[sl4][:, :], stats[:, :]))(),
                                    deps=[st2, nm, ln])
                        stats_reader[0] = ag2
                        ln2 = P.add("act",
                                    (lambda sl=sl, sl4=sl4: lambda e: e.activation(
                                        lnv_r[sl4][:, :], mv_r[sl4][:, 1:2], AF.Ln, bias=epsv[:, 0:1]))(),
                                    deps=[ag2, ex])
                        ex2 = P.add("act",
                                    (lambda sl=sl, sl4=sl4: lambda e: e.activation(
                                        rstd_r[sl4][:, :], lnv_r[sl4][:, :], AF.Exp, scale=-0.5))(),
                                    deps=[ln2, x1])
                        nm2 = P.add("dve",
                                    (lambda sl=sl, sl4=sl4: lambda e: e.scalar_tensor_tensor(
                                        negmur_r[sl4][:, :], mv_r[sl4][:, 0:1], -1.0,
                                        rstd_r[sl4][:, :], ALU.mult, ALU.mult))(),
                                    deps=[ex2, ag2, tx])
                        t2 = P.add("dve",
                                   (lambda sl=sl, sl4=sl4: lambda e: e.tensor_scalar(
                                       t2r[sl][:, :], x1r[sl][:, :], rstd_r[sl4][:, 0:1],
                                       negmur_r[sl4][:, 0:1], ALU.mult, ALU.add))(),
                                   deps=[nm2, t2r_reader[sl]])
                        mv_readers[sl4] = [nm2, ln2]
                        rstd_war[sl4] = t2
                        t2_of[tc] = t2

                    tr2_of = {}

                    def oproj_trans(tc):
                        sl = tc % 2
                        tr = None
                        for dc in range(NJ):
                            tr = P.add("pe",
                                       (lambda sl=sl, dc=dc: lambda e: e.transpose(
                                           tp2[sl][:, 128 * dc:128 * dc + 128],
                                           t2r[sl][:, 128 * dc:128 * dc + 128],
                                           identt[:, :]))(),
                                       deps=[t2_of[tc], tp2_reader[sl]] if dc == 0 else [])
                        t2r_reader[sl] = tr
                        tr2_of[tc] = tr

                    def oproj_evac(tc):
                        sl = tc % 2
                        ev = P.add("dve",
                                   (lambda sl=sl, tc=tc: lambda e: e.tensor_copy(
                                       t2T.ap().rearrange("p (c t) -> p c t", c=NJ)[:, :, 128 * tc:128 * tc + 128],
                                       tp2[sl].ap().rearrange("p (c t) -> p c t", c=NJ)))(),
                                   deps=[tr2_of[tc]])
                        tp2_reader[sl] = ev
                        t2T_evacs.append(ev)

                    for k in range(TC + 3):
                        if k < TC:
                            oproj_mms(k)
                        if k >= 2 and k - 2 < TC:
                            oproj_trans(k - 2)
                        if k >= 1 and k - 1 < TC:
                            oproj_chain(k - 1)
                        if k >= 3 and k - 3 < TC:
                            oproj_evac(k - 3)
                    t2T_ready = t2T_evacs[-1]

                with ExitStack() as fctx:
                    fsb = lambda name, shape, dt: fctx.enter_context(
                        nc.sbuf_tensor(f"{name}_b{b}", shape, dt))
                    fo = [fctx.enter_context(nc.psum_tensor(f"fo{i}_b{b}", [128, 1024], F32))
                          for i in range(3)]
                    x1l = [fsb(f"x1l{i}", [128, D], F32) for i in range(2)]
                    tmpf = [fsb(f"tmpf{i}", [128, D], F32) for i in range(2)]
                    outr = [fsb(f"outr{i}", [128, D], F32) for i in range(2)]

                    x1l_reader = [None, None]
                    outr_reader = [None, None]
                    fsums_reader = [None] * 4
                    fsumsq_reader = [None] * 4
                    mv_readers = [[] for _ in range(4)]
                    rstd_war = [None] * 4
                    fo_reader = [None, None, None]
                    fmm_of = {}
                    fld_of = {}

                    def ffn_mms(tc):
                        sl = tc % 2
                        ld = P.add("sync",
                                   (lambda tc=tc, sl=sl: lambda e: e.dma_start(
                                       out=x1l[sl][:, :],
                                       in_=x1_scr[tc * 128:(tc + 1) * 128, :]))(),
                                   deps=[x1l_reader[sl], x1_stores[tc],
                                         t2T_ready if tc < 2 else None],
                                   dma=True, q="xe" if sl == 0 else "xf")
                        fld_of[tc] = ld
                        slf = tc % 3
                        mm_last = None
                        for j in range(NJ):
                            lhs = t2T[:, SEQ * j + 128 * tc: SEQ * j + 128 * tc + 128]
                            items = [
                                (fo[slf][:, 512 * eb:512 * eb + 512], lhs,
                                 mt_sb[:, D * j + 512 * eb: D * j + 512 * eb + 512],
                                 j == 0, False, None)
                                for eb in range(2)]
                            dep0 = [t2T_ready, fo_reader[slf], op_biasld, ld_onesrow, mt_last] if j == 0 else ()
                            mm_last = P.add(
                                "pe",
                                (lambda items=items: lambda e: mm_group(e, items))(),
                                deps=dep0)
                        items = [
                            (fo[slf][:, 512 * eb:512 * eb + 512], onesrowt[:, :],
                             biasbf[:, 512 * eb:512 * eb + 512], False, True, None)
                            for eb in range(2)]
                        mm_last = P.add(
                            "pe",
                            (lambda items=items: lambda e: mm_group(e, items))(),
                            deps=())
                        fmm_of[tc] = mm_last

                    def ffn_chain(tc):
                        sl = tc % 2
                        sl4 = tc % 4
                        slf = tc % 3
                        mm_last = fmm_of[tc]
                        ld = fld_of[tc]
                        sm_ = P.add("act",
                                    (lambda slf=slf, sl4=sl4: lambda e: e.activation(
                                        dumpt[:, :], fo[slf][:, :], AF.Identity,
                                        accum_out=sums_r[sl4][:, 0:1]))(),
                                    deps=[mm_last, fsums_reader[sl4], sums_reader[sl4]])
                        sq_ = P.add("act",
                                    (lambda slf=slf, sl4=sl4: lambda e: e.activation(
                                        dumpt[:, :], fo[slf][:, :], AF.Square,
                                        accum_out=sumsq_r[sl4][:, 0:1]))(),
                                    deps=[sm_, fsumsq_reader[sl4], sumsq_reader[sl4]])
                        mean_ = P.add("dve",
                                      (lambda sl=sl, sl4=sl4: lambda e: e.tensor_scalar(
                                          mv_r[sl4][:, 0:1], sums_r[sl4][:, 0:1],
                                          1.0 / D, None, ALU.mult))(),
                                      deps=[sm_] + mv_readers[sl4])
                        fsums_reader[sl4] = mean_
                        nm2_ = P.add("dve",
                                     (lambda sl=sl, sl4=sl4: lambda e: e.scalar_tensor_tensor(
                                         negm2_r[sl4][:, 0:1], mv_r[sl4][:, 0:1], -1.0,
                                         mv_r[sl4][:, 0:1], ALU.mult, ALU.mult))(),
                                     deps=[mean_])
                        ag = P.add("dve",
                                   (lambda sl=sl, sl4=sl4: lambda e: e.scalar_tensor_tensor(
                                       mv_r[sl4][:, 1:2], sumsq_r[sl4][:, 0:1], 1.0 / D,
                                       negm2_r[sl4][:, 0:1], ALU.mult, ALU.add))(),
                                   deps=[sq_, nm2_])
                        fsumsq_reader[sl4] = ag
                        ln = P.add("act",
                                   (lambda sl=sl, sl4=sl4: lambda e: e.activation(
                                       lnv_r[sl4][:, :], mv_r[sl4][:, 1:2], AF.Ln, bias=epsv[:, 0:1]))(),
                                   deps=[ag])
                        ex = P.add("act",
                                   (lambda sl=sl, sl4=sl4: lambda e: e.activation(
                                       rstd_r[sl4][:, :], lnv_r[sl4][:, :], AF.Exp, scale=-0.5))(),
                                   deps=[ln, rstd_war[sl4]])
                        nm = P.add("dve",
                                   (lambda sl=sl, sl4=sl4: lambda e: e.scalar_tensor_tensor(
                                       negmur_r[sl4][:, :], mv_r[sl4][:, 0:1], -1.0,
                                       rstd_r[sl4][:, :], ALU.mult, ALU.mult))(),
                                   deps=[ex, ag])
                        tx = P.add("dve",
                                   (lambda sl=sl, sl4=sl4: lambda e: e.tensor_scalar(
                                       tmpf[sl][:, :], x1l[sl][:, :], negmur_r[sl4][:, 0:1],
                                       None, ALU.add))(),
                                   deps=[nm, ld])
                        x2 = P.add("dve",
                                   (lambda sl=sl, slf=slf, sl4=sl4: lambda e: e.scalar_tensor_tensor(
                                       outr[sl][:, :], fo[slf][:, :], rstd_r[sl4][:, 0:1],
                                       tmpf[sl][:, :], ALU.mult, ALU.add))(),
                                   deps=[tx, outr_reader[sl]])
                        rstd_war[sl4] = x2
                        x1l_reader[sl] = tx
                        mv_readers[sl4] = [nm, ln]
                        fo_reader[slf] = x2
                        sto = P.add("sync",
                                    (lambda b=b, tc=tc, sl=sl, sl4=sl4: lambda e: e.dma_start(
                                        out=y_hbm[b, tc * 128:(tc + 1) * 128, :],
                                        in_=outr[sl][:, :]))(),
                                    deps=[x2], dma=True, q="ya" if sl == 0 else "yb")
                        outr_reader[sl] = sto
                        out_dmas.append(sto)
                        return x2

                    for k in range(TC + 1):
                        if k < TC:
                            ffn_mms(k)
                        if k >= 1:
                            x2op = ffn_chain(k - 1)
                    prev_batch_last = x2op

        for od in out_dmas:
            od.needs_inc = True

        P.finalize()

        engines = {"pe": nc.tensor, "act": nc.scalar, "dve": nc.vector,
                   "pool": nc.gpsimd, "sync": nc.sync, "pdma": nc.gpsimd,
                   "gpsimd": nc.gpsimd}
        semmap = dict(sems)

        with nc.Block() as block:
            @block.sync
            def _(sync):
                P.run_engine("sync", engines, semmap)
                for g in ("ya", "yb"):
                    if P.totals.get(g):
                        sync.wait_ge(semmap[g], P.totals[g])

            @block.gpsimd
            def _(gpsimd):
                P.run_engine("gpsimd", engines, semmap)

            @block.vector
            def _(vector):
                P.run_engine("dve", engines, semmap)

            @block.scalar
            def _(scalar):
                P.run_engine("act", engines, semmap)

            @block.tensor
            def _(tensor):
                P.run_engine("pe", engines, semmap)

    return nc


_NC_CACHE = None


def kernel(L0_attn_hidden_states, L0_attn_Wq, L0_attn_Wk, L0_attn_Wv, L0_attn_Wo,
           L0_ffn_w_freq_real, L0_ffn_w_freq_imag, L0_ffn_bias):
    global _NC_CACHE
    x = np.ascontiguousarray(np.asarray(L0_attn_hidden_states, dtype=np.float32))
    wq = np.asarray(L0_attn_Wq, dtype=np.float32)
    wk = np.asarray(L0_attn_Wk, dtype=np.float32)
    wv = np.asarray(L0_attn_Wv, dtype=np.float32)
    wo = np.asarray(L0_attn_Wo, dtype=np.float32)
    wr = np.ascontiguousarray(np.asarray(L0_ffn_w_freq_real, dtype=np.float32))
    wi = np.ascontiguousarray(np.asarray(L0_ffn_w_freq_imag, dtype=np.float32))
    fb = np.ascontiguousarray(np.asarray(L0_ffn_bias, dtype=np.float32).reshape(1, D))

    wqt = np.ascontiguousarray(wq.T)
    wkt = np.ascontiguousarray(wk.T)
    wvt = np.ascontiguousarray(wv.T)
    wot = np.ascontiguousarray(wo.T)

    jj = np.arange(D, dtype=np.int64)
    ang = 2.0 * np.pi * (np.outer(jj, jj) % D).astype(np.float64) / D
    ctab = np.cos(ang).astype(ml_dtypes.bfloat16)
    stab = np.sin(ang).astype(ml_dtypes.bfloat16)
    snegtab = (-np.sin(ang)).astype(ml_dtypes.bfloat16)
    ident = np.eye(128, dtype=ml_dtypes.bfloat16)
    ones64 = np.ones((128, 64), dtype=ml_dtypes.bfloat16)
    onesrow = np.ones((1, 128), dtype=ml_dtypes.bfloat16)

    if _NC_CACHE is None:
        _NC_CACHE = build_nc()
    nc = _NC_CACHE

    shared = dict(wqt=wqt, wkt=wkt, wvt=wvt, wot=wot, wfr=wr, wfi=wi,
                  fbias=fb, ctab=ctab, snegtab=snegtab,
                  ident=ident, ones64=ones64, onesrow=onesrow)
    in_maps = []
    for c in range(NCORES):
        m = dict(shared)
        m["x"] = np.ascontiguousarray(x[c * BPC:(c + 1) * BPC])
        m["cshard"] = np.ascontiguousarray(ctab[:, 128 * c:128 * c + 128])
        m["sshard"] = np.ascontiguousarray(stab[:, 128 * c:128 * c + 128])
        m["sshardneg"] = np.ascontiguousarray(snegtab[:, 128 * c:128 * c + 128])
        in_maps.append(m)

    res = run_bass_kernel_spmd(nc, in_maps, core_ids=list(range(NCORES)))
    out = np.concatenate([np.asarray(r["y"]) for r in res.results], axis=0)
    return out


# version 34 (final)
# speedup vs baseline: 1.0283x; 1.0058x over previous
import numpy as np
import ml_dtypes
from contextlib import ExitStack

import concourse.bass as bass
import concourse.mybir as mybir
from concourse.bass_utils import run_bass_kernel_spmd

F32 = mybir.dt.float32
BF16 = mybir.dt.bfloat16
AF = mybir.ActivationFunctionType
ALU = mybir.AluOpType

D = 1024
SEQ = 2048
BPC = 2
H = 16
HD = 64
NCORES = 8
TC = SEQ // 128
NJ = D // 128
EPS = 1e-5
QSCALE = HD ** -0.5


class Op:
    __slots__ = ("idx", "engine", "emit", "deps", "dma", "sem_val", "needs_inc", "grp")

    def __init__(self, idx, engine, emit, deps, dma, grp):
        self.idx = idx
        self.engine = engine
        self.emit = emit
        self.deps = deps
        self.dma = dma
        self.sem_val = None
        self.needs_inc = False
        self.grp = grp


class Prog:

    def __init__(self):
        self.ops = []

    def add(self, engine, emit, deps=(), dma=False, q=None):
        deps = [d for d in deps if d is not None]
        assert not dma or q is not None, "DMA ops must name a semaphore group"
        op = Op(len(self.ops), engine, emit, deps, dma, q if dma else engine)
        for d in deps:
            assert d.idx < op.idx
        self.ops.append(op)
        return op

    def finalize(self):
        for op in self.ops:
            if op.dma:
                op.needs_inc = True
            kept = []
            for d in op.deps:
                if d.engine == op.engine and op.engine == "pe" and not d.dma and not op.dma:
                    continue
                if d.dma and op.dma and d.engine == op.engine:
                    continue
                kept.append(d)
                d.needs_inc = True
            op.deps = kept
        counters = {}
        grp_last = {}
        for op in self.ops:
            if op.needs_inc:
                counters[op.grp] = counters.get(op.grp, 0) + (16 if op.dma else 1)
                op.sem_val = counters[op.grp]
                grp_last[op.grp] = op
        for op in self.ops:
            for d in op.deps:
                if d.grp in ("cst", "pcst"):
                    d.sem_val = grp_last[d.grp].sem_val
        per_engine = {}
        stream_of = {"pdma": "gpsimd", "pool": "gpsimd"}
        for op in self.ops:
            per_engine.setdefault(stream_of.get(op.engine, op.engine), []).append(op)
        self.per_engine = per_engine
        self.totals = counters

    def run_engine(self, name, engines, sems):
        eng = engines[name]
        waited = {}
        for op in self.per_engine.get(name, []):
            for d in op.deps:
                v = d.sem_val
                assert v is not None
                if waited.get(d.grp, -1) >= v:
                    continue
                eng.wait_ge(sems[d.grp], v)
                waited[d.grp] = v
            last = op.emit(eng)
            if op.needs_inc:
                last.then_inc(sems[op.grp], 16 if op.dma else 1)


def mm_group(e, items):
    first = None
    last = None
    for (out_ap, lhsT_ap, rhs_ap, st, sp, tp) in items:
        if first is None:
            first = e.matmul(out_ap, lhsT_ap, rhs_ap, start=st, stop=sp,
                             tile_position=tp)
            last = first
        else:
            inst = mybir.InstMatmult(
                name=e.bass.get_next_instruction_name(),
                ins=[e.lower_ap_or_imm(rhs_ap), e.lower_ap_or_imm(lhsT_ap)],
                outs=[e.lower_ap_or_imm(out_ap)],
                replication_resolution=0, replication_shift_amnt=0,
                replication_num_rows=0,
                start_tensor_calc=st, stop_tensor_calc=sp,
                tile_position=first.ins.tile_position,
                tile_size=first.ins.tile_size,
                ldweights=False)
            last = e.add_instruction(inst)
    return last


def build_nc():
    nc = bass.Bass("TRN2", target_bir_lowering=False)

    x_hbm = nc.dram_tensor("x", [BPC, SEQ, D], F32, kind="ExternalInput").ap()
    wqt_h = nc.dram_tensor("wqt", [D, D], F32, kind="ExternalInput").ap()
    wkt_h = nc.dram_tensor("wkt", [D, D], F32, kind="ExternalInput").ap()
    wvt_h = nc.dram_tensor("wvt", [D, D], F32, kind="ExternalInput").ap()
    wot_h = nc.dram_tensor("wot", [D, D], F32, kind="ExternalInput").ap()
    wr_h = nc.dram_tensor("wfr", [D, D], F32, kind="ExternalInput").ap()
    wi_h = nc.dram_tensor("wfi", [D, D], F32, kind="ExternalInput").ap()
    bias_h = nc.dram_tensor("fbias", [1, D], F32, kind="ExternalInput").ap()
    c_h = nc.dram_tensor("ctab", [D, D], BF16, kind="ExternalInput").ap()
    sneg_h = nc.dram_tensor("snegtab", [D, D], BF16, kind="ExternalInput").ap()
    csh_h = nc.dram_tensor("cshard", [D, 128], BF16, kind="ExternalInput").ap()
    ssh_h = nc.dram_tensor("sshard", [D, 128], BF16, kind="ExternalInput").ap()
    sshn_h = nc.dram_tensor("sshardneg", [D, 128], BF16, kind="ExternalInput").ap()
    ccin = nc.dram_tensor("ccin", [128, D], BF16, kind="Internal").ap()
    ccout = nc.dram_tensor("ccout", [NCORES * 128, D], BF16, kind="Internal").ap()
    ident_h = nc.dram_tensor("ident", [128, 128], BF16, kind="ExternalInput").ap()
    ones64_h = nc.dram_tensor("ones64", [128, 64], BF16, kind="ExternalInput").ap()
    onesrow_h = nc.dram_tensor("onesrow", [1, 128], BF16, kind="ExternalInput").ap()

    y_hbm = nc.dram_tensor("y", [BPC, SEQ, D], F32, kind="ExternalOutput").ap()
    x1_scr = nc.dram_tensor("x1scr", [SEQ, D], F32, kind="Internal").ap()

    def chunked(hbm_ap):
        return hbm_ap.rearrange("(c p) d -> p c d", p=128)

    P = Prog()
    out_dmas = []

    with ExitStack() as top:
        sb = lambda name, shape, dt: top.enter_context(nc.sbuf_tensor(name, shape, dt))
        sems = {}
        for name in ("pe", "act", "dve", "pool", "cst", "pcst", "cc", "mt",
                     "xa", "xb", "xc", "xd", "xe", "xf", "wa", "wb",
                     "sa", "sb", "ya", "yb"):
            sems[name] = top.enter_context(nc.semaphore("sem_" + name))

        identt = sb("identt", [128, 128], BF16)
        ones64t = sb("ones64t", [128, 64], BF16)
        onesrowt = sb("onesrowt", [1, 128], BF16)
        epsv = sb("epsv", [128, 1], F32)
        biasbf = sb("biasbf", [1, D], BF16)
        mt_sb = sb("mt_sb", [128, NJ * D], BF16)
        stats = sb("stats", [128, 12], F32)
        stats1 = sb("stats1", [128, 12], F32)
        xr1 = [sb(f"xr1_{i}", [128, D], F32) for i in range(2)]
        t1r1 = [sb(f"t1r1_{i}", [128, D], BF16) for i in range(2)]
        mv1_r = [sb(f"mv1_{i}", [128, 2], F32) for i in range(2)]
        lnv1_r = [sb(f"lnv1_{i}", [128, 1], F32) for i in range(2)]
        rstd1_r = [sb(f"rstd1_{i}", [128, 1], F32) for i in range(2)]
        negmur1_r = [sb(f"negmur1_{i}", [128, 1], F32) for i in range(2)]
        ln1_state = {"xr_reader": [None, None], "t1r_reader": [None, None],
                     "mv_readers": [[], []], "rstd_war": [None, None]}
        mv_r = [sb(f"mv{i}", [128, 2], F32) for i in range(4)]
        lnv_r = [sb(f"lnv{i}", [128, 1], F32) for i in range(4)]
        rstd_r = [sb(f"rstd{i}", [128, 1], F32) for i in range(4)]
        negmur_r = [sb(f"negmur{i}", [128, 1], F32) for i in range(4)]
        sums_r = [sb(f"sums{i}", [128, 1], F32) for i in range(4)]
        sumsq_r = [sb(f"sumsq{i}", [128, 1], F32) for i in range(4)]
        negm2_r = [sb(f"negm2{i}", [128, 1], F32) for i in range(4)]
        dumpt = sb("dumpt", [128, D], BF16)

        ld_ident = P.add("sync", lambda e: e.dma_start(out=identt[:, :], in_=ident_h), dma=True, q="cst")
        ld_ones64 = P.add("sync", lambda e: e.dma_start(out=ones64t[:, :], in_=ones64_h), dma=True, q="cst")
        ld_onesrow = P.add("sync", lambda e: e.dma_start(out=onesrowt[:, :], in_=onesrow_h), dma=True, q="cst")
        op_eps = P.add("dve", lambda e: e.memset(epsv[:, :], EPS))
        op_biasld = P.add("pdma", lambda e: e.dma_start(out=biasbf[:, :], in_=bias_h), dma=True, q="pcst")

        with ExitStack() as mctx:
            msb = lambda name, shape, dt: mctx.enter_context(nc.sbuf_tensor(name, shape, dt))
            csb = msb("csb", [128, NJ * D], BF16)
            snegsb = msb("snegsb", [128, NJ * D], BF16)
            wrsb = msb("wrsb", [128, NJ * D], BF16)
            wisb = msb("wisb", [128, NJ * D], BF16)
            cshsb = msb("cshsb", [128, NJ * 128], BF16)
            sshsb = msb("sshsb", [128, NJ * 128], BF16)
            uvsb = msb("uvsb", [128, 2048], BF16)
            mtcsb = msb("mtcsb", [128, D], BF16)
            pm = mctx.enter_context(nc.psum_tensor("pm", [128, 2048], F32))

            ld_c = P.add("sync", lambda e: e.dma_start(
                out=csb.ap().rearrange("p (c d) -> p c d", c=NJ), in_=chunked(c_h)), dma=True, q="cst")
            ld_sneg = P.add("sync", lambda e: e.dma_start(
                out=snegsb.ap().rearrange("p (c d) -> p c d", c=NJ), in_=chunked(sneg_h)), dma=True, q="cst")
            ld_csh = P.add("sync", lambda e: e.dma_start(
                out=cshsb.ap().rearrange("p (c d) -> p c d", c=NJ),
                in_=csh_h.rearrange("(c p) d -> p c d", p=128)), dma=True, q="cst")
            ld_ssh = P.add("sync", lambda e: e.dma_start(
                out=sshsb.ap().rearrange("p (c d) -> p c d", c=NJ),
                in_=ssh_h.rearrange("(c p) d -> p c d", p=128)), dma=True, q="cst")
            ld_wr = P.add("pdma", lambda e: e.dma_start(
                out=wrsb.ap().rearrange("p (c d) -> p c d", c=NJ), in_=chunked(wr_h)), dma=True, q="pcst")
            ld_wi = P.add("pdma", lambda e: e.dma_start(
                out=wisb.ap().rearrange("p (c d) -> p c d", c=NJ), in_=chunked(wi_h)), dma=True, q="pcst")

            sshnsb = msb("sshnsb", [128, NJ * 128], BF16)
            ld_sshn = P.add("pdma", lambda e: e.dma_start(
                out=sshnsb.ap().rearrange("p (c d) -> p c d", c=NJ),
                in_=sshn_h.rearrange("(c p) d -> p c d", p=128)), dma=True, q="pcst")

            mm_last = None
            first_dep = [ld_c, ld_sneg, ld_csh, ld_ssh, ld_sshn, ld_wr, ld_wi]
            emitted_first = False
            for half, pairs in ((0, ((wrsb, cshsb), (wisb, sshsb))),
                                (1024, ((wisb, cshsb), (wrsb, sshnsb)))):
                for ob in range(NJ):
                    def emit_s1(half=half, ob=ob, pairs=pairs):
                        def go(e):
                            last = None
                            n = 0
                            for j in range(NJ):
                                for (lh, rh) in pairs:
                                    n += 1
                                    last = e.matmul(
                                        pm[:, half + 128 * ob: half + 128 * ob + 128],
                                        lh[:, D * j + 128 * ob: D * j + 128 * ob + 128],
                                        rh[:, 128 * j:128 * j + 128],
                                        start=(n == 1), stop=(n == 2 * NJ))
                            return last
                        return go
                    mm_last = P.add("pe", emit_s1(),
                                    deps=first_dep if not emitted_first else ())
                    emitted_first = True
            ev_uv = P.add("dve",
                          (lambda: lambda e: e.tensor_copy(uvsb[:, :], pm[:, :]))(),
                          deps=[mm_last])
            mm_last2 = None
            for ob2 in range(2):
                def emit_s2(ob2=ob2):
                    def go(e):
                        last = None
                        n = 0
                        for oc in range(NJ):
                            for (loff, rh) in ((0, csb), (1024, snegsb)):
                                n += 1
                                last = e.matmul(
                                    pm[:, 1024 + 512 * ob2:1024 + 512 * ob2 + 512],
                                    uvsb[:, loff + 128 * oc: loff + 128 * oc + 128],
                                    rh[:, D * oc + 512 * ob2: D * oc + 512 * ob2 + 512],
                                    start=(n == 1), stop=(n == 2 * NJ))
                        return last
                    return go
                mm_last2 = P.add("pe", emit_s2(), deps=[ev_uv] if ob2 == 0 else ())
            ev_mtc = P.add("dve",
                           (lambda: lambda e: e.tensor_scalar(
                               mtcsb[:, :], pm[:, 1024:2048], 1.0 / D, None, ALU.mult))(),
                           deps=[mm_last2])
            dma_ccin = P.add("pdma",
                             (lambda: lambda e: e.dma_start(out=ccin, in_=mtcsb[:, :]))(),
                             deps=[ev_mtc], dma=True, q="cc")
            cc = P.add("pool",
                       (lambda: lambda e: e.collective_compute(
                           "AllGather", ALU.bypass,
                           replica_groups=[list(range(NCORES))],
                           ins=[ccin.opt()], outs=[ccout.opt()]))(),
                       deps=[dma_ccin])
            cc_op = cc
        mt_last = None

        bufA = sb("bufA", [128, NJ * SEQ], BF16)
        bufQ = sb("bufQ", [128, NJ * SEQ], BF16)
        bufK = sb("bufK", [128, NJ * SEQ], BF16)
        bufV = sb("bufV", [128, TC * D], BF16)
        qz_readA = [None, None]
        qz_readB = [None, None]
        stats_reader = [None]
        prev_wq_readers = []

        prev_batch_last = None
        for b in range(BPC):
            with ExitStack() as bctx:
                t1T = bufA
                qT = bufQ
                kT = bufK
                v_sb = bufV

                def make_ln1(b, t1T, tpl, tp_extra, evac_extra):
                    st = ln1_state
                    ctx_ops = {}
                    tr1_of = {}
                    evacs = []

                    def ln1_iter(k):
                        if k < TC:
                            tc = k
                            sl = tc % 2
                            ld = P.add("sync",
                                       (lambda b=b, tc=tc, sl=sl:
                                        lambda e: e.dma_start(out=xr1[sl][:, :],
                                                              in_=x_hbm[b, tc * 128:(tc + 1) * 128, :]))(),
                                       deps=[st["xr_reader"][sl]], dma=True,
                                       q="xa" if sl == 0 else "xb")
                            st1 = P.add("dve",
                                        (lambda sl=sl: lambda e: (
                                            e.bn_stats(stats1[:, 0:6], xr1[sl][:, 0:512]),
                                            e.bn_stats(stats1[:, 6:12], xr1[sl][:, 512:1024]))[-1])(),
                                        deps=[ld, st.get("stats1_reader")])
                            ag = P.add("dve",
                                       (lambda sl=sl: lambda e: e.bn_aggr(mv1_r[sl][:, :], stats1[:, :]))(),
                                       deps=[st1] + st["mv_readers"][sl])
                            st["stats1_reader"] = ag
                            ln = P.add("act",
                                       (lambda sl=sl: lambda e: e.activation(
                                           lnv1_r[sl][:, :], mv1_r[sl][:, 1:2], AF.Ln, bias=epsv[:, 0:1]))(),
                                       deps=[ag, op_eps])
                            ex = P.add("act",
                                       (lambda sl=sl: lambda e: e.activation(
                                           rstd1_r[sl][:, :], lnv1_r[sl][:, :], AF.Exp, scale=-0.5))(),
                                       deps=[ln, st["rstd_war"][sl]])
                            ctx_ops[tc] = (ld, ag, ln, ex)
                        if k >= 1 and k - 1 < TC:
                            tc = k - 1
                            sl = tc % 2
                            (ld, ag, ln, ex) = ctx_ops[tc]
                            nm = P.add("dve",
                                       (lambda sl=sl: lambda e: e.scalar_tensor_tensor(
                                           negmur1_r[sl][:, :], mv1_r[sl][:, 0:1], -1.0,
                                           rstd1_r[sl][:, :], ALU.mult, ALU.mult))(),
                                       deps=[ex, ag])
                            ap_ = P.add("dve",
                                        (lambda sl=sl: lambda e: e.tensor_scalar(
                                            t1r1[sl][:, :], xr1[sl][:, :], rstd1_r[sl][:, 0:1],
                                            negmur1_r[sl][:, 0:1], ALU.mult, ALU.add))(),
                                        deps=[nm, st["t1r_reader"][sl]])
                            st["mv_readers"][sl] = [nm, ln]
                            st["rstd_war"][sl] = ap_
                            st["xr_reader"][sl] = ap_
                            tr = None
                            for dc in range(NJ):
                                tr = P.add("pe",
                                           (lambda sl=sl, dc=dc: lambda e: e.transpose(
                                               tpl[sl][:, 128 * dc:128 * dc + 128],
                                               t1r1[sl][:, 128 * dc:128 * dc + 128],
                                               identt[:, :]))(),
                                           deps=[ap_, st.get("tp_reader", [None, None])[sl],
                                                 ld_ident] + tp_extra if dc == 0 else [])
                            st["t1r_reader"][sl] = tr
                            tr1_of[tc] = tr
                        if k >= 2 and k - 2 < TC:
                            tc = k - 2
                            sl = tc % 2
                            ev = P.add("dve",
                                       (lambda sl=sl, tc=tc: lambda e: e.tensor_copy(
                                           t1T.ap().rearrange("p (c t) -> p c t", c=NJ)[:, :, 128 * tc:128 * tc + 128],
                                           tpl[sl].ap().rearrange("p (c t) -> p c t", c=NJ)))(),
                                       deps=[tr1_of[tc]] + evac_extra)
                            st.setdefault("tp_reader", [None, None])[sl] = ev
                            evacs.append(ev)
                    return ln1_iter, evacs

                with ExitStack() as lctx:
                    tpl0 = [lctx.enter_context(nc.psum_tensor(f"tpl0_{i}_b{b}", [128, 1024], BF16))
                            for i in range(2)]
                    ln1_state["tp_reader"] = [None, None]
                    ln1_it, ln1_evacs = make_ln1(b, t1T, tpl0, [], [])
                    for k in range(TC + 2):
                        ln1_it(k)
                    t1T_ready = ln1_evacs[-1]
                if mt_last is None:
                    mt_last = P.add("sync",
                                    (lambda: lambda e: e.dma_start(
                                        out=mt_sb.ap().rearrange("p (c d) -> p c d", c=NJ),
                                        in_=ccout.rearrange("(c p) d -> p c d", p=128)))(),
                                    deps=[cc_op], dma=True, q="mt")

                with ExitStack() as wctx:
                    wq_sb = wctx.enter_context(nc.sbuf_tensor(f"wq_b{b}", [128, NJ * D], BF16))
                    wk_sb = wctx.enter_context(nc.sbuf_tensor(f"wk_b{b}", [128, NJ * D], BF16))
                    pp = wctx.enter_context(nc.psum_tensor(f"pp_b{b}", [128, 2048], F32))
                    ld_wq = P.add("pdma", (lambda e: e.dma_start(
                        out=wq_sb.ap().rearrange("p (c d) -> p c d", c=NJ), in_=chunked(wqt_h))),
                        deps=[t1T_ready, prev_batch_last] + prev_wq_readers,
                        dma=True, q="wa")
                    ld_wk = P.add("pdma", (lambda e: e.dma_start(
                        out=wk_sb.ap().rearrange("p (c d) -> p c d", c=NJ), in_=chunked(wkt_h))),
                        deps=[t1T_ready, prev_batch_last] + prev_wq_readers,
                        dma=True, q="wb")

                    def project_fmajor(dst_sb, w_sb, deps0):
                        evs = []
                        slot_ev = [None, None]
                        un = 0
                        for ec in range(NJ):
                            for tbh in range(2):
                                slot = un % 2
                                first_deps = [t1T_ready] + deps0 + (
                                    [slot_ev[slot]] if slot_ev[slot] else [])
                                mm_last = None
                                for j in range(NJ):
                                    lhs = w_sb[:, D * j + 128 * ec: D * j + 128 * ec + 128]
                                    items = [
                                        (pp[:, 1024 * slot + 512 * tb:1024 * slot + 512 * tb + 512],
                                         lhs,
                                         t1T[:, SEQ * j + 1024 * tbh + 512 * tb:
                                             SEQ * j + 1024 * tbh + 512 * tb + 512],
                                         j == 0, j == NJ - 1, None)
                                        for tb in range(2)]
                                    mm_last = P.add(
                                        "pe",
                                        (lambda items=items: lambda e: mm_group(e, items))(),
                                        deps=first_deps if j == 0 else ())
                                ev = P.add("dve",
                                           (lambda ec=ec, tbh=tbh, slot=slot, dst_sb=dst_sb:
                                            lambda e: e.tensor_copy(
                                                dst_sb[:, SEQ * ec + 1024 * tbh:
                                                       SEQ * ec + 1024 * tbh + 1024],
                                                pp[:, 1024 * slot:1024 * slot + 1024]))(),
                                           deps=[mm_last])
                                slot_ev[slot] = ev
                                evs.append(ev)
                                un += 1
                        return evs

                    q_done = project_fmajor(qT, wq_sb, [ld_wq])
                    k_done = project_fmajor(kT, wk_sb, [ld_wk, q_done[-1]])
                    ld_wv = P.add("pdma", (lambda e: e.dma_start(
                        out=wq_sb.ap().rearrange("p (c d) -> p c d", c=NJ), in_=chunked(wvt_h))),
                        deps=[q_done[-1]], dma=True, q="wa")
                    v_evs = []
                    for tc in range(TC):
                        mm_last = None
                        for j in range(NJ):
                            lhs = t1T[:, SEQ * j + 128 * tc: SEQ * j + 128 * tc + 128]
                            items = [
                                (pp[:, 512 * (2 * (tc % 2) + cb):512 * (2 * (tc % 2) + cb) + 512],
                                 lhs,
                                 wq_sb[:, D * j + 512 * cb: D * j + 512 * cb + 512],
                                 j == 0, j == NJ - 1, None)
                                for cb in range(2)]
                            dep0 = ()
                            if j == 0:
                                dep0 = [ld_wv, k_done[-1]]
                                if len(v_evs) >= 2:
                                    dep0 = dep0 + [v_evs[-2]]
                            mm_last = P.add(
                                "pe",
                                (lambda items=items: lambda e: mm_group(e, items))(),
                                deps=dep0)
                        ev = P.add("dve",
                                   (lambda tc=tc: lambda e: e.tensor_copy(
                                       v_sb[:, D * tc:D * tc + D],
                                       pp[:, 1024 * (tc % 2):1024 * (tc % 2) + 1024]))(),
                                   deps=[mm_last])
                        v_evs.append(ev)
                    v_done = v_evs[-1]
                    prev_wq_readers = list(v_evs) + [q_done[-1], k_done[-1]]

                ctxT = bufA
                with ExitStack() as actx:
                    asb = lambda name, shape, dt: actx.enter_context(
                        nc.sbuf_tensor(f"{name}_b{b}", shape, dt))
                    psa = actx.enter_context(nc.psum_tensor(f"psa_b{b}", [128, 1024], F32))
                    psb = actx.enter_context(nc.psum_tensor(f"psb_b{b}", [128, 1024], F32))
                    ct2 = [actx.enter_context(nc.psum_tensor(f"ct{i}_b{b}", [128, 1024], F32))
                           for i in range(2)]
                    expS = [asb(f"expS{i}", [128, 2 * 1024], BF16) for i in range(3)]
                    zacc = asb("zacc", [128, 2 * 1024], BF16)
                    invz = asb("invz", [128, 1024], F32)
                    qzA = asb("qzA", [128, 2048], BF16)
                    qzB = asb("qzB", [128, 2048], BF16)
                    mzA = P.add("dve",
                                lambda e: e.memset(qzA[64:128, :], 0.0),
                                deps=[qz_readA[0], qz_readA[1], prev_batch_last])
                    mzB = P.add("dve",
                                lambda e: e.memset(qzB[0:64, :], 0.0),
                                deps=[qz_readB[0], qz_readB[1], prev_batch_last])

                    last_expA = None
                    last_expB = None
                    slot_consumers = [[], [], []]
                    ct_last_reader = [None, None]
                    last_zacc_readerA = None
                    last_zacc_readerB = None
                    last_zaA = None
                    last_zaB = None
                    last_tt = None

                    steps = [(p, qh, kb) for p in range(H // 2)
                             for qh in range(2) for kb in range(TC)]

                    qz_cpA = {}
                    qz_cpB = {}

                    def emit_qz(u):
                        p, qh = divmod(u, 2)
                        su = u % 2
                        qbase = SEQ * p + 1024 * qh
                        cA = P.add("dve",
                                   (lambda su=su, qbase=qbase: lambda e: e.tensor_copy(
                                       qzA[0:64, 1024 * su:1024 * su + 1024],
                                       qT[0:64, qbase:qbase + 1024]))(),
                                   deps=[q_done[-1], mzA, qz_readA[su]])
                        cB = P.add("dve",
                                   (lambda su=su, qbase=qbase: lambda e: e.tensor_copy(
                                       qzB[64:128, 1024 * su:1024 * su + 1024],
                                       qT[64:128, qbase:qbase + 1024]))(),
                                   deps=[q_done[-1], mzB, qz_readB[su]])
                        qz_cpA[u] = cA
                        qz_cpB[u] = cB

                    def emit_scores_A(step, extra):
                        p, qh, kb = step
                        u = p * 2 + qh
                        su = u % 2
                        lhsA = kT[:, SEQ * p + 128 * kb: SEQ * p + 128 * kb + 128]
                        items = [
                            (psa[:, 512 * i:512 * i + 512], lhsA,
                             qzA[:, 1024 * su + 512 * i: 1024 * su + 512 * i + 512],
                             True, True, None) for i in range(2)]
                        g = P.add("pe",
                                  (lambda items=items: lambda e: mm_group(e, items))(),
                                  deps=[last_expA, qz_cpA[u]] + extra)
                        qz_readA[su] = g
                        return g

                    def emit_scores_B(step, extra):
                        p, qh, kb = step
                        u = p * 2 + qh
                        su = u % 2
                        lhsB = kT[:, SEQ * p + 128 * kb: SEQ * p + 128 * kb + 128]
                        items = [
                            (psb[:, 512 * i:512 * i + 512], lhsB,
                             qzB[:, 1024 * su + 512 * i: 1024 * su + 512 * i + 512],
                             True, True, None) for i in range(2)]
                        g = P.add("pe",
                                  (lambda items=items: lambda e: mm_group(e, items))(),
                                  deps=[last_expB, qz_cpB[u]] + extra)
                        qz_readB[su] = g
                        return g

                    ext0 = [q_done[-1], k_done[-1], v_done]
                    pending_Z = None
                    emit_qz(0)
                    pend_A = emit_scores_A(steps[0], ext0)
                    pend_B = emit_scores_B(steps[0], ext0)
                    for i, (p, qh, kb) in enumerate(steps):
                        u = p * 2 + qh
                        qbase = SEQ * p + 1024 * qh
                        cts = ct2[u % 2]
                        sl = i % 3
                        if kb == 0 and u + 1 < 2 * (H // 2):
                            emit_qz(u + 1)
                        sA, sB = pend_A, pend_B
                        exA = P.add("act",
                                    (lambda sl=sl: lambda e: e.activation(
                                        expS[sl][:, 0:1024], psa[:, :],
                                        AF.Exp, scale=QSCALE))(),
                                    deps=[sA] + slot_consumers[sl])
                        exB = P.add("act",
                                    (lambda sl=sl: lambda e: e.activation(
                                        expS[sl][:, 1024:2048], psb[:, :],
                                        AF.Exp, scale=QSCALE))(),
                                    deps=[sB])
                        iz_gate = None
                        if pending_Z is not None:
                            bcB2, qb2, cts2, cB2, zct = pending_Z
                            pending_Z = None
                            lnz = P.add("act",
                                        (lambda zct=zct: lambda e: e.activation(
                                            zct[:, :], zct[:, :], AF.Ln))(),
                                        deps=[bcB2])
                            izz = P.add("act",
                                        (lambda zct=zct: lambda e: e.activation(
                                            invz[:, :], zct[:, :], AF.Exp,
                                            scale=-1.0))(),
                                        deps=[lnz, last_tt])
                            tt = P.add("dve",
                                       (lambda qb2=qb2, cts2=cts2: lambda e: e.tensor_tensor(
                                           ctxT[:, qb2:qb2 + 1024],
                                           cts2[:, :], invz[:, :], ALU.mult))(),
                                       deps=[izz, cB2])
                            ct_last_reader[(u - 1) % 2] = tt
                            last_tt = tt
                            iz_gate = izz
                        slot_consumers[sl] = []
                        last_expA, last_expB = exA, exB
                        last_step = (i + 1 == len(steps))
                        cdepA = [exA]
                        if kb == 0 and ct_last_reader[u % 2] is not None:
                            cdepA.append(ct_last_reader[u % 2])
                        if kb == 0 and iz_gate is not None:
                            cdepA.append(iz_gate)
                        if not last_step:
                            pend_A = emit_scores_A(steps[i + 1], [])
                        lhscA = v_sb[:, D * kb + HD * (2 * p): D * kb + HD * (2 * p) + HD]
                        itemsCA = [
                            (cts[0:64, 512 * i2:512 * i2 + 512], lhscA,
                             expS[sl][:, 512 * i2:512 * i2 + 512],
                             kb == 0, kb == TC - 1, (0, 0)) for i2 in range(2)]
                        cA = P.add("pe",
                                   (lambda items=itemsCA: lambda e: mm_group(e, items))(),
                                   deps=cdepA)
                        if not last_step:
                            pend_B = emit_scores_B(steps[i + 1], [])
                        lhscB = v_sb[:, D * kb + HD * (2 * p + 1): D * kb + HD * (2 * p + 1) + HD]
                        itemsCB = [
                            (cts[64:128, 512 * i2:512 * i2 + 512], lhscB,
                             expS[sl][:, 1024 + 512 * i2: 1024 + 512 * i2 + 512],
                             kb == 0, kb == TC - 1, (0, 64)) for i2 in range(2)]
                        cB = P.add("pe",
                                   (lambda items=itemsCB: lambda e: mm_group(e, items))(),
                                   deps=[exB])
                        if kb == 0:
                            zaA = P.add("dve",
                                        (lambda sl=sl: lambda e: e.tensor_copy(
                                            zacc[:, 0:1024], expS[sl][:, 0:1024]))(),
                                        deps=[exA, last_zacc_readerA])
                            zaB = P.add("dve",
                                        (lambda sl=sl: lambda e: e.tensor_copy(
                                            zacc[:, 1024:2048], expS[sl][:, 1024:2048]))(),
                                        deps=[exB, last_zacc_readerB])
                        else:
                            zaA = P.add("dve",
                                        (lambda sl=sl: lambda e: e.tensor_tensor(
                                            zacc[:, 0:1024], zacc[:, 0:1024],
                                            expS[sl][:, 0:1024], ALU.add))(),
                                        deps=[exA, last_zaA])
                            zaB = P.add("dve",
                                        (lambda sl=sl: lambda e: e.tensor_tensor(
                                            zacc[:, 1024:2048], zacc[:, 1024:2048],
                                            expS[sl][:, 1024:2048], ALU.add))(),
                                        deps=[exB, last_zaB])
                        last_zaA, last_zaB = zaA, zaB
                        slot_consumers[sl] += [cA, cB, zaA, zaB]
                        if kb == TC - 1:
                            zct = ct2[(u + 1) % 2]
                            itemsZA = [
                                (zct[0:64, 512 * j:512 * j + 512], ones64t[:, :],
                                 zacc[:, 512 * j:512 * j + 512], True, True, (0, 0))
                                for j in range(2)]
                            bcA = P.add("pe",
                                        (lambda items=itemsZA: lambda e: mm_group(e, items))(),
                                        deps=[last_zaA, ld_ones64, last_tt,
                                              ct_last_reader[(u + 1) % 2]])
                            itemsZB = [
                                (zct[64:128, 512 * j:512 * j + 512], ones64t[:, :],
                                 zacc[:, 1024 + 512 * j: 1024 + 512 * j + 512],
                                 True, True, (0, 64)) for j in range(2)]
                            bcB = P.add("pe",
                                        (lambda items=itemsZB: lambda e: mm_group(e, items))(),
                                        deps=[last_zaB])
                            last_zacc_readerA = bcA
                            last_zacc_readerB = bcB
                            pending_Z = (bcB, qbase, cts, cB, zct)
                            if last_step:
                                bcB2, qb2, cts2, cB2, zct2 = pending_Z
                                pending_Z = None
                                lnz = P.add("act",
                                            (lambda zct2=zct2: lambda e: e.activation(
                                                zct2[:, :], zct2[:, :], AF.Ln))(),
                                            deps=[bcB2])
                                izf = P.add("act",
                                            (lambda zct2=zct2: lambda e: e.activation(
                                                invz[:, :], zct2[:, :], AF.Exp,
                                                scale=-1.0))(),
                                            deps=[lnz, last_tt])
                                tt = P.add("dve",
                                           (lambda qb2=qb2, cts2=cts2: lambda e: e.tensor_tensor(
                                               ctxT[:, qb2:qb2 + 1024],
                                               cts2[:, :], invz[:, :], ALU.mult))(),
                                           deps=[izf, cB2])
                                ct_last_reader[u % 2] = tt
                                last_tt = tt
                    attn_done = last_tt

                t2T = bufQ
                with ExitStack() as octx:
                    osb = lambda name, shape, dt: octx.enter_context(
                        nc.sbuf_tensor(f"{name}_b{b}", shape, dt))
                    ao2 = [octx.enter_context(nc.psum_tensor(f"ao{i}_b{b}", [128, 1024], F32))
                           for i in range(3)]
                    tp2 = [octx.enter_context(nc.psum_tensor(f"tp2_{i}_b{b}", [128, 1024], BF16))
                           for i in range(2)]
                    wo_sb = osb("wo", [128, NJ * D], BF16)
                    xr2 = [osb(f"xr2{i}", [128, D], F32) for i in range(2)]
                    tmpx = [osb(f"tmpx{i}", [128, D], F32) for i in range(2)]
                    x1r = [osb(f"x1r{i}", [128, D], F32) for i in range(2)]
                    t2r = [osb(f"t2r{i}", [128, D], BF16) for i in range(2)]
                    ld_wo = P.add("pdma", (lambda e: e.dma_start(
                        out=wo_sb.ap().rearrange("p (c d) -> p c d", c=NJ), in_=chunked(wot_h))),
                        deps=[attn_done], dma=True, q="wb")

                    xr2_reader = [None, None]
                    stx_reader = [None, None]
                    sums_reader = [None] * 4
                    sumsq_reader = [None] * 4
                    t2r_reader = [None, None]
                    tp2_reader = [None, None]
                    mv_readers = [[] for _ in range(4)]
                    rstd_war = [None] * 4
                    ao_reader = [None, None, None]
                    mm_of = {}
                    ld_of = {}
                    t2_of = {}
                    t2T_evacs = []
                    x1_stores = []

                    def oproj_mms(tc):
                        sl = tc % 2
                        ld = P.add("sync",
                                   (lambda b=b, tc=tc, sl=sl:
                                    lambda e: e.dma_start(out=xr2[sl][:, :],
                                                          in_=x_hbm[b, tc * 128:(tc + 1) * 128, :]))(),
                                   deps=[xr2_reader[sl], attn_done if tc < 2 else None],
                                   dma=True, q="xc" if sl == 0 else "xd")
                        ld_of[tc] = ld
                        slf = tc % 3
                        mm_last = None
                        for cc in range(NJ):
                            lhs = ctxT[:, SEQ * cc + 128 * tc: SEQ * cc + 128 * tc + 128]
                            items = [
                                (ao2[slf][:, 512 * eb:512 * eb + 512], lhs,
                                 wo_sb[:, D * cc + 512 * eb: D * cc + 512 * eb + 512],
                                 cc == 0, cc == NJ - 1, None)
                                for eb in range(2)]
                            dep0 = [attn_done, ld_wo, ao_reader[slf]] if cc == 0 else ()
                            mm_last = P.add(
                                "pe",
                                (lambda items=items: lambda e: mm_group(e, items))(),
                                deps=dep0)
                        mm_of[tc] = mm_last

                    def oproj_chain(tc):
                        sl = tc % 2
                        sl4 = tc % 4
                        slf = tc % 3
                        ao = ao2[slf]
                        mm_last = mm_of[tc]
                        ld = ld_of[tc]
                        sm_ = P.add("act",
                                    (lambda sl=sl, sl4=sl4, ao=ao: lambda e: e.activation(
                                        dumpt[:, :], ao[:, :], AF.Identity,
                                        accum_out=sums_r[sl4][:, 0:1]))(),
                                    deps=[mm_last, sums_reader[sl4]])
                        sq_ = P.add("act",
                                    (lambda sl=sl, sl4=sl4, ao=ao: lambda e: e.activation(
                                        dumpt[:, :], ao[:, :], AF.Square,
                                        accum_out=sumsq_r[sl4][:, 0:1]))(),
                                    deps=[sm_, sumsq_reader[sl4]])
                        mean_ = P.add("dve",
                                      (lambda sl=sl, sl4=sl4: lambda e: e.tensor_scalar(
                                          mv_r[sl4][:, 0:1], sums_r[sl4][:, 0:1],
                                          1.0 / D, None, ALU.mult))(),
                                      deps=[sm_] + mv_readers[sl4])
                        sums_reader[sl4] = mean_
                        nm2_ = P.add("dve",
                                     (lambda sl=sl, sl4=sl4: lambda e: e.scalar_tensor_tensor(
                                         negm2_r[sl4][:, 0:1], mv_r[sl4][:, 0:1], -1.0,
                                         mv_r[sl4][:, 0:1], ALU.mult, ALU.mult))(),
                                     deps=[mean_])
                        ag = P.add("dve",
                                   (lambda sl=sl, sl4=sl4: lambda e: e.scalar_tensor_tensor(
                                       mv_r[sl4][:, 1:2], sumsq_r[sl4][:, 0:1], 1.0 / D,
                                       negm2_r[sl4][:, 0:1], ALU.mult, ALU.add))(),
                                   deps=[sq_, nm2_])
                        sumsq_reader[sl4] = ag
                        ln = P.add("act",
                                   (lambda sl=sl, sl4=sl4: lambda e: e.activation(
                                       lnv_r[sl4][:, :], mv_r[sl4][:, 1:2], AF.Ln, bias=epsv[:, 0:1]))(),
                                   deps=[ag])
                        ex = P.add("act",
                                   (lambda sl=sl, sl4=sl4: lambda e: e.activation(
                                       rstd_r[sl4][:, :], lnv_r[sl4][:, :], AF.Exp, scale=-0.5))(),
                                   deps=[ln, rstd_war[sl4]])
                        nm = P.add("dve",
                                   (lambda sl=sl, sl4=sl4: lambda e: e.scalar_tensor_tensor(
                                       negmur_r[sl4][:, :], mv_r[sl4][:, 0:1], -1.0,
                                       rstd_r[sl4][:, :], ALU.mult, ALU.mult))(),
                                   deps=[ex, ag])
                        tx = P.add("dve",
                                   (lambda sl=sl, sl4=sl4: lambda e: e.tensor_scalar(
                                       tmpx[sl][:, :], xr2[sl][:, :], negmur_r[sl4][:, 0:1],
                                       None, ALU.add))(),
                                   deps=[nm, ld])
                        x1 = P.add("dve",
                                   (lambda sl=sl, sl4=sl4, ao=ao: lambda e: e.scalar_tensor_tensor(
                                       x1r[sl][:, :], ao[:, :], rstd_r[sl4][:, 0:1],
                                       tmpx[sl][:, :], ALU.mult, ALU.add))(),
                                   deps=[tx, stx_reader[sl]])
                        xr2_reader[sl] = tx
                        ao_reader[slf] = x1
                        stx = P.add("sync",
                                    (lambda tc=tc, sl=sl, sl4=sl4: lambda e: e.dma_start(
                                        out=x1_scr[tc * 128:(tc + 1) * 128, :], in_=x1r[sl][:, :]))(),
                                    deps=[x1], dma=True, q="sa" if sl == 0 else "sb")
                        x1_stores.append(stx)
                        stx_reader[sl] = stx
                        st2 = P.add("dve",
                                    (lambda sl=sl, sl4=sl4: lambda e: (
                                        e.bn_stats(stats[:, 0:6], x1r[sl][:, 0:512]),
                                        e.bn_stats(stats[:, 6:12], x1r[sl][:, 512:1024]))[-1])(),
                                    deps=[x1, stats_reader[0]])
                        ag2 = P.add("dve",
                                    (lambda sl=sl, sl4=sl4: lambda e: e.bn_aggr(mv_r[sl4][:, :], stats[:, :]))(),
                                    deps=[st2, nm, ln])
                        stats_reader[0] = ag2
                        ln2 = P.add("act",
                                    (lambda sl=sl, sl4=sl4: lambda e: e.activation(
                                        lnv_r[sl4][:, :], mv_r[sl4][:, 1:2], AF.Ln, bias=epsv[:, 0:1]))(),
                                    deps=[ag2, ex])
                        ex2 = P.add("act",
                                    (lambda sl=sl, sl4=sl4: lambda e: e.activation(
                                        rstd_r[sl4][:, :], lnv_r[sl4][:, :], AF.Exp, scale=-0.5))(),
                                    deps=[ln2, x1])
                        nm2 = P.add("dve",
                                    (lambda sl=sl, sl4=sl4: lambda e: e.scalar_tensor_tensor(
                                        negmur_r[sl4][:, :], mv_r[sl4][:, 0:1], -1.0,
                                        rstd_r[sl4][:, :], ALU.mult, ALU.mult))(),
                                    deps=[ex2, ag2, tx])
                        t2 = P.add("dve",
                                   (lambda sl=sl, sl4=sl4: lambda e: e.tensor_scalar(
                                       t2r[sl][:, :], x1r[sl][:, :], rstd_r[sl4][:, 0:1],
                                       negmur_r[sl4][:, 0:1], ALU.mult, ALU.add))(),
                                   deps=[nm2, t2r_reader[sl]])
                        mv_readers[sl4] = [nm2, ln2]
                        rstd_war[sl4] = t2
                        t2_of[tc] = t2

                    tr2_of = {}

                    def oproj_trans(tc):
                        sl = tc % 2
                        tr = None
                        for dc in range(NJ):
                            tr = P.add("pe",
                                       (lambda sl=sl, dc=dc: lambda e: e.transpose(
                                           tp2[sl][:, 128 * dc:128 * dc + 128],
                                           t2r[sl][:, 128 * dc:128 * dc + 128],
                                           identt[:, :]))(),
                                       deps=[t2_of[tc], tp2_reader[sl]] if dc == 0 else [])
                        t2r_reader[sl] = tr
                        tr2_of[tc] = tr

                    def oproj_evac(tc):
                        sl = tc % 2
                        ev = P.add("dve",
                                   (lambda sl=sl, tc=tc: lambda e: e.tensor_copy(
                                       t2T.ap().rearrange("p (c t) -> p c t", c=NJ)[:, :, 128 * tc:128 * tc + 128],
                                       tp2[sl].ap().rearrange("p (c t) -> p c t", c=NJ)))(),
                                   deps=[tr2_of[tc]])
                        tp2_reader[sl] = ev
                        t2T_evacs.append(ev)

                    for k in range(TC + 3):
                        if k < TC:
                            oproj_mms(k)
                        if k >= 2 and k - 2 < TC:
                            oproj_trans(k - 2)
                        if k >= 1 and k - 1 < TC:
                            oproj_chain(k - 1)
                        if k >= 3 and k - 3 < TC:
                            oproj_evac(k - 3)
                    t2T_ready = t2T_evacs[-1]

                with ExitStack() as fctx:
                    fsb = lambda name, shape, dt: fctx.enter_context(
                        nc.sbuf_tensor(f"{name}_b{b}", shape, dt))
                    fo = [fctx.enter_context(nc.psum_tensor(f"fo{i}_b{b}", [128, 1024], F32))
                          for i in range(3)]
                    x1l = [fsb(f"x1l{i}", [128, D], F32) for i in range(2)]
                    tmpf = [fsb(f"tmpf{i}", [128, D], F32) for i in range(2)]
                    outr = [fsb(f"outr{i}", [128, D], F32) for i in range(2)]

                    x1l_reader = [None, None]
                    outr_reader = [None, None]
                    fsums_reader = [None] * 4
                    fsumsq_reader = [None] * 4
                    mv_readers = [[] for _ in range(4)]
                    rstd_war = [None] * 4
                    fo_reader = [None, None, None]
                    fmm_of = {}
                    fld_of = {}

                    def ffn_mms(tc):
                        sl = tc % 2
                        ld = P.add("sync",
                                   (lambda tc=tc, sl=sl: lambda e: e.dma_start(
                                       out=x1l[sl][:, :],
                                       in_=x1_scr[tc * 128:(tc + 1) * 128, :]))(),
                                   deps=[x1l_reader[sl], x1_stores[tc],
                                         t2T_ready if tc < 2 else None],
                                   dma=True, q="xe" if sl == 0 else "xf")
                        fld_of[tc] = ld
                        slf = tc % 3
                        mm_last = None
                        for j in range(NJ):
                            lhs = t2T[:, SEQ * j + 128 * tc: SEQ * j + 128 * tc + 128]
                            items = [
                                (fo[slf][:, 512 * eb:512 * eb + 512], lhs,
                                 mt_sb[:, D * j + 512 * eb: D * j + 512 * eb + 512],
                                 j == 0, False, None)
                                for eb in range(2)]
                            dep0 = [t2T_ready, fo_reader[slf], op_biasld, ld_onesrow, mt_last] if j == 0 else ()
                            mm_last = P.add(
                                "pe",
                                (lambda items=items: lambda e: mm_group(e, items))(),
                                deps=dep0)
                        items = [
                            (fo[slf][:, 512 * eb:512 * eb + 512], onesrowt[:, :],
                             biasbf[:, 512 * eb:512 * eb + 512], False, True, None)
                            for eb in range(2)]
                        mm_last = P.add(
                            "pe",
                            (lambda items=items: lambda e: mm_group(e, items))(),
                            deps=())
                        fmm_of[tc] = mm_last

                    def ffn_chain(tc):
                        sl = tc % 2
                        sl4 = tc % 4
                        slf = tc % 3
                        mm_last = fmm_of[tc]
                        ld = fld_of[tc]
                        sm_ = P.add("act",
                                    (lambda slf=slf, sl4=sl4: lambda e: e.activation(
                                        dumpt[:, :], fo[slf][:, :], AF.Identity,
                                        accum_out=sums_r[sl4][:, 0:1]))(),
                                    deps=[mm_last, fsums_reader[sl4], sums_reader[sl4]])
                        sq_ = P.add("act",
                                    (lambda slf=slf, sl4=sl4: lambda e: e.activation(
                                        dumpt[:, :], fo[slf][:, :], AF.Square,
                                        accum_out=sumsq_r[sl4][:, 0:1]))(),
                                    deps=[sm_, fsumsq_reader[sl4], sumsq_reader[sl4]])
                        mean_ = P.add("dve",
                                      (lambda sl=sl, sl4=sl4: lambda e: e.tensor_scalar(
                                          mv_r[sl4][:, 0:1], sums_r[sl4][:, 0:1],
                                          1.0 / D, None, ALU.mult))(),
                                      deps=[sm_] + mv_readers[sl4])
                        fsums_reader[sl4] = mean_
                        nm2_ = P.add("dve",
                                     (lambda sl=sl, sl4=sl4: lambda e: e.scalar_tensor_tensor(
                                         negm2_r[sl4][:, 0:1], mv_r[sl4][:, 0:1], -1.0,
                                         mv_r[sl4][:, 0:1], ALU.mult, ALU.mult))(),
                                     deps=[mean_])
                        ag = P.add("dve",
                                   (lambda sl=sl, sl4=sl4: lambda e: e.scalar_tensor_tensor(
                                       mv_r[sl4][:, 1:2], sumsq_r[sl4][:, 0:1], 1.0 / D,
                                       negm2_r[sl4][:, 0:1], ALU.mult, ALU.add))(),
                                   deps=[sq_, nm2_])
                        fsumsq_reader[sl4] = ag
                        ln = P.add("act",
                                   (lambda sl=sl, sl4=sl4: lambda e: e.activation(
                                       lnv_r[sl4][:, :], mv_r[sl4][:, 1:2], AF.Ln, bias=epsv[:, 0:1]))(),
                                   deps=[ag])
                        ex = P.add("act",
                                   (lambda sl=sl, sl4=sl4: lambda e: e.activation(
                                       rstd_r[sl4][:, :], lnv_r[sl4][:, :], AF.Exp, scale=-0.5))(),
                                   deps=[ln, rstd_war[sl4]])
                        nm = P.add("dve",
                                   (lambda sl=sl, sl4=sl4: lambda e: e.scalar_tensor_tensor(
                                       negmur_r[sl4][:, :], mv_r[sl4][:, 0:1], -1.0,
                                       rstd_r[sl4][:, :], ALU.mult, ALU.mult))(),
                                   deps=[ex, ag])
                        tx = P.add("dve",
                                   (lambda sl=sl, sl4=sl4: lambda e: e.tensor_scalar(
                                       tmpf[sl][:, :], x1l[sl][:, :], negmur_r[sl4][:, 0:1],
                                       None, ALU.add))(),
                                   deps=[nm, ld])
                        x2 = P.add("dve",
                                   (lambda sl=sl, slf=slf, sl4=sl4: lambda e: e.scalar_tensor_tensor(
                                       outr[sl][:, :], fo[slf][:, :], rstd_r[sl4][:, 0:1],
                                       tmpf[sl][:, :], ALU.mult, ALU.add))(),
                                   deps=[tx, outr_reader[sl]])
                        rstd_war[sl4] = x2
                        x1l_reader[sl] = tx
                        mv_readers[sl4] = [nm, ln]
                        fo_reader[slf] = x2
                        sto = P.add("sync",
                                    (lambda b=b, tc=tc, sl=sl, sl4=sl4: lambda e: e.dma_start(
                                        out=y_hbm[b, tc * 128:(tc + 1) * 128, :],
                                        in_=outr[sl][:, :]))(),
                                    deps=[x2], dma=True, q="ya" if sl == 0 else "yb")
                        outr_reader[sl] = sto
                        out_dmas.append(sto)
                        return x2

                    for k in range(TC + 1):
                        if k < TC:
                            ffn_mms(k)
                        if k >= 1:
                            x2op = ffn_chain(k - 1)
                    prev_batch_last = x2op

        for od in out_dmas:
            od.needs_inc = True

        P.finalize()

        engines = {"pe": nc.tensor, "act": nc.scalar, "dve": nc.vector,
                   "pool": nc.gpsimd, "sync": nc.sync, "pdma": nc.gpsimd,
                   "gpsimd": nc.gpsimd}
        semmap = dict(sems)

        with nc.Block() as block:
            @block.sync
            def _(sync):
                P.run_engine("sync", engines, semmap)
                for g in ("ya", "yb"):
                    if P.totals.get(g):
                        sync.wait_ge(semmap[g], P.totals[g])

            @block.gpsimd
            def _(gpsimd):
                P.run_engine("gpsimd", engines, semmap)

            @block.vector
            def _(vector):
                P.run_engine("dve", engines, semmap)

            @block.scalar
            def _(scalar):
                P.run_engine("act", engines, semmap)

            @block.tensor
            def _(tensor):
                P.run_engine("pe", engines, semmap)

    return nc


_NC_CACHE = None


def kernel(L0_attn_hidden_states, L0_attn_Wq, L0_attn_Wk, L0_attn_Wv, L0_attn_Wo,
           L0_ffn_w_freq_real, L0_ffn_w_freq_imag, L0_ffn_bias):
    global _NC_CACHE
    x = np.ascontiguousarray(np.asarray(L0_attn_hidden_states, dtype=np.float32))
    wq = np.asarray(L0_attn_Wq, dtype=np.float32)
    wk = np.asarray(L0_attn_Wk, dtype=np.float32)
    wv = np.asarray(L0_attn_Wv, dtype=np.float32)
    wo = np.asarray(L0_attn_Wo, dtype=np.float32)
    wr = np.ascontiguousarray(np.asarray(L0_ffn_w_freq_real, dtype=np.float32))
    wi = np.ascontiguousarray(np.asarray(L0_ffn_w_freq_imag, dtype=np.float32))
    fb = np.ascontiguousarray(np.asarray(L0_ffn_bias, dtype=np.float32).reshape(1, D))

    wqt = np.ascontiguousarray(wq.T)
    wkt = np.ascontiguousarray(wk.T)
    wvt = np.ascontiguousarray(wv.T)
    wot = np.ascontiguousarray(wo.T)

    jj = np.arange(D, dtype=np.int64)
    ang = 2.0 * np.pi * (np.outer(jj, jj) % D).astype(np.float64) / D
    ctab = np.cos(ang).astype(ml_dtypes.bfloat16)
    stab = np.sin(ang).astype(ml_dtypes.bfloat16)
    snegtab = (-np.sin(ang)).astype(ml_dtypes.bfloat16)
    ident = np.eye(128, dtype=ml_dtypes.bfloat16)
    ones64 = np.ones((128, 64), dtype=ml_dtypes.bfloat16)
    onesrow = np.ones((1, 128), dtype=ml_dtypes.bfloat16)

    if _NC_CACHE is None:
        _NC_CACHE = build_nc()
    nc = _NC_CACHE

    shared = dict(wqt=wqt, wkt=wkt, wvt=wvt, wot=wot, wfr=wr, wfi=wi,
                  fbias=fb, ctab=ctab, snegtab=snegtab,
                  ident=ident, ones64=ones64, onesrow=onesrow)
    in_maps = []
    for c in range(NCORES):
        m = dict(shared)
        m["x"] = np.ascontiguousarray(x[c * BPC:(c + 1) * BPC])
        m["cshard"] = np.ascontiguousarray(ctab[:, 128 * c:128 * c + 128])
        m["sshard"] = np.ascontiguousarray(stab[:, 128 * c:128 * c + 128])
        m["sshardneg"] = np.ascontiguousarray(snegtab[:, 128 * c:128 * c + 128])
        in_maps.append(m)

    res = run_bass_kernel_spmd(nc, in_maps, core_ids=list(range(NCORES)))
    out = np.concatenate([np.asarray(r["y"]) for r in res.results], axis=0)
    return out
